# version 05
# baseline (speedup 1.0000x reference)
import contextlib
import numpy as np
import ml_dtypes

import concourse.bass as bass
import concourse.mybir as mybir
from concourse.bass_utils import run_bass_kernel_spmd

F32 = mybir.dt.float32
BF16 = mybir.dt.bfloat16
AF = mybir.ActivationFunctionType
ALU = mybir.AluOpType
AX = mybir.AxisListType

NCORE = 8
D = 2048
FC = 16
T = 1024
HB = 512
TH = T + HB
H = 16
DH = 128
GH = 4
NG = H // GH
R = 16
MLP = 16384
NCK = 16
LNC = 256
NLN1 = TH // LNC
NLN2 = T // LNC

BF = ml_dtypes.bfloat16
DUMP = ()
FFN_NO_DOWN = False
FFN_NO_GU = False
FFN_DOWN_SCRATCH = False
FFN_DOWN_SER = False



def _rope_tables():
    half = DH // 2
    theta = 1.0 / 10000.0 ** (np.arange(half, dtype=np.float64) / half)
    pos = np.arange(-HB, 8192, dtype=np.float64)
    ang = pos[:, None] * theta[None, :]
    cos = np.cos(ang).astype(np.float32)
    sin = np.sin(ang).astype(np.float32)
    cos_p = np.empty((128, HB + 8192), np.float32)
    sin_p = np.empty((128, HB + 8192), np.float32)
    cos_p[0::2] = cos.T
    cos_p[1::2] = cos.T
    sin_p[0::2] = -sin.T
    sin_p[1::2] = sin.T
    return cos_p, sin_p


def _perm_idx():
    idx = np.empty(D, np.int64)
    for h in range(H):
        for j in range(64):
            idx[h * 128 + 2 * j] = h * 128 + j
            idx[h * 128 + 2 * j + 1] = h * 128 + 64 + j
    return idx


def _tile_wT(wT):
    return np.ascontiguousarray(wT.reshape(D, FC, 128).transpose(1, 0, 2))


def _masks_for_core(core):
    NEG = np.float32(-1e9)
    qi = np.arange(128)[:, None]
    kj = np.arange(128)[None, :]
    std = np.zeros((2, 128, 128), np.float32)
    std[0][(kj + 0) < (qi + 1)] = NEG
    std[1][(kj + 512) > (qi + 512)] = NEG
    early = np.zeros((4, 5, 128, 128), np.float32)
    for qt in range(4):
        gq0 = core * T + qt * 128
        for s in range(5):
            m = np.zeros((128, 128), np.float32)
            kglob = gq0 - 512 + s * 128 + kj
            qglob = gq0 + qi
            bad = (kglob > qglob) | (kglob < qglob - 511) | (kglob < 0)
            m[bad] = NEG
            early[qt, s] = m
    return early, std


def _unpack_nvfp4(Wp):
    b = np.asarray(Wp).astype(np.uint8)
    lo = b & 15
    hi = (b >> 4) & 15
    return np.stack([lo, hi], axis=-1).reshape(b.shape[0], -1)


def prepare_inputs(d):
    x = np.asarray(d["L0_attn_x"], np.float32)[0]
    perm = _perm_idx()

    wqT = np.ascontiguousarray(np.asarray(d["L0_attn_wq"], np.float32).T)[:, perm]
    wkT = np.ascontiguousarray(np.asarray(d["L0_attn_wk"], np.float32).T)[:, perm]
    wvT = np.ascontiguousarray(np.asarray(d["L0_attn_wv"], np.float32).T)
    woT = np.ascontiguousarray(np.asarray(d["L0_attn_wo"], np.float32).T)
    wq_t, wk_t, wv_t, wo_t = map(_tile_wT, (wqT, wkT, wvT, woT))

    lAm = np.stack([
        np.asarray(d["L0_attn_qA"], np.float32).T.reshape(FC, 128, R),
        np.asarray(d["L0_attn_kA"], np.float32).T.reshape(FC, 128, R),
        np.asarray(d["L0_attn_vA"], np.float32).T.reshape(FC, 128, R),
        np.asarray(d["L0_attn_oA"], np.float32).T.reshape(FC, 128, R),
    ], axis=0)
    lqB = np.ascontiguousarray(np.asarray(d["L0_attn_qB"], np.float32).T[:, perm]).astype(BF)
    lkB = np.ascontiguousarray(np.asarray(d["L0_attn_kB"], np.float32).T[:, perm]).astype(BF)
    lvB = np.ascontiguousarray(np.asarray(d["L0_attn_vB"], np.float32).T).astype(BF)
    loB = np.ascontiguousarray(np.asarray(d["L0_attn_oB"], np.float32).T).astype(BF)

    normw = np.ascontiguousarray(np.asarray(d["L0_attn_norm_w"], np.float32).reshape(FC, 128).T)
    cvec = np.ascontiguousarray(np.asarray(d["L0_c"], np.float32).reshape(FC, 128).T)
    wmod = np.asarray(d["L0_wmod"], np.float32)

    cos_p, sin_p = _rope_tables()
    SC = np.float32(1.0 / np.sqrt(DH))

    gcT = _unpack_nvfp4(d["L0_ffn_gate_W_packed"]).T
    ucT = _unpack_nvfp4(d["L0_ffn_up_W_packed"]).T
    dcT = _unpack_nvfp4(d["L0_ffn_down_W_packed"]).T
    gsT = np.ascontiguousarray(np.asarray(d["L0_ffn_gate_scales"], np.float32).T)
    usT = np.ascontiguousarray(np.asarray(d["L0_ffn_up_scales"], np.float32).T)
    dsT = np.ascontiguousarray(np.asarray(d["L0_ffn_down_scales"], np.float32).T)

    selm = np.zeros((4, 128), np.float32)
    for p in range(128):
        selm[p >> 5, p] = 1.0

    shared = {
        "wq": wq_t, "wk": wk_t, "wv": wv_t, "wo": wo_t,
        "lA": np.ascontiguousarray(lAm),
        "lqB": lqB, "lkB": lkB, "lvB": lvB, "loB": loB,
        "normw": normw, "cvec": cvec,
        "ident": np.eye(128, dtype=np.float32).astype(BF),
        "onescol": np.ones((128, 1), np.float32).astype(BF),
        "onesrow": np.ones((1, 128), np.float32).astype(BF),
        "one11": np.ones((1, 1), np.float32),
        "epsv": np.array([[1e-5, 1e-6]], np.float32),
        "sel": selm.astype(BF),
    }

    in_maps = []
    for c in range(NCORE):
        s0 = c * T
        xs = np.zeros((TH, D), np.float32)
        lo = max(0, s0 - HB)
        if s0 > lo:
            xs[HB - (s0 - lo):HB] = x[lo:s0]
        xs[HB:] = x[s0:s0 + T]
        xTc = np.ascontiguousarray(xs.T)

        early, std = _masks_for_core(c)
        roff = s0
        im = dict(shared)
        im.update({
            "xT": xTc,
            "wmslab": np.ascontiguousarray(
                wmod[:, c * 1536:(c + 1) * 1536].reshape(FC, 128, 1536)),
            "ropeqc": (np.ascontiguousarray(cos_p[:, roff + HB:roff + HB + T]) * SC).astype(BF),
            "ropeqs": (np.ascontiguousarray(sin_p[:, roff + HB:roff + HB + T]) * SC).astype(BF),
            "ropekc": np.ascontiguousarray(cos_p[:, roff:roff + TH]).astype(BF),
            "ropeks": np.ascontiguousarray(sin_p[:, roff:roff + TH]).astype(BF),
            "maskE": early, "maskS": std,
        })
        n0 = c * NCK * 128
        im["gcodes"] = np.ascontiguousarray(
            gcT[:, n0:n0 + NCK * 128].reshape(FC, 128, NCK, 128)
            .transpose(2, 1, 0, 3)).astype(BF)
        im["ucodes"] = np.ascontiguousarray(
            ucT[:, n0:n0 + NCK * 128].reshape(FC, 128, NCK, 128)
            .transpose(2, 1, 0, 3)).astype(BF)
        im["dcodes"] = np.ascontiguousarray(
            dcT[n0:n0 + NCK * 128].reshape(NCK, 128, D)).astype(BF)
        im["gscale"] = np.ascontiguousarray(
            gsT[:, n0:n0 + NCK * 128].reshape(FC, 4, NCK, 128)
            .transpose(2, 1, 0, 3).reshape(NCK, 4, D))
        im["uscale"] = np.ascontiguousarray(
            usT[:, n0:n0 + NCK * 128].reshape(FC, 4, NCK, 128)
            .transpose(2, 1, 0, 3).reshape(NCK, 4, D))
        im["dscale"] = np.ascontiguousarray(
            dsT[c * 64:(c + 1) * 64].reshape(NCK, 4, D))
        in_maps.append(im)
    return in_maps




class Sched:
    ENG_SEM = {"sync": "ld", "tensor": "tn", "vector": "vc",
               "scalar": "sc", "gpsimd": "gp"}

    def __init__(self, V):
        self.steps = []
        self.base = {v: V.get(v, 0) for v in self.ENG_SEM.values()}

    def add(self, eng, deps, fn, n=1, sem=None):
        sid = len(self.steps)
        self.steps.append((eng, list(deps), fn, n, sem))
        return sid

    def emit(self, nc, S, V, inc):
        cnt = {}
        after = {}
        semof = {}
        for sid, (eng, deps, fn, n, sem) in enumerate(self.steps):
            sname = sem or self.ENG_SEM[eng]
            if sname not in self.base:
                self.base[sname] = V.get(sname, 0)
            cnt[sname] = cnt.get(sname, 0) + n
            after[sid] = (eng, sname, cnt[sname])
        per_eng = {e: [] for e in self.ENG_SEM}
        for sid, (eng, deps, fn, n, sem) in enumerate(self.steps):
            per_eng[eng].append(sid)

        sched = self

        with nc.Block() as blk:
            def make(eng_name):
                def body(e):
                    for sid in per_eng[eng_name]:
                        _, deps, fn, n, sem = sched.steps[sid]
                        sname = sem or sched.ENG_SEM[eng_name]
                        need = {}
                        for dp in deps:
                            deng, dsem, dcnt = after[dp]
                            if deng == eng_name and dsem == sched.ENG_SEM[deng]:
                                continue
                            if deng == eng_name and dsem != sched.ENG_SEM[deng]:
                                pass
                            need[dsem] = max(need.get(dsem, 0),
                                             sched.base[dsem] + dcnt)
                        for s2, val in need.items():
                            e.wait_ge(S[s2], val)
                        last = fn(e)
                        inc(last, sname, n)
                return body
            if per_eng["sync"]:
                blk.sync(make("sync"))
            if per_eng["tensor"]:
                blk.tensor(make("tensor"))
            if per_eng["scalar"]:
                blk.scalar(make("scalar"))
            if per_eng["vector"]:
                blk.vector(make("vector"))
            if per_eng["gpsimd"]:
                blk.gpsimd(make("gpsimd"))


def build_nc(dump=()):
    nc = bass.Bass("TRN2", target_bir_lowering=False)

    def din(name, shape, dtype):
        return nc.dram_tensor(name, shape, dtype, kind="ExternalInput")

    xT = din("xT", [D, TH], F32)
    wW = {"q": din("wq", [FC, D, 128], F32), "k": din("wk", [FC, D, 128], F32),
          "v": din("wv", [FC, D, 128], F32), "o": din("wo", [FC, D, 128], F32)}
    lA = din("lA", [4, FC, 128, R], F32)
    lB = {"q": din("lqB", [R, D], BF16), "k": din("lkB", [R, D], BF16),
          "v": din("lvB", [R, D], BF16), "o": din("loB", [R, D], BF16)}
    normw = din("normw", [128, FC], F32)
    cvec = din("cvec", [128, FC], F32)
    wmslab = din("wmslab", [FC, 128, 1536], F32)
    ropeqc = din("ropeqc", [128, T], BF16)
    ropeqs = din("ropeqs", [128, T], BF16)
    ropekc = din("ropekc", [128, TH], BF16)
    ropeks = din("ropeks", [128, TH], BF16)
    maskE = din("maskE", [4, 5, 128, 128], F32)
    maskS = din("maskS", [2, 128, 128], F32)
    identD = din("ident", [128, 128], BF16)
    onescolD = din("onescol", [128, 1], BF16)
    onesrowD = din("onesrow", [1, 128], BF16)
    one11 = din("one11", [1, 1], F32)
    epsD = din("epsv", [1, 2], F32)
    selD = din("sel", [4, 128], BF16)
    codesD = {"g": din("gcodes", [NCK, 128, FC, 128], BF16),
              "u": din("ucodes", [NCK, 128, FC, 128], BF16),
              "d": din("dcodes", [NCK, 128, D], BF16)}
    scaleD = {"g": din("gscale", [NCK, 4, D], F32),
              "u": din("uscale", [NCK, 4, D], F32),
              "d": din("dscale", [NCK, 4, D], F32)}

    xout = nc.dram_tensor("xout", [D, T], F32, kind="ExternalOutput")
    dumps = {}
    if "hn" in dump:
        dumps["hn"] = nc.dram_tensor("d_hn", [128, FC, TH], BF16, kind="ExternalOutput")
    if "xp" in dump:
        dumps["xp"] = nc.dram_tensor("d_xp", [128, FC, T], F32, kind="ExternalOutput")
    if "mv" in dump:
        dumps["mv"] = nc.dram_tensor("d_mv", [128, 96], F32, kind="ExternalOutput")
    if "qkv" in dump:
        dumps["q"] = nc.dram_tensor("d_q", [128, GH, T], BF16, kind="ExternalOutput")
        dumps["k"] = nc.dram_tensor("d_k", [128, GH, TH], BF16, kind="ExternalOutput")
        dumps["v"] = nc.dram_tensor("d_v", [128, TH // 128, GH * 128], BF16, kind="ExternalOutput")
    if "attn" in dump:
        dumps["attn"] = nc.dram_tensor("d_attn", [128, GH, T], BF16, kind="ExternalOutput")
    if "h2" in dump:
        dumps["h2"] = nc.dram_tensor("d_h2", [128, FC, T], BF16, kind="ExternalOutput")
    if "arena" in dump:
        dumps["arena"] = nc.dram_tensor("d_arena", [128, 70912], BF16, kind="ExternalOutput")
    if "inter" in dump:
        dumps["inter"] = nc.dram_tensor("d_inter", [128, FC, 512], BF16, kind="ExternalOutput")
    if "wdq" in dump:
        dumps["wdq"] = nc.dram_tensor("d_wdq", [3 * NCK * FC * 128 * 128], BF16,
                                      kind="ExternalOutput")

    NSL = NCK * FC * 128 * 128
    deq_in = nc.dram_tensor("deq_in", [3 * NSL], BF16)
    deq_out = nc.dram_tensor("deq_out", [NCORE, 3 * NSL], BF16, addr_space="Shared")
    mod_b = nc.dram_tensor("mod_b", [1, 1536], F32)
    mod_g = nc.dram_tensor("mod_g", [NCORE, 1536], F32)

    stk = contextlib.ExitStack()
    ent = stk.enter_context
    S = {k: ent(nc.semaphore(k)) for k in
         ["ld", "st", "cc", "tn", "vc", "sc", "gp",
          "lb0", "lb1", "lb2", "lb3", "ldd", "lws", "lmk",
          "lc0", "lc1", "lsf", "lwm", "lmd", "lhx", "st0", "st1"]}
    V = {k: 0 for k in S}

    def inc(instr, k, n=1):
        instr.then_inc(S[k], n)
        V[k] += n
        return V[k]

    xo = ent(nc.sbuf_tensor("xo", [128, FC, T], F32))
    mskSb = ent(nc.sbuf_tensor("mskSb", [128, 2, 128], F32))
    mskEb = ent(nc.sbuf_tensor("mskEb", [128, 5, 128], F32))
    idn = ent(nc.sbuf_tensor("idn", [128, 128], BF16))
    diag = ent(nc.sbuf_tensor("diag", [128, 128], BF16))
    mv = ent(nc.sbuf_tensor("mv", [128, 96], F32))
    nwsb = ent(nc.sbuf_tensor("nwsb", [128, FC], F32))
    csb = ent(nc.sbuf_tensor("csb", [128, FC], F32))
    csbb = ent(nc.sbuf_tensor("csbb", [128, FC], BF16))
    onec = ent(nc.sbuf_tensor("onec", [128, 1], BF16))
    oner = ent(nc.sbuf_tensor("oner", [1, 128], BF16))
    one1 = ent(nc.sbuf_tensor("one1", [1, 1], F32))
    epsb = ent(nc.sbuf_tensor("epsb", [1, 2], F32))
    selb = ent(nc.sbuf_tensor("selb", [4, 128], BF16))
    gcl = ent(nc.sbuf_tensor("gcl", [128, FC], F32))
    racc1 = ent(nc.sbuf_tensor("racc1", [128, 1], F32))
    racc2 = ent(nc.sbuf_tensor("racc2", [128, 1], F32))
    rmx1 = ent(nc.sbuf_tensor("rmx1", [128, 1], F32))
    rmx2 = ent(nc.sbuf_tensor("rmx2", [128, 1], F32))
    rsum = ent(nc.sbuf_tensor("rsum", [128, 1], F32))

    ARN = 70912
    arena = ent(nc.sbuf_tensor("arena", [128, ARN], BF16))

    def arb(off, n):
        return arena[:, off:off + n]

    def arf(off, n):
        return arena[:, off:off + 2 * n].bitcast(F32)

    MODROW_O = 0
    WMF_O = 3072
    WMB_O = 6144
    SCF_O = 7680
    SCB_O = 11776
    DQ_C = [13824, 15872]
    DQ_T = [17920, 19968, 22016, 24064]
    DQ_W = [26112, 28160]
    XST_O = 0
    HN_O = 16384
    H1_O = 40960
    CST_O = 49152
    RW_O = 53248
    RAB_O = 55808
    RQC_O, RQS_O, RKC_O, RKS_O = 0, 1024, 2048, 3584
    LAS_O = 5120
    LBS_O = {"q": 7168, "k": 9216, "v": 11264, "o": 13312}
    LASB_O = 15360
    TMPL_O = 40960
    TMPO_O = 42496
    QG_O = 44544
    KG_O = 48640
    VG_O = 54784
    ATG_O = 60928
    WS_O = 65024
    WB_O = [67072]
    SCR_O = 69120
    EXP_O = 69120
    PT_O = 69760
    VT_O = 70400
    QR_O = [69120, 69632, 70144]
    H2_O = 16384
    INT_O = 32768
    H1B2_O = 49152
    CST2_O = 57344
    RW2_O = 61440
    RAB2_O = 64000
    FW_O = [57344, 59392, 61440, 63488, 65536, 67584]
    GEL_O = 69632

    psA = ent(nc.psum_tensor("psA", [128, 2048], F32))
    psB = ent(nc.psum_tensor("psB", [128, 1024], F32))
    psC = ent(nc.psum_tensor("psC", [128, 512], F32))
    psT = ent(nc.psum_tensor("psT", [128, 1024], BF16))

    with nc.Block() as blk:
        @blk.sync
        def _(e):
            for dst, src in [
                (csb[:, :], cvec[:, :]), (nwsb[:, :], normw[:, :]),
                (one1[:, :], one11[:, :]), (selb[:, :], selD[:, :]),
                (epsb[:, :], epsD[:, :]),
                (idn[:, :], identD[:, :]),
                (onec[:, :], onescolD[:, :]), (oner[:, :], onesrowD[:, :]),
                (mskSb[:, 0, :], maskS[0]), (mskSb[:, 1, :], maskS[1]),
            ]:
                inc(e.dma_start(dst, src), "ld", 16)
            inc(e.dma_start(
                xo[:, :, :],
                xT[:, HB:TH].rearrange("(a p) t -> p a t", p=128)), "ld", 16)
            V["A1"] = V["ld"]

        @blk.scalar
        def _(e):
            e.wait_ge(S["ld"], V["A1"])
            inc(e.activation(csb[:, :], csb[:, :], AF.Silu), "sc")

    wm_f = arf(WMF_O, 1536)
    with nc.Block() as blk:
        b_ld, b_sc, b_tn = V["ld"], V["sc"], V["tn"]

        @blk.sync
        def _(e):
            for kc in range(FC):
                if kc >= 1:
                    e.wait_ge(S["tn"], b_tn + kc)
                inc(e.dma_start(wm_f[:, :], wmslab[kc]), "lwm", 16)

        @blk.tensor
        def _(e):
            e.wait_ge(S["sc"], b_sc)
            for kc in range(FC):
                e.wait_ge(S["lwm"], 16 * (kc + 1))
                for cc in range(3):
                    mm = e.matmul(psA[0:1, cc * 512:(cc + 1) * 512],
                                  csb[:, kc:kc + 1],
                                  wm_f[:, cc * 512:(cc + 1) * 512],
                                  start=(kc == 0), stop=(kc == FC - 1))
                inc(mm, "tn")

    modrow = arf(MODROW_O, 1536)
    with nc.Block() as blk:
        @blk.vector
        def _(e):
            inc(e.tensor_copy(modrow[0:1, :], psA[0:1, 0:1536]), "vc")
            V["modev"] = V["vc"]

        @blk.gpsimd
        def _(e):
            e.wait_ge(S["vc"], V["modev"])
            inc(e.dma_start(mod_b[:, :], modrow[0:1, :]), "gp", 16)
            e.wait_ge(S["gp"], V["gp"])
            inc(e.collective_compute(
                "AllGather", ALU.bypass, replica_groups=[list(range(NCORE))],
                ins=[mod_b.ap().opt()], outs=[mod_g.ap().opt()]), "cc")
            V["modcc"] = V["cc"]

    with nc.Block() as blk:
        b_ld, b_tn = V["ld"], V["tn"]

        @blk.sync
        def _(e):
            e.wait_ge(S["cc"], V["modcc"])
            for s in range(NCORE):
                if s >= 1:
                    e.wait_ge(S["tn"], b_tn + s)
                inc(e.dma_start(modrow[0:1, :], mod_g[s:s + 1, :]), "lmd", 16)

        @blk.tensor
        def _(e):
            for s in range(NCORE):
                e.wait_ge(S["lmd"], 16 * (s + 1))
                for j in range(12):
                    mm = e.matmul(psB[:, s * 12 + j:s * 12 + j + 1],
                                  modrow[0:1, j * 128:(j + 1) * 128],
                                  one1[:, :], start=True, stop=True)
                inc(mm, "tn")
            V["mvT"] = V["tn"]

        @blk.vector
        def _(e):
            e.wait_ge(S["tn"], V["mvT"])
            e.tensor_copy(mv[:, :], psB[:, 0:96])
            e.drain()
            for col in (32, 80):
                g = mv[:, col:col + FC]
                t0 = gcl[:, :]
                e.tensor_scalar(t0, g, -1.0, None, ALU.mult)
                e.drain()
                e.tensor_tensor(t0, g, t0, ALU.max)
                e.drain()
                e.tensor_scalar(t0, t0, 0.25, None, ALU.max)
                e.tensor_scalar(g, g, 0.0, 2.0, ALU.is_ge, ALU.mult)
                e.drain()
                e.tensor_scalar(g, g, -1.0, None, ALU.add)
                e.drain()
                e.tensor_tensor(g, t0, g, ALU.mult)
                e.drain()
            e.tensor_scalar(mv[:, 16:32], mv[:, 16:32], 1.0, None, ALU.add)
            inc(e.tensor_scalar(mv[:, 64:80], mv[:, 64:80], 1.0, None, ALU.add),
                "vc")
            V["mvdone"] = V["vc"]

    MVO = {"sa": 0, "ca1": 16, "ga": 32, "sb": 48, "cb1": 64, "gb": 80}

    scf = arf(SCF_O, 2048)
    scb = arb(SCB_O, 2048)
    NSLICES = 3 * NCK
    with nc.Block() as blk:
        b_ld, b_vc, b_tn, b_gp = V["ld"], V["vc"], V["tn"], V["gp"]

        @blk.sync
        def _(e):
            for i in range(NSLICES):
                mat = "gud"[i // NCK]
                ncl = i % NCK
                buf = i % 2
                if i >= 2:
                    e.wait_ge(S["vc"], b_vc + 2 * (i - 1))
                cb = arb(DQ_C[buf], 2048)
                if mat == "d":
                    inc(e.dma_start(cb, codesD["d"][ncl]), "lc%d" % buf, 16)
                else:
                    inc(e.dma_start(cb.rearrange("p (a b) -> p a b", a=FC),
                                    codesD[mat][ncl]), "lc%d" % buf, 16)
                if i >= 1:
                    e.wait_ge(S["vc"], b_vc + 2 * (i - 1) + 1)
                inc(e.dma_start(scf[0:4, 0:D], scaleD[mat][ncl]), "lsf", 16)

        @blk.vector
        def _(e):
            for i in range(NSLICES):
                buf = i % 2
                cb = arb(DQ_C[buf], 2048)
                tA = arb(DQ_T[0], 2048)
                tB = arb(DQ_T[1], 2048)
                tC = arb(DQ_T[2], 2048)
                tD = arb(DQ_T[3], 2048)
                wout = arb(DQ_W[buf], 2048)
                e.wait_ge(S["lc%d" % buf], 16 * (i // 2 + 1))
                e.wait_ge(S["lsf"], 16 * (i + 1))
                if i >= 1:
                    e.wait_ge(S["tn"], b_tn + i)
                inc(e.tensor_copy(scb[0:4, 0:D], scf[0:4, 0:D]), "vc")
                e.wait_ge(S["tn"], b_tn + i + 1)
                e.tensor_scalar(tA, cb, 8.0, -8.0, ALU.is_ge, ALU.mult)
                e.tensor_tensor(tB, cb, tA, ALU.add)
                e.tensor_scalar(tC, tA, 0.125, 0.5, ALU.mult, ALU.add)
                e.tensor_scalar(tD, tB, 2.0, -4.0, ALU.mult, ALU.add)
                e.tensor_tensor(tD, tB, tD, ALU.max)
                e.tensor_scalar(tB, tB, 4.0, -16.0, ALU.mult, ALU.add)
                e.tensor_tensor(tB, tD, tB, ALU.max)
                e.tensor_tensor(tB, tB, tC, ALU.mult)
                if i >= 2:
                    e.wait_ge(S["st%d" % buf], 16 * (i // 2))
                inc(e.tensor_tensor(wout, tB, psA[:, 0:2048], ALU.mult), "vc")

        @blk.tensor
        def _(e):
            for i in range(NSLICES):
                e.wait_ge(S["vc"], b_vc + 2 * i + 1)
                for kc in range(FC):
                    mm = e.matmul(psA[:, kc * 128:(kc + 1) * 128], selb[:, :],
                                  scb[0:4, kc * 128:(kc + 1) * 128],
                                  start=True, stop=True)
                inc(mm, "tn")

        @blk.gpsimd
        def _(e):
            for i in range(NSLICES):
                ncl = i % NCK
                buf = i % 2
                wout = arb(DQ_W[buf], 2048)
                e.wait_ge(S["vc"], b_vc + 2 * i + 2)
                off = (i // NCK) * NSL + ncl * (FC * 128 * 128)
                inc(e.dma_start(
                    deq_in[off:off + 2048 * 128].rearrange("(p n) -> p n", p=128),
                    wout), "st%d" % buf, 16)
            e.wait_ge(S["st0"], 16 * (NSLICES // 2))
            e.wait_ge(S["st1"], 16 * (NSLICES // 2))
            inc(e.collective_compute(
                "AllGather", ALU.bypass, replica_groups=[list(range(NCORE))],
                ins=[deq_in.ap().opt()], outs=[deq_out.ap().opt()]), "cc")
            V["deqcc"] = V["cc"]

    if "wdq" in dumps:
        with nc.Block() as blk:
            @blk.gpsimd
            def _(e):
                inc(e.dma_start(dumps["wdq"][:], deq_in[:]), "gp", 16)
                e.wait_ge(S["gp"], V["gp"])

    def emit_ln(blk_nc, xsrc_fn, nchunks, hn_view_fn, mvo_c, mvo_s, mvo_g,
                with_rms, h1_off, cst_off, rw_off, rab_off):
        pass

    cst = arb(CST_O, FC * LNC)
    h1b = arf(H1_O, FC * LNC)
    hn = arb(HN_O, FC * TH)
    rwv = {}
    for j, k in enumerate(["mean", "ms", "var", "rr", "bb"]):
        rwv[k] = arf(RW_O + 2 * j * LNC, LNC)[0:1, :]
    rAB = arb(RAB_O, 2 * LNC)[0:1, :]

    def hn_v(kc, t0):
        return hn.rearrange("p (a b) -> p a b", a=FC)[:, kc, t0:t0 + LNC]

    def xsrc1(ch, kc):
        if ch < 2:
            return arf(XST_O, FC * 512).rearrange("p (a b) -> p a b", a=FC)[
                :, kc, ch * LNC:(ch + 1) * LNC]
        return xo[:, kc, (ch - 2) * LNC:(ch - 1) * LNC]

    def cst_v(kc):
        return cst.rearrange("p (a b) -> p a b", a=FC)[:, kc, :]

    def h1_v(kc):
        return h1b.rearrange("p (a b) -> p a b", a=FC)[:, kc, :]

    with nc.Block() as blk:
        b_ld, b_sc, b_tn, b_vc = V["ld"], V["sc"], V["tn"], V["vc"]

        @blk.sync
        def _(e):
            inc(e.dma_start(
                arf(XST_O, FC * 512).rearrange("p (a b) -> p a b", a=FC),
                xT[:, 0:HB].rearrange("(a p) t -> p a t", p=128)), "lhx", 16)

        @blk.scalar
        def _(e):
            for ch in range(NLN1):
                e.wait_ge(S["lhx"], 16)
                if ch >= 1:
                    e.wait_ge(S["tn"], b_tn + 5 * (ch - 1) + 4)
                xin = (xo[:, :, (ch - 2) * LNC:(ch - 1) * LNC] if ch >= 2 else
                       arf(XST_O, FC * 512).rearrange("p (a b) -> p a b", a=FC)
                       [:, :, ch * LNC:(ch + 1) * LNC])
                inc(e.activation(cst.rearrange("p (a b) -> p a b", a=FC),
                                 xin, AF.Copy), "sc")
                e.wait_ge(S["tn"], b_tn + 5 * ch + 1)
                inc(e.activation(cst, cst, AF.Square), "sc")
                e.wait_ge(S["vc"], b_vc + 6 * ch + 1)
                inc(e.activation(rwv["var"], rwv["var"], AF.Sqrt,
                                 bias=epsb[0:1, 0:1]), "sc")
                e.wait_ge(S["vc"], b_vc + 6 * ch + 3)
                inc(e.activation(cst.rearrange("p (a b) -> p a b", a=FC),
                                 h1b.rearrange("p (a b) -> p a b", a=FC),
                                 AF.Square), "sc")
                e.wait_ge(S["vc"], b_vc + 6 * ch + 4)
                inc(e.activation(rwv["ms"], rwv["ms"], AF.Sqrt,
                                 bias=epsb[0:1, 1:2]), "sc")

        @blk.tensor
        def _(e):
            for ch in range(NLN1):
                e.wait_ge(S["sc"], b_sc + 5 * ch + 1)
                if ch >= 1:
                    e.wait_ge(S["vc"], b_vc + 6 * (ch - 1) + 4)
                for kc in range(FC):
                    mm = e.matmul(psC[0:1, 0:LNC], onec[:, :], cst_v(kc),
                                  start=(kc == 0), stop=(kc == FC - 1))
                inc(mm, "tn")
                e.wait_ge(S["sc"], b_sc + 5 * ch + 2)
                for kc in range(FC):
                    mm = e.matmul(psC[0:1, LNC:2 * LNC], onec[:, :], cst_v(kc),
                                  start=(kc == 0), stop=(kc == FC - 1))
                inc(mm, "tn")
                e.wait_ge(S["vc"], b_vc + 6 * ch + 2)
                inc(e.matmul(psB[:, 0:2 * LNC], oner[:, :], rAB,
                             start=True, stop=True), "tn")
                e.wait_ge(S["sc"], b_sc + 5 * ch + 4)
                for kc in range(FC):
                    mm = e.matmul(psC[0:1, LNC:2 * LNC], onec[:, :], cst_v(kc),
                                  start=(kc == 0), stop=(kc == FC - 1))
                inc(mm, "tn")
                e.wait_ge(S["vc"], b_vc + 6 * ch + 5)
                inc(e.matmul(psB[:, 512:512 + LNC], oner[:, :],
                             rAB[:, 0:LNC], start=True, stop=True), "tn")

        @blk.vector
        def _(e):
            for ch in range(NLN1):
                own = ch >= 2
                t0 = ch * LNC
                e.wait_ge(S["tn"], b_tn + 5 * ch + 2)
                e.tensor_scalar(rwv["mean"], psC[0:1, 0:LNC], 1.0 / D, None, ALU.mult)
                e.tensor_scalar(rwv["ms"], psC[0:1, LNC:2 * LNC], 1.0 / D, None, ALU.mult)
                e.tensor_tensor(rwv["var"], rwv["mean"], rwv["mean"], ALU.mult)
                inc(e.tensor_tensor(rwv["var"], rwv["ms"], rwv["var"],
                                    ALU.subtract), "vc")
                e.wait_ge(S["sc"], b_sc + 5 * ch + 3)
                e.reciprocal(rwv["rr"], rwv["var"])
                e.tensor_tensor(rwv["bb"], rwv["mean"], rwv["rr"], ALU.mult)
                e.tensor_scalar(rwv["bb"], rwv["bb"], -1.0, None, ALU.mult)
                e.tensor_copy(rAB[:, 0:LNC], rwv["rr"])
                inc(e.tensor_copy(rAB[:, LNC:2 * LNC], rwv["bb"]), "vc")
                e.wait_ge(S["tn"], b_tn + 5 * ch + 3)
                for kc in range(FC):
                    e.tensor_tensor(h1_v(kc), xsrc1(ch, kc), psB[:, 0:LNC], ALU.mult)
                    e.tensor_tensor(h1_v(kc), h1_v(kc), psB[:, LNC:2 * LNC], ALU.add)
                    e.tensor_scalar(h1_v(kc), h1_v(kc),
                                    mv[:, MVO["ca1"] + kc:MVO["ca1"] + kc + 1],
                                    mv[:, MVO["sa"] + kc:MVO["sa"] + kc + 1],
                                    ALU.mult, ALU.add)
                if own:
                    for kc in range(FC):
                        xv = xo[:, kc, (ch - 2) * LNC:(ch - 1) * LNC]
                        last = e.scalar_tensor_tensor(
                            xv, h1_v(kc),
                            mv[:, MVO["ga"] + kc:MVO["ga"] + kc + 1],
                            xv, ALU.mult, ALU.add)
                    inc(last, "vc")
                else:
                    inc(e.tensor_copy(rmx1[:, :], rmx1[:, :]), "vc")
                e.wait_ge(S["tn"], b_tn + 5 * ch + 4)
                inc(e.tensor_scalar(rwv["ms"], psC[0:1, LNC:2 * LNC],
                                    1.0 / D, None, ALU.mult), "vc")
                e.wait_ge(S["sc"], b_sc + 5 * ch + 5)
                e.reciprocal(rwv["rr"], rwv["ms"])
                inc(e.tensor_copy(rAB[:, 0:LNC], rwv["rr"]), "vc")
                e.wait_ge(S["tn"], b_tn + 5 * ch + 5)
                for kc in range(FC):
                    last = e.scalar_tensor_tensor(
                        hn_v(kc, t0), h1_v(kc),
                        nwsb[:, kc:kc + 1], psB[:, 512:512 + LNC],
                        ALU.mult, ALU.mult)
                inc(last, "vc")

    if "hn" in dumps or "xp" in dumps or "mv" in dumps:
        with nc.Block() as blk:
            @blk.gpsimd
            def _(e):
                if "hn" in dumps:
                    inc(e.dma_start(
                        dumps["hn"].ap().rearrange("p a b -> p (a b)"), hn),
                        "gp", 16)
                if "xp" in dumps:
                    inc(e.dma_start(dumps["xp"][:, :, :], xo[:, :, :]), "gp", 16)
                if "mv" in dumps:
                    inc(e.dma_start(dumps["mv"][:, :], mv[:, :]), "gp", 16)
                e.wait_ge(S["gp"], V["gp"])

    SHUF = []
    for i in range(32):
        SHUF.append(i ^ 1)

    rqcv = arb(RQC_O, T)
    rqsv = arb(RQS_O, T)
    rkcv = arb(RKC_O, TH)
    rksv = arb(RKS_O, TH)
    lasv = arf(LAS_O, 1024)
    lasb = arb(LASB_O, 1024)
    lbsv = {k: arb(LBS_O[k], 2048) for k in "qkvo"}
    tmplv = arb(TMPL_O, 1536)
    tmpov = arf(TMPO_O, 1024)
    qgv = arb(QG_O, GH * T).rearrange("p (a b) -> p a b", a=GH)
    kgv = arb(KG_O, GH * TH).rearrange("p (a b) -> p a b", a=GH)
    vgv = arb(VG_O, 12 * GH * 128).rearrange("p (a b) -> p a b", a=12)
    atgv = arb(ATG_O, GH * T)
    wsv = arf(WS_O, 512)
    wbv = arb(WB_O[0], 2048).rearrange("p (a b) -> p a b", a=FC)
    expPv = arb(EXP_O, 640)
    ptbv = arb(PT_O, 640)
    vtmpv = arb(VT_O, 512)
    qrawv = arb(QR_O[0], 512)
    qrotv = arb(QR_O[1], 512)
    rbbv = arb(QR_O[2], 512)
    hn_r = hn.rearrange("p (a b) -> p a b", a=FC)

    with nc.Block() as blk:
        @blk.sync
        def _(e):
            inc(e.dma_start(rqcv, ropeqc[:, :]), "ld", 16)
            inc(e.dma_start(rqsv, ropeqs[:, :]), "ld", 16)
            inc(e.dma_start(rkcv, ropekc[:, :]), "ld", 16)
            inc(e.dma_start(rksv, ropeks[:, :]), "ld", 16)
            inc(e.dma_start(
                lasv.rearrange("p (a b c) -> p a b c", a=4, b=FC),
                lA[:, :, :, :].transpose([2, 0, 1, 3])), "ld", 16)
            for k in "qkvo":
                inc(e.dma_start(lbsv[k][0:R, :], lB[k][:, :]), "ld", 16)
            V["p3ld"] = V["ld"]

        @blk.scalar
        def _(e):
            e.wait_ge(S["ld"], V["p3ld"])
            inc(e.activation(lasb, lasv, AF.Copy), "sc")

    PRJ = [("q", 0, 2, HB), ("k", 1, 3, 0), ("v", 2, 3, 0)]

    def emit_proj(g):
        sd = Sched(V)
        S_ = S
        psum_free = [None, None]
        wb_free = None
        ws_free = None
        psT_free = None
        pd = {}
        qraw_free = None
        qrot_free = None
        rbb_free = None
        vtmp_free = None

        for pr, pi, NT, toff in PRJ:
            for tch in range(NT):
                def mkA(pi=pi, tch=tch, toff=toff):
                    def f(e):
                        for kc in range(FC):
                            mm = e.matmul(
                                psC[0:R, 0:512],
                                lasb[:, (pi * FC + kc) * R:(pi * FC + kc + 1) * R],
                                hn_r[:, kc, toff + tch * 512:toff + (tch + 1) * 512],
                                start=(kc == 0), stop=(kc == FC - 1))
                        return mm
                    return f
                deps = [pd[(pr, tch - 1, "ev")]] if tch >= 1 else                     ([pd[(PRJ[pi - 1][0], PRJ[pi - 1][2] - 1, "ev")]] if pi >= 1 else [])
                tA = sd.add("tensor", deps, mkA())
                def mkAe(pr=pr, tch=tch):
                    def f(e):
                        return e.activation(
                            tmplv[0:R, tch * 512:(tch + 1) * 512],
                            psC[0:R, 0:512], AF.Copy, scale=1.0 / R)
                    return f
                eA = sd.add("scalar", [tA], mkAe())
                pd[(pr, tch, "ev")] = eA

            for oc4 in range(GH):
                h = GH * g + oc4
                casts = []
                for q4 in range(4):
                    def mkD(pr=pr, h=h, q4=q4):
                        def f(e):
                            return e.dma_start(
                                wsv.rearrange("p (a b) -> p a b", a=4),
                                wW[pr][h][q4 * 512:(q4 + 1) * 512, :]
                                .rearrange("(a p) j -> p a j", p=128))
                        return f
                    dd = [ws_free] if ws_free is not None else []
                    dq = sd.add("sync", dd, mkD(), n=16, sem="lws")
                    def mkC(q4=q4):
                        def f(e):
                            return e.activation(
                                wbv[:, 4 * q4:4 * (q4 + 1), :],
                                wsv.rearrange("p (a b) -> p a b", a=4),
                                AF.Copy)
                        return f
                    cd = [dq] + ([wb_free] if wb_free is not None else [])
                    ct = sd.add("scalar", cd, mkC())
                    ws_free = ct
                    casts.append(ct)

                for tch in range(NT):
                    slot = (oc4 * NT + tch) % 2
                    def mkM(pr=pr, h=h, tch=tch, toff=toff, slot=slot):
                        def f(e):
                            p = psA[:, slot * 512:(slot + 1) * 512]
                            for kc in range(FC):
                                e.matmul(p, wbv[:, kc, :],
                                         hn_r[:, kc, toff + tch * 512:
                                              toff + (tch + 1) * 512],
                                         start=(kc == 0), stop=False)
                            return e.matmul(
                                p, lbsv[pr][0:R, h * 128:(h + 1) * 128],
                                tmplv[0:R, tch * 512:(tch + 1) * 512],
                                start=False, stop=True)
                        return f
                    mdeps = casts + [pd[(pr, tch, "ev")]]
                    if psum_free[slot] is not None:
                        mdeps += psum_free[slot]
                    mm = sd.add("tensor", mdeps, mkM())
                    wb_free = mm

                    if pr in ("q", "k"):
                        cosv = rqcv if pr == "q" else rkcv
                        sinv = rqsv if pr == "q" else rksv
                        dst = (qgv if pr == "q" else kgv)
                        def mkES(slot=slot):
                            def f(e):
                                return e.copy(
                                    qrawv, psA[:, slot * 512:(slot + 1) * 512])
                            return f
                        esd = [mm] + ([qraw_free] if qraw_free is not None else [])
                        evS = sd.add("scalar", esd, mkES())
                        def mkSh():
                            def f(e):
                                return e.stream_shuffle(qrotv, qrawv, SHUF)
                            return f
                        sh_d = [evS] + ([qrot_free] if qrot_free is not None else [])
                        evV1 = sd.add("vector", sh_d, mkSh())
                        qraw_free = evV1
                        def mkA2(slot=slot, oc4=oc4, tch=tch, cosv=cosv, dst=dst):
                            def f(e):
                                return e.tensor_tensor(
                                    dst[:, oc4, tch * 512:(tch + 1) * 512],
                                    psA[:, slot * 512:(slot + 1) * 512],
                                    cosv[:, tch * 512:(tch + 1) * 512], ALU.mult)
                            return f
                        evV2 = sd.add("vector", [mm], mkA2())
                        def mkB(sinv=sinv, tch=tch):
                            def f(e):
                                return e.tensor_tensor(
                                    rbbv, qrotv,
                                    sinv[:, tch * 512:(tch + 1) * 512], ALU.mult)
                            return f
                        gb_d = [evV1] + ([rbb_free] if rbb_free is not None else [])
                        evG = sd.add("gpsimd", gb_d, mkB())
                        qrot_free = evG
                        def mkAdd(oc4=oc4, tch=tch, dst=dst):
                            def f(e):
                                return e.tensor_tensor(
                                    dst[:, oc4, tch * 512:(tch + 1) * 512],
                                    dst[:, oc4, tch * 512:(tch + 1) * 512],
                                    rbbv, ALU.add)
                            return f
                        evV3 = sd.add("vector", [evG], mkAdd())
                        rbb_free = evV3
                        psum_free[slot] = [evS, evV2]
                    else:
                        def mkEV(slot=slot):
                            def f(e):
                                return e.copy(
                                    vtmpv, psA[:, slot * 512:(slot + 1) * 512])
                            return f
                        evd = [mm] + ([vtmp_free] if vtmp_free is not None else [])
                        evS = sd.add("scalar", evd, mkEV())
                        def mkT():
                            def f(e):
                                for j in range(4):
                                    t = e.transpose(psT[:, j * 128:(j + 1) * 128],
                                                    vtmpv[:, j * 128:(j + 1) * 128],
                                                    idn[:, :])
                                return t
                            return f
                        td = [evS] + ([psT_free] if psT_free is not None else [])
                        tT = sd.add("tensor", td, mkT())
                        vtmp_free = tT
                        def mkVE(oc4=oc4, tch=tch):
                            def f(e):
                                for j in range(4):
                                    t = e.tensor_copy(
                                        vgv[:, tch * 4 + j,
                                            oc4 * 128:(oc4 + 1) * 128],
                                        psT[:, j * 128:(j + 1) * 128])
                                return t
                            return f
                        evV = sd.add("vector", [tT], mkVE())
                        psT_free = evV
                        psum_free[slot] = [evS]
        sd.emit(nc, S_, V, inc)

    def emit_attn(g):
        sd = Sched(V)
        exp_prev = []
        av_prev = None
        pt_slot_free = [None, None]
        expP_free = None
        mask_step = None
        for qt in range(8):
            if qt < 4:
                def mkMD(qt=qt):
                    def f(e):
                        return e.dma_start(mskEb[:, :, :],
                                           maskE[qt].transpose([1, 0, 2]))
                    return f
                mask_step = sd.add("sync", exp_prev + ([mask_step] if mask_step else []), mkMD(), n=16, sem="lmk")
            for oc4 in range(GH):
                def mkS(g=g, oc4=oc4, qt=qt):
                    def f(e):
                        qtile = qgv[:, oc4, qt * 128:(qt + 1) * 128]
                        e.matmul(psA[:, 1024:1536], qtile,
                                 kgv[:, oc4, qt * 128:qt * 128 + 512],
                                 start=True, stop=True)
                        return e.matmul(psA[:, 1536:1664], qtile,
                                        kgv[:, oc4, qt * 128 + 512:qt * 128 + 640],
                                        start=True, stop=True)
                    return f
                sdeps = list(exp_prev)
                if av_prev is not None:
                    sdeps.append(av_prev)
                s12 = sd.add("tensor", sdeps, mkS())
                def mkMk(qt=qt):
                    def f(e):
                        if qt < 4:
                            for s in range(4):
                                t = e.tensor_tensor(
                                    psA[:, 1024 + s * 128:1024 + (s + 1) * 128],
                                    psA[:, 1024 + s * 128:1024 + (s + 1) * 128],
                                    mskEb[:, s, :], ALU.add)
                            t = e.tensor_tensor(psA[:, 1536:1664],
                                                psA[:, 1536:1664],
                                                mskEb[:, 4, :], ALU.add)
                        else:
                            e.tensor_tensor(psA[:, 1024:1152], psA[:, 1024:1152],
                                            mskSb[:, 0, :], ALU.add)
                            t = e.tensor_tensor(psA[:, 1536:1664], psA[:, 1536:1664],
                                                mskSb[:, 1, :], ALU.add)
                        return t
                    return f
                mdeps = [s12] + ([mask_step] if qt < 4 else [])
                mk = sd.add("vector", mdeps, mkMk())
                def mkRx():
                    def f(e):
                        e.reduce_max(rmx1[:, :], psA[:, 1024:1536], AX.X)
                        e.reduce_max(rmx2[:, :], psA[:, 1536:1664], AX.X)
                        e.drain()
                        e.tensor_tensor(rmx1[:, :], rmx1[:, :], rmx2[:, :], ALU.max)
                        e.drain()
                        return e.tensor_scalar(rmx1[:, :], rmx1[:, :], -1.0,
                                               None, ALU.mult)
                    return f
                rx = sd.add("vector", [mk], mkRx())
                def mkEx():
                    def f(e):
                        e.activation(expPv[0:512].rearrange("p n -> p n") if False
                                     else expPv[:, 0:512] if False else expPv[:, :][:, 0:512],
                                     psA[:, 1024:1536], AF.Exp,
                                     bias=rmx1[:, :], accum_out=racc1[:, :])
                        return e.activation(expPv[:, :][:, 512:640],
                                            psA[:, 1536:1664], AF.Exp,
                                            bias=rmx1[:, :], accum_out=racc2[:, :])
                    return f
                edeps = [rx] + ([expP_free] if expP_free is not None else [])
                ex = sd.add("scalar", edeps, mkEx())
                exp_prev = [ex]
                def mkRs():
                    def f(e):
                        e.tensor_tensor(rsum[:, :], racc1[:, :], racc2[:, :], ALU.add)
                        e.drain()
                        e.reciprocal(rsum[:, :], rsum[:, :])
                        e.drain()
                        return e.tensor_scalar(diag[:, :], idn[:, :],
                                               rsum[:, :], None, ALU.mult)
                    return f
                rs = sd.add("vector", [ex], mkRs())
                evpt = []
                for s5 in range(5):
                    slot = s5 % 2
                    off = 0 if slot == 0 else 512
                    def mkPT(s5=s5, off=off):
                        def f(e):
                            return e.matmul(psB[:, off:off + 128],
                                            expPv[:, :][:, s5 * 128:(s5 + 1) * 128],
                                            diag[:, :], start=True, stop=True)
                        return f
                    pdeps = [rs]
                    if pt_slot_free[slot] is not None:
                        pdeps.append(pt_slot_free[slot])
                    pt = sd.add("tensor", pdeps, mkPT())
                    def mkPE(s5=s5, off=off):
                        def f(e):
                            return e.tensor_copy(
                                ptbv[:, s5 * 128:(s5 + 1) * 128],
                                psB[:, off:off + 128])
                        return f
                    pe = sd.add("vector", [pt], mkPE())
                    pt_slot_free[slot] = pe
                    evpt.append(pe)
                expP_free = evpt[-1]
                def mkPV(g=g, oc4=oc4, qt=qt):
                    def f(e):
                        for s5 in range(5):
                            t = e.matmul(psA[:, 1664:1792],
                                         vgv[:, qt + s5, oc4 * 128:(oc4 + 1) * 128],
                                         ptbv[:, s5 * 128:(s5 + 1) * 128],
                                         start=(s5 == 0), stop=(s5 == 4))
                        return t
                    return f
                pvdeps = list(evpt)
                if av_prev is not None:
                    pvdeps.append(av_prev)
                pv = sd.add("tensor", pvdeps, mkPV())
                def mkAV(oc4=oc4, qt=qt):
                    def f(e):
                        return e.tensor_copy(
                            atgv[:, oc4 * T + qt * 128:oc4 * T + (qt + 1) * 128],
                            psA[:, 1664:1792])
                    return f
                av = sd.add("vector", [pv], mkAV())
                av_prev = av

        ws_free = None
        wb_free = None
        psum_free = [None, None]
        for oc in range(FC):
            def mkD(oc=oc, g=g):
                def f(e):
                    return e.dma_start(
                        wsv.rearrange("p (a b) -> p a b", a=4),
                        wW["o"][oc][g * 512:(g + 1) * 512, :]
                        .rearrange("(a p) j -> p a j", p=128))
                return f
            dd = [ws_free] if ws_free is not None else []
            dq = sd.add("sync", dd, mkD(), n=16, sem="lws")
            def mkC():
                def f(e):
                    return e.activation(
                        wbv[:, 0:4, :],
                        wsv.rearrange("p (a b) -> p a b", a=4), AF.Copy)
                return f
            cd = [dq, av_prev] + ([wb_free] if wb_free is not None else [])
            ct = sd.add("scalar", cd, mkC())
            ws_free = ct
            for tch in range(2):
                slot = tch
                def mkOM(oc=oc, tch=tch, slot=slot):
                    def f(e):
                        p = psA[:, slot * 512:(slot + 1) * 512]
                        for hl in range(GH):
                            t = e.matmul(p, wbv[:, hl, :],
                                         atgv[:, hl * T + tch * 512:
                                              hl * T + (tch + 1) * 512],
                                         start=(hl == 0), stop=(hl == GH - 1))
                        return t
                    return f
                odeps = [ct]
                if psum_free[slot] is not None:
                    odeps.append(psum_free[slot])
                om = sd.add("tensor", odeps, mkOM())
                wb_free = om
                def mkOX(oc=oc, tch=tch, slot=slot):
                    def f(e):
                        xv = xo[:, oc, tch * 512:(tch + 1) * 512]
                        return e.scalar_tensor_tensor(
                            xv, psA[:, slot * 512:(slot + 1) * 512],
                            mv[:, MVO["ga"] + oc:MVO["ga"] + oc + 1],
                            xv, ALU.mult, ALU.add)
                    return f
                ox = sd.add("vector", [om], mkOX())
                psum_free[slot] = ox
        oe_prev = None
        for tch in range(2):
            def mkOA(g=g, tch=tch):
                def f(e):
                    for hl in range(GH):
                        h = GH * g + hl
                        t = e.matmul(
                            psC[0:R, 0:512],
                            lasb[:, (3 * FC + h) * R:(3 * FC + h + 1) * R],
                            atgv[:, hl * T + tch * 512:hl * T + (tch + 1) * 512],
                            start=(hl == 0), stop=(hl == GH - 1))
                    return t
                return f
            adeps = [av_prev] + ([oe_prev] if oe_prev is not None else [])
            oa = sd.add("tensor", adeps, mkOA())
            def mkOE(g=g, tch=tch):
                def f(e):
                    dst = tmpov[0:R, tch * 512:(tch + 1) * 512]
                    if g == 0:
                        return e.tensor_copy(dst, psC[0:R, 0:512])
                    return e.tensor_tensor(dst, dst, psC[0:R, 0:512], ALU.add)
                return f
            oe = sd.add("vector", [oa], mkOE())
            oe_prev = oe
        sd.emit(nc, S_, V, inc)

    S_ = S
    for g in range(NG):
        emit_proj(g)
        if g == 0 and "arena" in dumps:
            with nc.Block() as blk:
                @blk.gpsimd
                def _(e):
                    inc(e.dma_start(dumps["arena"][:, :], arena[:, :]), "gp", 16)
                    e.wait_ge(S["gp"], V["gp"])
        if g == 0 and "q" in dumps:
            with nc.Block() as blk:
                @blk.gpsimd
                def _(e):
                    inc(e.dma_start(dumps["q"][:, :, :],
                                    arb(QG_O, GH * T).rearrange("p (a b) -> p a b", a=GH)), "gp", 16)
                    inc(e.dma_start(dumps["k"][:, :, :],
                                    arb(KG_O, GH * TH).rearrange("p (a b) -> p a b", a=GH)), "gp", 16)
                    inc(e.dma_start(dumps["v"][:, :, :],
                                    arb(VG_O, 12 * GH * 128).rearrange("p (a b) -> p a b", a=12)), "gp", 16)
                    e.wait_ge(S["gp"], V["gp"])
        emit_attn(g)
        if g == 0 and "attn" in dumps:
            with nc.Block() as blk:
                @blk.gpsimd
                def _(e):
                    inc(e.dma_start(dumps["attn"][:, :, :],
                                    atgv.rearrange("p (a b) -> p a b", a=GH)), "gp", 16)
                    e.wait_ge(S["gp"], V["gp"])

    with nc.Block() as blk:
        @blk.scalar
        def _(e):
            inc(e.activation(tmplv[0:R, 0:T], tmpov[0:R, 0:T], AF.Copy,
                             scale=1.0 / R), "sc")
            V["obc"] = V["sc"]

    sd = Sched(V)
    psum_free = [None, None]
    dummy = sd.add("scalar", [], lambda e: e.activation(
        vtmpv[0:1, 0:2], vtmpv[0:1, 0:2], AF.Copy))
    for oc in range(FC):
        for tch in range(2):
            slot = tch
            def mkBM(oc=oc, tch=tch, slot=slot):
                def f(e):
                    return e.matmul(psA[:, slot * 512:(slot + 1) * 512],
                                    lbsv["o"][0:R, oc * 128:(oc + 1) * 128],
                                    tmplv[0:R, tch * 512:(tch + 1) * 512],
                                    start=True, stop=True)
                return f
            bdeps = [dummy]
            if psum_free[slot] is not None:
                bdeps.append(psum_free[slot])
            bm = sd.add("tensor", bdeps, mkBM())
            def mkBX(oc=oc, tch=tch, slot=slot):
                def f(e):
                    xv = xo[:, oc, tch * 512:(tch + 1) * 512]
                    return e.scalar_tensor_tensor(
                        xv, psA[:, slot * 512:(slot + 1) * 512],
                        mv[:, MVO["ga"] + oc:MVO["ga"] + oc + 1],
                        xv, ALU.mult, ALU.add)
                return f
            bx = sd.add("vector", [bm], mkBX())
            psum_free[slot] = bx
    sd.emit(nc, S_, V, inc)

    H2_O2 = 0
    INT_O2 = 16384
    H1B2 = 16384
    CST2 = 24576
    RW2 = 28672
    RAB2 = 31232
    WDS_O = 32768
    GUB_O = [65536, 67584]
    GELB_O = 69632

    h2v = arb(H2_O2, FC * T)
    h2_r = h2v.rearrange("p (a b) -> p a b", a=FC)
    cst2 = arb(CST2, FC * LNC)
    h1b2 = arf(H1B2, FC * LNC)
    rwv2 = {}
    for j, k in enumerate(["mean", "ms", "var", "rr", "bb"]):
        rwv2[k] = arf(RW2 + 2 * j * LNC, LNC)[0:1, :]
    rAB2 = arb(RAB2, 2 * LNC)[0:1, :]

    def cst2_v(kc):
        return cst2.rearrange("p (a b) -> p a b", a=FC)[:, kc, :]

    def h1b2_v(kc):
        return h1b2.rearrange("p (a b) -> p a b", a=FC)[:, kc, :]

    with nc.Block() as blk:
        b_ld, b_sc, b_tn, b_vc = V["ld"], V["sc"], V["tn"], V["vc"]

        @blk.scalar
        def _(e):
            for ch in range(NLN2):
                if ch >= 1:
                    e.wait_ge(S["tn"], b_tn + 3 * (ch - 1) + 2)
                inc(e.activation(cst2.rearrange("p (a b) -> p a b", a=FC),
                                 xo[:, :, ch * LNC:(ch + 1) * LNC],
                                 AF.Copy), "sc")
                e.wait_ge(S["tn"], b_tn + 3 * ch + 1)
                inc(e.activation(cst2, cst2, AF.Square), "sc")
                e.wait_ge(S["vc"], b_vc + 3 * ch + 1)
                inc(e.activation(rwv2["var"], rwv2["var"], AF.Sqrt,
                                 bias=epsb[0:1, 0:1]), "sc")

        @blk.tensor
        def _(e):
            for ch in range(NLN2):
                e.wait_ge(S["sc"], b_sc + 3 * ch + 1)
                if ch >= 1:
                    e.wait_ge(S["vc"], b_vc + 3 * (ch - 1) + 1)
                for kc in range(FC):
                    mm = e.matmul(psC[0:1, 0:LNC], onec[:, :], cst2_v(kc),
                                  start=(kc == 0), stop=(kc == FC - 1))
                inc(mm, "tn")
                e.wait_ge(S["sc"], b_sc + 3 * ch + 2)
                for kc in range(FC):
                    mm = e.matmul(psC[0:1, LNC:2 * LNC], onec[:, :], cst2_v(kc),
                                  start=(kc == 0), stop=(kc == FC - 1))
                inc(mm, "tn")
                e.wait_ge(S["vc"], b_vc + 3 * ch + 2)
                inc(e.matmul(psB[:, 0:2 * LNC], oner[:, :], rAB2,
                             start=True, stop=True), "tn")

        @blk.vector
        def _(e):
            for ch in range(NLN2):
                e.wait_ge(S["tn"], b_tn + 3 * ch + 2)
                e.tensor_scalar(rwv2["mean"], psC[0:1, 0:LNC], 1.0 / D, None, ALU.mult)
                e.tensor_scalar(rwv2["ms"], psC[0:1, LNC:2 * LNC], 1.0 / D, None, ALU.mult)
                e.tensor_tensor(rwv2["var"], rwv2["mean"], rwv2["mean"], ALU.mult)
                inc(e.tensor_tensor(rwv2["var"], rwv2["ms"], rwv2["var"],
                                    ALU.subtract), "vc")
                e.wait_ge(S["sc"], b_sc + 3 * ch + 3)
                e.reciprocal(rwv2["rr"], rwv2["var"])
                e.tensor_tensor(rwv2["bb"], rwv2["mean"], rwv2["rr"], ALU.mult)
                e.tensor_scalar(rwv2["bb"], rwv2["bb"], -1.0, None, ALU.mult)
                e.tensor_copy(rAB2[:, 0:LNC], rwv2["rr"])
                inc(e.tensor_copy(rAB2[:, LNC:2 * LNC], rwv2["bb"]), "vc")
                e.wait_ge(S["tn"], b_tn + 3 * ch + 3)
                for kc in range(FC):
                    e.tensor_tensor(h1b2_v(kc), xo[:, kc, ch * LNC:(ch + 1) * LNC],
                                    psB[:, 0:LNC], ALU.mult)
                    e.tensor_tensor(h1b2_v(kc), h1b2_v(kc), psB[:, LNC:2 * LNC],
                                    ALU.add)
                    last = e.tensor_scalar(
                        h2_r[:, kc, ch * LNC:(ch + 1) * LNC], h1b2_v(kc),
                        mv[:, MVO["cb1"] + kc:MVO["cb1"] + kc + 1],
                        mv[:, MVO["sb"] + kc:MVO["sb"] + kc + 1],
                        ALU.mult, ALU.add)
                inc(last, "vc")

    if "h2" in dumps:
        with nc.Block() as blk:
            @blk.gpsimd
            def _(e):
                inc(e.dma_start(dumps["h2"][:, :, :], h2_r), "gp", 16)
                e.wait_ge(S["gp"], V["gp"])

    INT_O3 = 16384
    WDS_O = 24576
    GUB_O = [57344, 59392, 61440, 63488]
    GELB_O = [65536, 66048]

    interh = arb(INT_O3, FC * 512).rearrange("p (a b) -> p a b", a=FC)

    sd = Sched(V)
    first_dma = True
    wds_free = [None] * FC
    inter_free = None
    super_barrier = []
    gub_free = [None] * 4
    gel_hist = {}
    imul_hist = {}
    dstt_hist = {}
    for hf in range(2):
        t0 = hf * 512
        for sup in range(8):
            ddmas = []
            for nl2 in range(FC):
                ncg = sup * FC + nl2
                score = ncg >> 4
                loc = ncg & 15
                doff = 2 * NSL + loc * (128 * D)
                def mkDD(score=score, doff=doff, nl2=nl2, first=first_dma):
                    def f(e):
                        if first:
                            e.wait_ge(S["cc"], V["deqcc"])
                        return e.dma_start(
                            arb(WDS_O + nl2 * 2048, 2048),
                            deq_out[score, doff:doff + 128 * D]
                            .rearrange("(p n) -> p n", p=128))
                    return f
                ddd = [wds_free[nl2]] if wds_free[nl2] is not None else []
                ddmas.append(sd.add("sync", ddd, mkDD(), n=16, sem="ldd"))
                first_dma = False

            imuls = []
            for nl in range(FC):
                ncg = sup * FC + nl
                score = ncg >> 4
                loc = ncg & 15
                goff = loc * (FC * 128 * 128)
                gslot = (2 * nl) % 4
                uslot = (2 * nl + 1) % 4
                pslot = nl % 2

                def mkGD(score=score, goff=goff, gslot=gslot, first=first_dma):
                    def f(e):
                        if first:
                            e.wait_ge(S["cc"], V["deqcc"])
                        return e.dma_start(
                            arb(GUB_O[gslot], 2048),
                            deq_out[score, goff:goff + 2048 * 128]
                            .rearrange("(p n) -> p n", p=128))
                    return f
                gdd = [gub_free[gslot]] if gub_free[gslot] is not None else []
                gdma = sd.add("sync", gdd, mkGD(), n=16, sem="lb%d" % gslot)
                first_dma = False

                def mkUD(score=score, goff=goff, uslot=uslot):
                    def f(e):
                        return e.dma_start(
                            arb(GUB_O[uslot], 2048),
                            deq_out[score, NSL + goff:NSL + goff + 2048 * 128]
                            .rearrange("(p n) -> p n", p=128))
                    return f
                udd = [gub_free[uslot]] if gub_free[uslot] is not None else []
                udma = sd.add("sync", udd, mkUD(), n=16, sem="lb%d" % uslot)

                def mkGM(gslot=gslot, pslot=pslot, t0=t0):
                    def f(e):
                        gb = arb(GUB_O[gslot], 2048).rearrange(
                            "p (a b) -> p a b", a=FC)
                        for kc in range(FC):
                            mm = e.matmul(
                                psA[:, pslot * 512:(pslot + 1) * 512],
                                gb[:, kc, :],
                                h2_r[:, kc, t0:t0 + 512],
                                start=(kc == 0), stop=(kc == FC - 1))
                        return mm
                    return f
                gdeps = [gdma]
                if (hf, sup, nl - 2) in gel_hist:
                    gdeps.append(gel_hist[(hf, sup, nl - 2)])
                elif nl < 2 and (sup > 0 or hf > 0):
                    psup, phf = (sup - 1, hf) if sup > 0 else (7, hf - 1)
                    gdeps.append(gel_hist[(phf, psup, FC - 2 + nl)])
                gmm = sd.add("tensor", gdeps, mkGM())
                gub_free[gslot] = gmm

                def mkUM(uslot=uslot, pslot=pslot, t0=t0):
                    def f(e):
                        ub = arb(GUB_O[uslot], 2048).rearrange(
                            "p (a b) -> p a b", a=FC)
                        for kc in range(FC):
                            mm = e.matmul(
                                psA[:, 1024 + pslot * 512:1024 + (pslot + 1) * 512],
                                ub[:, kc, :],
                                h2_r[:, kc, t0:t0 + 512],
                                start=(kc == 0), stop=(kc == FC - 1))
                        return mm
                    return f
                udeps = [udma]
                if (hf, sup, nl - 2) in imul_hist:
                    udeps.append(imul_hist[(hf, sup, nl - 2)])
                elif nl < 2 and (sup > 0 or hf > 0):
                    psup, phf = (sup - 1, hf) if sup > 0 else (7, hf - 1)
                    udeps.append(imul_hist[(phf, psup, FC - 2 + nl)])
                umm = sd.add("tensor", udeps, mkUM())
                gub_free[uslot] = umm

                def mkGel(pslot=pslot, nl=nl):
                    def f(e):
                        return e.activation(
                            arb(GELB_O[nl % 2], 512),
                            psA[:, pslot * 512:(pslot + 1) * 512],
                            AF.Gelu_apprx_tanh)
                    return f
                geld = [gmm]
                if (hf, sup, nl - 2) in imul_hist:
                    geld.append(imul_hist[(hf, sup, nl - 2)])
                elif nl < 2 and (sup > 0 or hf > 0):
                    psup, phf = (sup - 1, hf) if sup > 0 else (7, hf - 1)
                    geld.append(imul_hist[(phf, psup, FC - 2 + nl)])
                gel = sd.add("scalar", geld, mkGel())
                gel_hist[(hf, sup, nl)] = gel

                def mkIM(nl=nl, pslot=pslot):
                    def f(e):
                        return e.tensor_tensor(
                            interh[:, nl, :],
                            arb(GELB_O[nl % 2], 512),
                            psA[:, 1024 + pslot * 512:1024 + (pslot + 1) * 512],
                            ALU.mult)
                    return f
                imdeps = [gel, umm]
                if inter_free is not None:
                    imdeps.append(inter_free)
                im = sd.add("vector", imdeps, mkIM())
                imul_hist[(hf, sup, nl)] = im
                imuls.append(im)

            if hf == 1 and sup == 7 and "inter" in dumps:
                def mkID():
                    def f(e):
                        return e.dma_start(dumps["inter"][:, :, :], interh)
                    return f
                sd.add("gpsimd", list(imuls), mkID(), n=16)
            last_dmm = None
            for hc in range(FC):
                slot = hc % 2
                def mkDM(hc=hc, slot=slot):
                    def f(e):
                        for ns in range(FC):
                            wt = arb(WDS_O + ns * 2048, 2048).rearrange(
                                "p (a b) -> p a b", a=FC)
                            mm = e.matmul(
                                psB[:, slot * 512:(slot + 1) * 512],
                                wt[:, hc, :],
                                interh[:, ns, :],
                                start=(ns == 0), stop=(ns == FC - 1))
                        return mm
                    return f
                dmdeps = ddmas + imuls
                if (hf, sup, hc - 2) in dstt_hist:
                    dmdeps.append(dstt_hist[(hf, sup, hc - 2)])
                elif hc < 2 and (sup > 0 or hf > 0):
                    psup, phf = (sup - 1, hf) if sup > 0 else (7, hf - 1)
                    dmdeps.append(dstt_hist[(phf, psup, FC - 2 + hc)])

                dmm = sd.add("tensor", dmdeps, mkDM())
                last_dmm = dmm
                def mkDX(hc=hc, slot=slot, t0=t0):
                    def f(e):
                        xv = xo[:, hc, t0:t0 + 512]
                        if FFN_NO_DOWN:
                            return e.tensor_copy(gcl[:, 0:2], gcl[:, 0:2])
                        return e.scalar_tensor_tensor(
                            xv, psB[:, slot * 512:(slot + 1) * 512],
                            mv[:, MVO["gb"] + hc:MVO["gb"] + hc + 1],
                            xv, ALU.mult, ALU.add)
                    return f
                dx = sd.add("vector", [dmm], mkDX())
                dstt_hist[(hf, sup, hc)] = dx
            for nl2 in range(FC):
                wds_free[nl2] = last_dmm
            inter_free = last_dmm
    sd.emit(nc, S_, V, inc)

    with nc.Block() as blk:
        @blk.gpsimd
        def _(e):
            inc(e.dma_start(xout.rearrange("(a p) t -> p a t", p=128),
                            xo[:, :, :]), "gp", 16)
            e.wait_ge(S["gp"], V["gp"])

    stk.close()
    return nc, dumps, V


def _run(inputs, dump=()):
    in_maps = prepare_inputs(inputs)
    nc, dumps, V = build_nc(dump=dump)
    res = run_bass_kernel_spmd(nc, in_maps, core_ids=list(range(NCORE)))
    return res


def kernel(**inputs):
    res = _run(inputs, dump=DUMP)
    outs = [r["xout"] for r in res.results]
    full = np.concatenate([o.T for o in outs], axis=0)
    return np.ascontiguousarray(full.reshape(1, 8192, D).astype(np.float32))


# version 06
# speedup vs baseline: 1.0528x; 1.0528x over previous
import contextlib
import numpy as np
import ml_dtypes

import concourse.bass as bass
import concourse.mybir as mybir
from concourse.bass_utils import run_bass_kernel_spmd

F32 = mybir.dt.float32
BF16 = mybir.dt.bfloat16
AF = mybir.ActivationFunctionType
ALU = mybir.AluOpType
AX = mybir.AxisListType

NCORE = 8
D = 2048
FC = 16
T = 1024
HB = 512
TH = T + HB
H = 16
DH = 128
GH = 4
NG = H // GH
R = 16
MLP = 16384
NCK = 16
LNC = 256
NLN1 = TH // LNC
NLN2 = T // LNC

BF = ml_dtypes.bfloat16
DUMP = ()
FFN_NO_DOWN = False
FFN_NO_GU = False
FFN_DOWN_SCRATCH = False
FFN_DOWN_SER = False



def _rope_tables():
    half = DH // 2
    theta = 1.0 / 10000.0 ** (np.arange(half, dtype=np.float64) / half)
    pos = np.arange(-HB, 8192, dtype=np.float64)
    ang = pos[:, None] * theta[None, :]
    cos = np.cos(ang).astype(np.float32)
    sin = np.sin(ang).astype(np.float32)
    cos_p = np.empty((128, HB + 8192), np.float32)
    sin_p = np.empty((128, HB + 8192), np.float32)
    cos_p[0::2] = cos.T
    cos_p[1::2] = cos.T
    sin_p[0::2] = -sin.T
    sin_p[1::2] = sin.T
    return cos_p, sin_p


def _perm_idx():
    idx = np.empty(D, np.int64)
    for h in range(H):
        for j in range(64):
            idx[h * 128 + 2 * j] = h * 128 + j
            idx[h * 128 + 2 * j + 1] = h * 128 + 64 + j
    return idx


def _tile_wT(wT):
    return np.ascontiguousarray(wT.reshape(D, FC, 128).transpose(1, 0, 2))


def _masks_for_core(core):
    NEG = np.float32(-1e9)
    qi = np.arange(128)[:, None]
    kj = np.arange(128)[None, :]
    std = np.zeros((2, 128, 128), np.float32)
    std[0][(kj + 0) < (qi + 1)] = NEG
    std[1][(kj + 512) > (qi + 512)] = NEG
    early = np.zeros((4, 5, 128, 128), np.float32)
    for qt in range(4):
        gq0 = core * T + qt * 128
        for s in range(5):
            m = np.zeros((128, 128), np.float32)
            kglob = gq0 - 512 + s * 128 + kj
            qglob = gq0 + qi
            bad = (kglob > qglob) | (kglob < qglob - 511) | (kglob < 0)
            m[bad] = NEG
            early[qt, s] = m
    return early, std


def _unpack_nvfp4(Wp):
    b = np.asarray(Wp).astype(np.uint8)
    lo = b & 15
    hi = (b >> 4) & 15
    return np.stack([lo, hi], axis=-1).reshape(b.shape[0], -1)


def prepare_inputs(d):
    x = np.asarray(d["L0_attn_x"], np.float32)[0]
    perm = _perm_idx()

    wqT = np.ascontiguousarray(np.asarray(d["L0_attn_wq"], np.float32).T)[:, perm]
    wkT = np.ascontiguousarray(np.asarray(d["L0_attn_wk"], np.float32).T)[:, perm]
    wvT = np.ascontiguousarray(np.asarray(d["L0_attn_wv"], np.float32).T)
    woT = np.ascontiguousarray(np.asarray(d["L0_attn_wo"], np.float32).T)
    wq_t, wk_t, wv_t, wo_t = map(_tile_wT, (wqT, wkT, wvT, woT))

    lAm = np.stack([
        np.asarray(d["L0_attn_qA"], np.float32).T.reshape(FC, 128, R),
        np.asarray(d["L0_attn_kA"], np.float32).T.reshape(FC, 128, R),
        np.asarray(d["L0_attn_vA"], np.float32).T.reshape(FC, 128, R),
        np.asarray(d["L0_attn_oA"], np.float32).T.reshape(FC, 128, R),
    ], axis=0)
    lqB = np.ascontiguousarray(np.asarray(d["L0_attn_qB"], np.float32).T[:, perm]).astype(BF)
    lkB = np.ascontiguousarray(np.asarray(d["L0_attn_kB"], np.float32).T[:, perm]).astype(BF)
    lvB = np.ascontiguousarray(np.asarray(d["L0_attn_vB"], np.float32).T).astype(BF)
    loB = np.ascontiguousarray(np.asarray(d["L0_attn_oB"], np.float32).T).astype(BF)

    normw = np.ascontiguousarray(np.asarray(d["L0_attn_norm_w"], np.float32).reshape(FC, 128).T)
    cvec = np.ascontiguousarray(np.asarray(d["L0_c"], np.float32).reshape(FC, 128).T)
    wmod = np.asarray(d["L0_wmod"], np.float32)

    cos_p, sin_p = _rope_tables()
    SC = np.float32(1.0 / np.sqrt(DH))

    gcT = _unpack_nvfp4(d["L0_ffn_gate_W_packed"]).T
    ucT = _unpack_nvfp4(d["L0_ffn_up_W_packed"]).T
    dcT = _unpack_nvfp4(d["L0_ffn_down_W_packed"]).T
    gsT = np.ascontiguousarray(np.asarray(d["L0_ffn_gate_scales"], np.float32).T)
    usT = np.ascontiguousarray(np.asarray(d["L0_ffn_up_scales"], np.float32).T)
    dsT = np.ascontiguousarray(np.asarray(d["L0_ffn_down_scales"], np.float32).T)

    selm = np.zeros((4, 128), np.float32)
    for p in range(128):
        selm[p >> 5, p] = 1.0

    shared = {
        "wq": wq_t, "wk": wk_t, "wv": wv_t, "wo": wo_t,
        "lA": np.ascontiguousarray(lAm),
        "lqB": lqB, "lkB": lkB, "lvB": lvB, "loB": loB,
        "normw": normw, "cvec": cvec,
        "ident": np.eye(128, dtype=np.float32).astype(BF),
        "onescol": np.ones((128, 1), np.float32).astype(BF),
        "onesrow": np.ones((1, 128), np.float32).astype(BF),
        "one11": np.ones((1, 1), np.float32),
        "epsv": np.array([[1e-5, 1e-6]], np.float32),
        "sel": selm.astype(BF),
    }

    in_maps = []
    for c in range(NCORE):
        s0 = c * T
        xs = np.zeros((TH, D), np.float32)
        lo = max(0, s0 - HB)
        if s0 > lo:
            xs[HB - (s0 - lo):HB] = x[lo:s0]
        xs[HB:] = x[s0:s0 + T]
        xTc = np.ascontiguousarray(xs.T)

        early, std = _masks_for_core(c)
        roff = s0
        im = dict(shared)
        im.update({
            "xT": xTc,
            "wmslab": np.ascontiguousarray(
                wmod[:, c * 1536:(c + 1) * 1536].reshape(FC, 128, 1536)),
            "ropeqc": (np.ascontiguousarray(cos_p[:, roff + HB:roff + HB + T]) * SC).astype(BF),
            "ropeqs": (np.ascontiguousarray(sin_p[:, roff + HB:roff + HB + T]) * SC).astype(BF),
            "ropekc": np.ascontiguousarray(cos_p[:, roff:roff + TH]).astype(BF),
            "ropeks": np.ascontiguousarray(sin_p[:, roff:roff + TH]).astype(BF),
            "maskE": early, "maskS": std,
        })
        n0 = c * NCK * 128
        im["gcodes"] = np.ascontiguousarray(
            gcT[:, n0:n0 + NCK * 128].reshape(FC, 128, NCK, 128)
            .transpose(2, 1, 0, 3)).astype(BF)
        im["ucodes"] = np.ascontiguousarray(
            ucT[:, n0:n0 + NCK * 128].reshape(FC, 128, NCK, 128)
            .transpose(2, 1, 0, 3)).astype(BF)
        im["dcodes"] = np.ascontiguousarray(
            dcT[n0:n0 + NCK * 128].reshape(NCK, 128, D)).astype(BF)
        im["gscale"] = np.ascontiguousarray(
            gsT[:, n0:n0 + NCK * 128].reshape(FC, 4, NCK, 128)
            .transpose(2, 1, 0, 3).reshape(NCK, 4, D))
        im["uscale"] = np.ascontiguousarray(
            usT[:, n0:n0 + NCK * 128].reshape(FC, 4, NCK, 128)
            .transpose(2, 1, 0, 3).reshape(NCK, 4, D))
        im["dscale"] = np.ascontiguousarray(
            dsT[c * 64:(c + 1) * 64].reshape(NCK, 4, D))
        in_maps.append(im)
    return in_maps




class Sched:
    ENG_SEM = {"sync": "ld", "tensor": "tn", "vector": "vc",
               "scalar": "sc", "gpsimd": "gp"}

    def __init__(self, V):
        self.steps = []
        self.base = {v: V.get(v, 0) for v in self.ENG_SEM.values()}

    def add(self, eng, deps, fn, n=1, sem=None):
        sid = len(self.steps)
        self.steps.append((eng, list(deps), fn, n, sem))
        return sid

    def emit(self, nc, S, V, inc):
        cnt = {}
        after = {}
        semof = {}
        for sid, (eng, deps, fn, n, sem) in enumerate(self.steps):
            sname = sem or self.ENG_SEM[eng]
            if sname not in self.base:
                self.base[sname] = V.get(sname, 0)
            cnt[sname] = cnt.get(sname, 0) + n
            after[sid] = (eng, sname, cnt[sname])
        per_eng = {e: [] for e in self.ENG_SEM}
        for sid, (eng, deps, fn, n, sem) in enumerate(self.steps):
            per_eng[eng].append(sid)

        sched = self

        with nc.Block() as blk:
            def make(eng_name):
                def body(e):
                    for sid in per_eng[eng_name]:
                        _, deps, fn, n, sem = sched.steps[sid]
                        sname = sem or sched.ENG_SEM[eng_name]
                        need = {}
                        for dp in deps:
                            deng, dsem, dcnt = after[dp]
                            if deng == eng_name and dsem == sched.ENG_SEM[deng]:
                                continue
                            if deng == eng_name and dsem != sched.ENG_SEM[deng]:
                                pass
                            need[dsem] = max(need.get(dsem, 0),
                                             sched.base[dsem] + dcnt)
                        for s2, val in need.items():
                            e.wait_ge(S[s2], val)
                        last = fn(e)
                        inc(last, sname, n)
                return body
            if per_eng["sync"]:
                blk.sync(make("sync"))
            if per_eng["tensor"]:
                blk.tensor(make("tensor"))
            if per_eng["scalar"]:
                blk.scalar(make("scalar"))
            if per_eng["vector"]:
                blk.vector(make("vector"))
            if per_eng["gpsimd"]:
                blk.gpsimd(make("gpsimd"))


def build_nc(dump=()):
    nc = bass.Bass("TRN2", target_bir_lowering=False)

    def din(name, shape, dtype):
        return nc.dram_tensor(name, shape, dtype, kind="ExternalInput")

    xT = din("xT", [D, TH], F32)
    wW = {"q": din("wq", [FC, D, 128], F32), "k": din("wk", [FC, D, 128], F32),
          "v": din("wv", [FC, D, 128], F32), "o": din("wo", [FC, D, 128], F32)}
    lA = din("lA", [4, FC, 128, R], F32)
    lB = {"q": din("lqB", [R, D], BF16), "k": din("lkB", [R, D], BF16),
          "v": din("lvB", [R, D], BF16), "o": din("loB", [R, D], BF16)}
    normw = din("normw", [128, FC], F32)
    cvec = din("cvec", [128, FC], F32)
    wmslab = din("wmslab", [FC, 128, 1536], F32)
    ropeqc = din("ropeqc", [128, T], BF16)
    ropeqs = din("ropeqs", [128, T], BF16)
    ropekc = din("ropekc", [128, TH], BF16)
    ropeks = din("ropeks", [128, TH], BF16)
    maskE = din("maskE", [4, 5, 128, 128], F32)
    maskS = din("maskS", [2, 128, 128], F32)
    identD = din("ident", [128, 128], BF16)
    onescolD = din("onescol", [128, 1], BF16)
    onesrowD = din("onesrow", [1, 128], BF16)
    one11 = din("one11", [1, 1], F32)
    epsD = din("epsv", [1, 2], F32)
    selD = din("sel", [4, 128], BF16)
    codesD = {"g": din("gcodes", [NCK, 128, FC, 128], BF16),
              "u": din("ucodes", [NCK, 128, FC, 128], BF16),
              "d": din("dcodes", [NCK, 128, D], BF16)}
    scaleD = {"g": din("gscale", [NCK, 4, D], F32),
              "u": din("uscale", [NCK, 4, D], F32),
              "d": din("dscale", [NCK, 4, D], F32)}

    xout = nc.dram_tensor("xout", [D, T], F32, kind="ExternalOutput")
    dumps = {}
    if "hn" in dump:
        dumps["hn"] = nc.dram_tensor("d_hn", [128, FC, TH], BF16, kind="ExternalOutput")
    if "xp" in dump:
        dumps["xp"] = nc.dram_tensor("d_xp", [128, FC, T], F32, kind="ExternalOutput")
    if "mv" in dump:
        dumps["mv"] = nc.dram_tensor("d_mv", [128, 96], F32, kind="ExternalOutput")
    if "qkv" in dump:
        dumps["q"] = nc.dram_tensor("d_q", [128, GH, T], BF16, kind="ExternalOutput")
        dumps["k"] = nc.dram_tensor("d_k", [128, GH, TH], BF16, kind="ExternalOutput")
        dumps["v"] = nc.dram_tensor("d_v", [128, TH // 128, GH * 128], BF16, kind="ExternalOutput")
    if "attn" in dump:
        dumps["attn"] = nc.dram_tensor("d_attn", [128, GH, T], BF16, kind="ExternalOutput")
    if "h2" in dump:
        dumps["h2"] = nc.dram_tensor("d_h2", [128, FC, T], BF16, kind="ExternalOutput")
    if "arena" in dump:
        dumps["arena"] = nc.dram_tensor("d_arena", [128, 70912], BF16, kind="ExternalOutput")
    if "inter" in dump:
        dumps["inter"] = nc.dram_tensor("d_inter", [128, FC, 512], BF16, kind="ExternalOutput")
    if "wdq" in dump:
        dumps["wdq"] = nc.dram_tensor("d_wdq", [3 * NCK * FC * 128 * 128], BF16,
                                      kind="ExternalOutput")

    NSL = NCK * FC * 128 * 128
    deq_in = nc.dram_tensor("deq_in", [3 * NSL], BF16)
    deq_out = nc.dram_tensor("deq_out", [NCORE, 3 * NSL], BF16, addr_space="Shared")
    mod_b = nc.dram_tensor("mod_b", [1, 1536], F32)
    mod_g = nc.dram_tensor("mod_g", [NCORE, 1536], F32)

    stk = contextlib.ExitStack()
    ent = stk.enter_context
    S = {k: ent(nc.semaphore(k)) for k in
         ["ld", "st", "cc", "tn", "vc", "sc", "gp",
          "lb0", "lb1", "lb2", "lb3", "lb4", "lb5", "ldd", "lws", "lmk",
          "lc0", "lc1", "lsf", "lwm", "lmd", "lhx", "st0", "st1",
          "dd0", "dd1", "dd2"]}
    V = {k: 0 for k in S}

    def inc(instr, k, n=1):
        instr.then_inc(S[k], n)
        V[k] += n
        return V[k]

    xo = ent(nc.sbuf_tensor("xo", [128, FC, T], F32))
    mskSb = ent(nc.sbuf_tensor("mskSb", [128, 2, 128], F32))
    mskEb = ent(nc.sbuf_tensor("mskEb", [128, 5, 128], F32))
    idn = ent(nc.sbuf_tensor("idn", [128, 128], BF16))
    diag = ent(nc.sbuf_tensor("diag", [128, 128], BF16))
    mv = ent(nc.sbuf_tensor("mv", [128, 96], F32))
    nwsb = ent(nc.sbuf_tensor("nwsb", [128, FC], F32))
    csb = ent(nc.sbuf_tensor("csb", [128, FC], F32))
    csbb = ent(nc.sbuf_tensor("csbb", [128, FC], BF16))
    onec = ent(nc.sbuf_tensor("onec", [128, 1], BF16))
    oner = ent(nc.sbuf_tensor("oner", [1, 128], BF16))
    one1 = ent(nc.sbuf_tensor("one1", [1, 1], F32))
    epsb = ent(nc.sbuf_tensor("epsb", [1, 2], F32))
    selb = ent(nc.sbuf_tensor("selb", [4, 128], BF16))
    gcl = ent(nc.sbuf_tensor("gcl", [128, FC], F32))
    racc1 = ent(nc.sbuf_tensor("racc1", [128, 1], F32))
    racc2 = ent(nc.sbuf_tensor("racc2", [128, 1], F32))
    rmx1 = ent(nc.sbuf_tensor("rmx1", [128, 1], F32))
    rmx2 = ent(nc.sbuf_tensor("rmx2", [128, 1], F32))
    rsum = ent(nc.sbuf_tensor("rsum", [128, 1], F32))

    ARN = 70912
    arena = ent(nc.sbuf_tensor("arena", [128, ARN], BF16))

    def arb(off, n):
        return arena[:, off:off + n]

    def arf(off, n):
        return arena[:, off:off + 2 * n].bitcast(F32)

    MODROW_O = 0
    WMF_O = 3072
    WMB_O = 6144
    SCF_O = 7680
    SCB_O = 11776
    DQ_C = [13824, 15872]
    DQ_T = [17920, 19968, 22016, 24064]
    DQ_W = [26112, 28160]
    XST_O = 0
    HN_O = 16384
    H1_O = 40960
    CST_O = 49152
    RW_O = 53248
    RAB_O = 55808
    RQC_O, RQS_O, RKC_O, RKS_O = 0, 1024, 2048, 3584
    LAS_O = 5120
    LBS_O = {"q": 7168, "k": 9216, "v": 11264, "o": 13312}
    LASB_O = 15360
    TMPL_O = 40960
    TMPO_O = 42496
    QG_O = 44544
    KG_O = 48640
    VG_O = 54784
    ATG_O = 60928
    WS_O = 65024
    WB_O = [67072]
    SCR_O = 69120
    EXP_O = 69120
    PT_O = 69760
    VT_O = 70400
    QR_O = [69120, 69632, 70144]
    H2_O = 16384
    INT_O = 32768
    H1B2_O = 49152
    CST2_O = 57344
    RW2_O = 61440
    RAB2_O = 64000
    FW_O = [57344, 59392, 61440, 63488, 65536, 67584]
    GEL_O = 69632

    psA = ent(nc.psum_tensor("psA", [128, 2048], F32))
    psB = ent(nc.psum_tensor("psB", [128, 1024], F32))
    psC = ent(nc.psum_tensor("psC", [128, 512], F32))
    psT = ent(nc.psum_tensor("psT", [128, 1024], BF16))

    with nc.Block() as blk:
        @blk.sync
        def _(e):
            for dst, src in [
                (csb[:, :], cvec[:, :]), (nwsb[:, :], normw[:, :]),
                (one1[:, :], one11[:, :]), (selb[:, :], selD[:, :]),
                (epsb[:, :], epsD[:, :]),
                (idn[:, :], identD[:, :]),
                (onec[:, :], onescolD[:, :]), (oner[:, :], onesrowD[:, :]),
                (mskSb[:, 0, :], maskS[0]), (mskSb[:, 1, :], maskS[1]),
            ]:
                inc(e.dma_start(dst, src), "ld", 16)
            inc(e.dma_start(
                xo[:, :, :],
                xT[:, HB:TH].rearrange("(a p) t -> p a t", p=128)), "ld", 16)
            V["A1"] = V["ld"]

        @blk.scalar
        def _(e):
            e.wait_ge(S["ld"], V["A1"])
            inc(e.activation(csb[:, :], csb[:, :], AF.Silu), "sc")

    wm_f = arf(WMF_O, 1536)
    with nc.Block() as blk:
        b_ld, b_sc, b_tn = V["ld"], V["sc"], V["tn"]

        @blk.sync
        def _(e):
            for kc in range(FC):
                if kc >= 1:
                    e.wait_ge(S["tn"], b_tn + kc)
                inc(e.dma_start(wm_f[:, :], wmslab[kc]), "lwm", 16)

        @blk.tensor
        def _(e):
            e.wait_ge(S["sc"], b_sc)
            for kc in range(FC):
                e.wait_ge(S["lwm"], 16 * (kc + 1))
                for cc in range(3):
                    mm = e.matmul(psA[0:1, cc * 512:(cc + 1) * 512],
                                  csb[:, kc:kc + 1],
                                  wm_f[:, cc * 512:(cc + 1) * 512],
                                  start=(kc == 0), stop=(kc == FC - 1))
                inc(mm, "tn")

    modrow = arf(MODROW_O, 1536)
    with nc.Block() as blk:
        @blk.vector
        def _(e):
            inc(e.tensor_copy(modrow[0:1, :], psA[0:1, 0:1536]), "vc")
            V["modev"] = V["vc"]

        @blk.gpsimd
        def _(e):
            e.wait_ge(S["vc"], V["modev"])
            inc(e.dma_start(mod_b[:, :], modrow[0:1, :]), "gp", 16)
            e.wait_ge(S["gp"], V["gp"])
            inc(e.collective_compute(
                "AllGather", ALU.bypass, replica_groups=[list(range(NCORE))],
                ins=[mod_b.ap().opt()], outs=[mod_g.ap().opt()]), "cc")
            V["modcc"] = V["cc"]

    with nc.Block() as blk:
        b_ld, b_tn = V["ld"], V["tn"]

        @blk.sync
        def _(e):
            e.wait_ge(S["cc"], V["modcc"])
            for s in range(NCORE):
                if s >= 1:
                    e.wait_ge(S["tn"], b_tn + s)
                inc(e.dma_start(modrow[0:1, :], mod_g[s:s + 1, :]), "lmd", 16)

        @blk.tensor
        def _(e):
            for s in range(NCORE):
                e.wait_ge(S["lmd"], 16 * (s + 1))
                for j in range(12):
                    mm = e.matmul(psB[:, s * 12 + j:s * 12 + j + 1],
                                  modrow[0:1, j * 128:(j + 1) * 128],
                                  one1[:, :], start=True, stop=True)
                inc(mm, "tn")
            V["mvT"] = V["tn"]

        @blk.vector
        def _(e):
            e.wait_ge(S["tn"], V["mvT"])
            e.tensor_copy(mv[:, :], psB[:, 0:96])
            e.drain()
            for col in (32, 80):
                g = mv[:, col:col + FC]
                t0 = gcl[:, :]
                e.tensor_scalar(t0, g, -1.0, None, ALU.mult)
                e.drain()
                e.tensor_tensor(t0, g, t0, ALU.max)
                e.drain()
                e.tensor_scalar(t0, t0, 0.25, None, ALU.max)
                e.tensor_scalar(g, g, 0.0, 2.0, ALU.is_ge, ALU.mult)
                e.drain()
                e.tensor_scalar(g, g, -1.0, None, ALU.add)
                e.drain()
                e.tensor_tensor(g, t0, g, ALU.mult)
                e.drain()
            e.tensor_scalar(mv[:, 16:32], mv[:, 16:32], 1.0, None, ALU.add)
            inc(e.tensor_scalar(mv[:, 64:80], mv[:, 64:80], 1.0, None, ALU.add),
                "vc")
            V["mvdone"] = V["vc"]

    MVO = {"sa": 0, "ca1": 16, "ga": 32, "sb": 48, "cb1": 64, "gb": 80}

    scf = arf(SCF_O, 2048)
    scb = arb(SCB_O, 2048)
    NSLICES = 3 * NCK
    with nc.Block() as blk:
        b_ld, b_vc, b_tn, b_gp = V["ld"], V["vc"], V["tn"], V["gp"]

        @blk.sync
        def _(e):
            for i in range(NSLICES):
                mat = "gud"[i // NCK]
                ncl = i % NCK
                buf = i % 2
                if i >= 2:
                    e.wait_ge(S["vc"], b_vc + 2 * (i - 1))
                cb = arb(DQ_C[buf], 2048)
                if mat == "d":
                    inc(e.dma_start(cb, codesD["d"][ncl]), "lc%d" % buf, 16)
                else:
                    inc(e.dma_start(cb.rearrange("p (a b) -> p a b", a=FC),
                                    codesD[mat][ncl]), "lc%d" % buf, 16)
                if i >= 1:
                    e.wait_ge(S["vc"], b_vc + 2 * (i - 1) + 1)
                inc(e.dma_start(scf[0:4, 0:D], scaleD[mat][ncl]), "lsf", 16)

        @blk.vector
        def _(e):
            for i in range(NSLICES):
                buf = i % 2
                cb = arb(DQ_C[buf], 2048)
                tA = arb(DQ_T[0], 2048)
                tB = arb(DQ_T[1], 2048)
                tC = arb(DQ_T[2], 2048)
                tD = arb(DQ_T[3], 2048)
                wout = arb(DQ_W[buf], 2048)
                e.wait_ge(S["lc%d" % buf], 16 * (i // 2 + 1))
                e.wait_ge(S["lsf"], 16 * (i + 1))
                if i >= 1:
                    e.wait_ge(S["tn"], b_tn + i)
                inc(e.tensor_copy(scb[0:4, 0:D], scf[0:4, 0:D]), "vc")
                e.wait_ge(S["tn"], b_tn + i + 1)
                e.tensor_scalar(tA, cb, 8.0, -8.0, ALU.is_ge, ALU.mult)
                e.tensor_tensor(tB, cb, tA, ALU.add)
                e.tensor_scalar(tC, tA, 0.125, 0.5, ALU.mult, ALU.add)
                e.tensor_scalar(tD, tB, 2.0, -4.0, ALU.mult, ALU.add)
                e.tensor_tensor(tD, tB, tD, ALU.max)
                e.tensor_scalar(tB, tB, 4.0, -16.0, ALU.mult, ALU.add)
                e.tensor_tensor(tB, tD, tB, ALU.max)
                e.tensor_tensor(tB, tB, tC, ALU.mult)
                if i >= 2:
                    e.wait_ge(S["st%d" % buf], 16 * (i // 2))
                inc(e.tensor_tensor(wout, tB, psA[:, 0:2048], ALU.mult), "vc")

        @blk.tensor
        def _(e):
            for i in range(NSLICES):
                e.wait_ge(S["vc"], b_vc + 2 * i + 1)
                for kc in range(FC):
                    mm = e.matmul(psA[:, kc * 128:(kc + 1) * 128], selb[:, :],
                                  scb[0:4, kc * 128:(kc + 1) * 128],
                                  start=True, stop=True)
                inc(mm, "tn")

        @blk.gpsimd
        def _(e):
            for i in range(NSLICES):
                ncl = i % NCK
                buf = i % 2
                wout = arb(DQ_W[buf], 2048)
                e.wait_ge(S["vc"], b_vc + 2 * i + 2)
                if i // NCK == 2:
                    dreg = deq_in[2 * NSL:3 * NSL].rearrange(
                        "(a p n c) -> a p n c", a=FC, p=128, n=NCK)
                    inc(e.dma_start(
                        dreg[:, :, ncl, :].transpose([1, 0, 2]),
                        wout.rearrange("p (a c) -> p a c", a=FC)),
                        "st%d" % buf, 16)
                else:
                    off = (i // NCK) * NSL + ncl * (FC * 128 * 128)
                    inc(e.dma_start(
                        deq_in[off:off + 2048 * 128].rearrange("(p n) -> p n", p=128),
                        wout), "st%d" % buf, 16)
            e.wait_ge(S["st0"], 16 * (NSLICES // 2))
            e.wait_ge(S["st1"], 16 * (NSLICES // 2))
            inc(e.collective_compute(
                "AllGather", ALU.bypass, replica_groups=[list(range(NCORE))],
                ins=[deq_in.ap().opt()], outs=[deq_out.ap().opt()]), "cc")
            V["deqcc"] = V["cc"]

    if "wdq" in dumps:
        with nc.Block() as blk:
            @blk.gpsimd
            def _(e):
                inc(e.dma_start(dumps["wdq"][:], deq_in[:]), "gp", 16)
                e.wait_ge(S["gp"], V["gp"])

    def emit_ln(blk_nc, xsrc_fn, nchunks, hn_view_fn, mvo_c, mvo_s, mvo_g,
                with_rms, h1_off, cst_off, rw_off, rab_off):
        pass

    cst = arb(CST_O, FC * LNC)
    h1b = arf(H1_O, FC * LNC)
    hn = arb(HN_O, FC * TH)
    rwv = {}
    for j, k in enumerate(["mean", "ms", "var", "rr", "bb"]):
        rwv[k] = arf(RW_O + 2 * j * LNC, LNC)[0:1, :]
    rAB = arb(RAB_O, 2 * LNC)[0:1, :]

    def hn_v(kc, t0):
        return hn.rearrange("p (a b) -> p a b", a=FC)[:, kc, t0:t0 + LNC]

    def xsrc1(ch, kc):
        if ch < 2:
            return arf(XST_O, FC * 512).rearrange("p (a b) -> p a b", a=FC)[
                :, kc, ch * LNC:(ch + 1) * LNC]
        return xo[:, kc, (ch - 2) * LNC:(ch - 1) * LNC]

    def cst_v(kc):
        return cst.rearrange("p (a b) -> p a b", a=FC)[:, kc, :]

    def h1_v(kc):
        return h1b.rearrange("p (a b) -> p a b", a=FC)[:, kc, :]

    with nc.Block() as blk:
        b_ld, b_sc, b_tn, b_vc = V["ld"], V["sc"], V["tn"], V["vc"]

        @blk.sync
        def _(e):
            inc(e.dma_start(
                arf(XST_O, FC * 512).rearrange("p (a b) -> p a b", a=FC),
                xT[:, 0:HB].rearrange("(a p) t -> p a t", p=128)), "lhx", 16)

        @blk.scalar
        def _(e):
            for ch in range(NLN1):
                e.wait_ge(S["lhx"], 16)
                if ch >= 1:
                    e.wait_ge(S["tn"], b_tn + 5 * (ch - 1) + 4)
                xin = (xo[:, :, (ch - 2) * LNC:(ch - 1) * LNC] if ch >= 2 else
                       arf(XST_O, FC * 512).rearrange("p (a b) -> p a b", a=FC)
                       [:, :, ch * LNC:(ch + 1) * LNC])
                inc(e.activation(cst.rearrange("p (a b) -> p a b", a=FC),
                                 xin, AF.Copy), "sc")
                e.wait_ge(S["tn"], b_tn + 5 * ch + 1)
                inc(e.activation(cst, cst, AF.Square), "sc")
                e.wait_ge(S["vc"], b_vc + 6 * ch + 1)
                inc(e.activation(rwv["var"], rwv["var"], AF.Sqrt,
                                 bias=epsb[0:1, 0:1]), "sc")
                e.wait_ge(S["vc"], b_vc + 6 * ch + 3)
                inc(e.activation(cst.rearrange("p (a b) -> p a b", a=FC),
                                 h1b.rearrange("p (a b) -> p a b", a=FC),
                                 AF.Square), "sc")
                e.wait_ge(S["vc"], b_vc + 6 * ch + 4)
                inc(e.activation(rwv["ms"], rwv["ms"], AF.Sqrt,
                                 bias=epsb[0:1, 1:2]), "sc")

        @blk.tensor
        def _(e):
            for ch in range(NLN1):
                e.wait_ge(S["sc"], b_sc + 5 * ch + 1)
                if ch >= 1:
                    e.wait_ge(S["vc"], b_vc + 6 * (ch - 1) + 4)
                for kc in range(FC):
                    mm = e.matmul(psC[0:1, 0:LNC], onec[:, :], cst_v(kc),
                                  start=(kc == 0), stop=(kc == FC - 1))
                inc(mm, "tn")
                e.wait_ge(S["sc"], b_sc + 5 * ch + 2)
                for kc in range(FC):
                    mm = e.matmul(psC[0:1, LNC:2 * LNC], onec[:, :], cst_v(kc),
                                  start=(kc == 0), stop=(kc == FC - 1))
                inc(mm, "tn")
                e.wait_ge(S["vc"], b_vc + 6 * ch + 2)
                inc(e.matmul(psB[:, 0:2 * LNC], oner[:, :], rAB,
                             start=True, stop=True), "tn")
                e.wait_ge(S["sc"], b_sc + 5 * ch + 4)
                for kc in range(FC):
                    mm = e.matmul(psC[0:1, LNC:2 * LNC], onec[:, :], cst_v(kc),
                                  start=(kc == 0), stop=(kc == FC - 1))
                inc(mm, "tn")
                e.wait_ge(S["vc"], b_vc + 6 * ch + 5)
                inc(e.matmul(psB[:, 512:512 + LNC], oner[:, :],
                             rAB[:, 0:LNC], start=True, stop=True), "tn")

        @blk.vector
        def _(e):
            for ch in range(NLN1):
                own = ch >= 2
                t0 = ch * LNC
                e.wait_ge(S["tn"], b_tn + 5 * ch + 2)
                e.tensor_scalar(rwv["mean"], psC[0:1, 0:LNC], 1.0 / D, None, ALU.mult)
                e.tensor_scalar(rwv["ms"], psC[0:1, LNC:2 * LNC], 1.0 / D, None, ALU.mult)
                e.tensor_tensor(rwv["var"], rwv["mean"], rwv["mean"], ALU.mult)
                inc(e.tensor_tensor(rwv["var"], rwv["ms"], rwv["var"],
                                    ALU.subtract), "vc")
                e.wait_ge(S["sc"], b_sc + 5 * ch + 3)
                e.reciprocal(rwv["rr"], rwv["var"])
                e.tensor_tensor(rwv["bb"], rwv["mean"], rwv["rr"], ALU.mult)
                e.tensor_scalar(rwv["bb"], rwv["bb"], -1.0, None, ALU.mult)
                e.tensor_copy(rAB[:, 0:LNC], rwv["rr"])
                inc(e.tensor_copy(rAB[:, LNC:2 * LNC], rwv["bb"]), "vc")
                e.wait_ge(S["tn"], b_tn + 5 * ch + 3)
                for kc in range(FC):
                    e.tensor_tensor(h1_v(kc), xsrc1(ch, kc), psB[:, 0:LNC], ALU.mult)
                    e.tensor_tensor(h1_v(kc), h1_v(kc), psB[:, LNC:2 * LNC], ALU.add)
                    e.tensor_scalar(h1_v(kc), h1_v(kc),
                                    mv[:, MVO["ca1"] + kc:MVO["ca1"] + kc + 1],
                                    mv[:, MVO["sa"] + kc:MVO["sa"] + kc + 1],
                                    ALU.mult, ALU.add)
                if own:
                    for kc in range(FC):
                        xv = xo[:, kc, (ch - 2) * LNC:(ch - 1) * LNC]
                        last = e.scalar_tensor_tensor(
                            xv, h1_v(kc),
                            mv[:, MVO["ga"] + kc:MVO["ga"] + kc + 1],
                            xv, ALU.mult, ALU.add)
                    inc(last, "vc")
                else:
                    inc(e.tensor_copy(rmx1[:, :], rmx1[:, :]), "vc")
                e.wait_ge(S["tn"], b_tn + 5 * ch + 4)
                inc(e.tensor_scalar(rwv["ms"], psC[0:1, LNC:2 * LNC],
                                    1.0 / D, None, ALU.mult), "vc")
                e.wait_ge(S["sc"], b_sc + 5 * ch + 5)
                e.reciprocal(rwv["rr"], rwv["ms"])
                inc(e.tensor_copy(rAB[:, 0:LNC], rwv["rr"]), "vc")
                e.wait_ge(S["tn"], b_tn + 5 * ch + 5)
                for kc in range(FC):
                    last = e.scalar_tensor_tensor(
                        hn_v(kc, t0), h1_v(kc),
                        nwsb[:, kc:kc + 1], psB[:, 512:512 + LNC],
                        ALU.mult, ALU.mult)
                inc(last, "vc")

    if "hn" in dumps or "xp" in dumps or "mv" in dumps:
        with nc.Block() as blk:
            @blk.gpsimd
            def _(e):
                if "hn" in dumps:
                    inc(e.dma_start(
                        dumps["hn"].ap().rearrange("p a b -> p (a b)"), hn),
                        "gp", 16)
                if "xp" in dumps:
                    inc(e.dma_start(dumps["xp"][:, :, :], xo[:, :, :]), "gp", 16)
                if "mv" in dumps:
                    inc(e.dma_start(dumps["mv"][:, :], mv[:, :]), "gp", 16)
                e.wait_ge(S["gp"], V["gp"])

    SHUF = []
    for i in range(32):
        SHUF.append(i ^ 1)

    rqcv = arb(RQC_O, T)
    rqsv = arb(RQS_O, T)
    rkcv = arb(RKC_O, TH)
    rksv = arb(RKS_O, TH)
    lasv = arf(LAS_O, 1024)
    lasb = arb(LASB_O, 1024)
    lbsv = {k: arb(LBS_O[k], 2048) for k in "qkvo"}
    tmplv = arb(TMPL_O, 1536)
    tmpov = arf(TMPO_O, 1024)
    qgv = arb(QG_O, GH * T).rearrange("p (a b) -> p a b", a=GH)
    kgv = arb(KG_O, GH * TH).rearrange("p (a b) -> p a b", a=GH)
    vgv = arb(VG_O, 12 * GH * 128).rearrange("p (a b) -> p a b", a=12)
    atgv = arb(ATG_O, GH * T)
    wsv = arf(WS_O, 512)
    wbv = arb(WB_O[0], 2048).rearrange("p (a b) -> p a b", a=FC)
    expPv = arb(EXP_O, 640)
    ptbv = arb(PT_O, 640)
    vtmpv = arb(VT_O, 512)
    qrawv = arb(QR_O[0], 512)
    qrotv = arb(QR_O[1], 512)
    rbbv = arb(QR_O[2], 512)
    hn_r = hn.rearrange("p (a b) -> p a b", a=FC)

    with nc.Block() as blk:
        @blk.sync
        def _(e):
            inc(e.dma_start(rqcv, ropeqc[:, :]), "ld", 16)
            inc(e.dma_start(rqsv, ropeqs[:, :]), "ld", 16)
            inc(e.dma_start(rkcv, ropekc[:, :]), "ld", 16)
            inc(e.dma_start(rksv, ropeks[:, :]), "ld", 16)
            inc(e.dma_start(
                lasv.rearrange("p (a b c) -> p a b c", a=4, b=FC),
                lA[:, :, :, :].transpose([2, 0, 1, 3])), "ld", 16)
            for k in "qkvo":
                inc(e.dma_start(lbsv[k][0:R, :], lB[k][:, :]), "ld", 16)
            V["p3ld"] = V["ld"]

        @blk.scalar
        def _(e):
            e.wait_ge(S["ld"], V["p3ld"])
            inc(e.activation(lasb, lasv, AF.Copy), "sc")

    PRJ = [("q", 0, 2, HB), ("k", 1, 3, 0), ("v", 2, 3, 0)]

    def emit_proj(g):
        sd = Sched(V)
        S_ = S
        psum_free = [None, None]
        wb_free = None
        ws_free = None
        psT_free = None
        pd = {}
        qraw_free = None
        qrot_free = None
        rbb_free = None
        vtmp_free = None

        for pr, pi, NT, toff in PRJ:
            for tch in range(NT):
                def mkA(pi=pi, tch=tch, toff=toff):
                    def f(e):
                        for kc in range(FC):
                            mm = e.matmul(
                                psC[0:R, 0:512],
                                lasb[:, (pi * FC + kc) * R:(pi * FC + kc + 1) * R],
                                hn_r[:, kc, toff + tch * 512:toff + (tch + 1) * 512],
                                start=(kc == 0), stop=(kc == FC - 1))
                        return mm
                    return f
                deps = [pd[(pr, tch - 1, "ev")]] if tch >= 1 else                     ([pd[(PRJ[pi - 1][0], PRJ[pi - 1][2] - 1, "ev")]] if pi >= 1 else [])
                tA = sd.add("tensor", deps, mkA())
                def mkAe(pr=pr, tch=tch):
                    def f(e):
                        return e.activation(
                            tmplv[0:R, tch * 512:(tch + 1) * 512],
                            psC[0:R, 0:512], AF.Copy, scale=1.0 / R)
                    return f
                eA = sd.add("scalar", [tA], mkAe())
                pd[(pr, tch, "ev")] = eA

            for oc4 in range(GH):
                h = GH * g + oc4
                casts = []
                for q4 in range(4):
                    def mkD(pr=pr, h=h, q4=q4):
                        def f(e):
                            return e.dma_start(
                                wsv.rearrange("p (a b) -> p a b", a=4),
                                wW[pr][h][q4 * 512:(q4 + 1) * 512, :]
                                .rearrange("(a p) j -> p a j", p=128))
                        return f
                    dd = [ws_free] if ws_free is not None else []
                    dq = sd.add("sync", dd, mkD(), n=16, sem="lws")
                    def mkC(q4=q4):
                        def f(e):
                            return e.activation(
                                wbv[:, 4 * q4:4 * (q4 + 1), :],
                                wsv.rearrange("p (a b) -> p a b", a=4),
                                AF.Copy)
                        return f
                    cd = [dq] + ([wb_free] if wb_free is not None else [])
                    ct = sd.add("scalar", cd, mkC())
                    ws_free = ct
                    casts.append(ct)

                for tch in range(NT):
                    slot = (oc4 * NT + tch) % 2
                    def mkM(pr=pr, h=h, tch=tch, toff=toff, slot=slot):
                        def f(e):
                            p = psA[:, slot * 512:(slot + 1) * 512]
                            for kc in range(FC):
                                e.matmul(p, wbv[:, kc, :],
                                         hn_r[:, kc, toff + tch * 512:
                                              toff + (tch + 1) * 512],
                                         start=(kc == 0), stop=False)
                            return e.matmul(
                                p, lbsv[pr][0:R, h * 128:(h + 1) * 128],
                                tmplv[0:R, tch * 512:(tch + 1) * 512],
                                start=False, stop=True)
                        return f
                    mdeps = casts + [pd[(pr, tch, "ev")]]
                    if psum_free[slot] is not None:
                        mdeps += psum_free[slot]
                    mm = sd.add("tensor", mdeps, mkM())
                    wb_free = mm

                    if pr in ("q", "k"):
                        cosv = rqcv if pr == "q" else rkcv
                        sinv = rqsv if pr == "q" else rksv
                        dst = (qgv if pr == "q" else kgv)
                        def mkES(slot=slot):
                            def f(e):
                                return e.copy(
                                    qrawv, psA[:, slot * 512:(slot + 1) * 512])
                            return f
                        esd = [mm] + ([qraw_free] if qraw_free is not None else [])
                        evS = sd.add("scalar", esd, mkES())
                        def mkSh():
                            def f(e):
                                return e.stream_shuffle(qrotv, qrawv, SHUF)
                            return f
                        sh_d = [evS] + ([qrot_free] if qrot_free is not None else [])
                        evV1 = sd.add("vector", sh_d, mkSh())
                        qraw_free = evV1
                        def mkA2(slot=slot, oc4=oc4, tch=tch, cosv=cosv, dst=dst):
                            def f(e):
                                return e.tensor_tensor(
                                    dst[:, oc4, tch * 512:(tch + 1) * 512],
                                    psA[:, slot * 512:(slot + 1) * 512],
                                    cosv[:, tch * 512:(tch + 1) * 512], ALU.mult)
                            return f
                        evV2 = sd.add("vector", [mm], mkA2())
                        def mkB(sinv=sinv, tch=tch):
                            def f(e):
                                return e.tensor_tensor(
                                    rbbv, qrotv,
                                    sinv[:, tch * 512:(tch + 1) * 512], ALU.mult)
                            return f
                        gb_d = [evV1] + ([rbb_free] if rbb_free is not None else [])
                        evG = sd.add("gpsimd", gb_d, mkB())
                        qrot_free = evG
                        def mkAdd(oc4=oc4, tch=tch, dst=dst):
                            def f(e):
                                return e.tensor_tensor(
                                    dst[:, oc4, tch * 512:(tch + 1) * 512],
                                    dst[:, oc4, tch * 512:(tch + 1) * 512],
                                    rbbv, ALU.add)
                            return f
                        evV3 = sd.add("vector", [evG], mkAdd())
                        rbb_free = evV3
                        psum_free[slot] = [evS, evV2]
                    else:
                        def mkEV(slot=slot):
                            def f(e):
                                return e.copy(
                                    vtmpv, psA[:, slot * 512:(slot + 1) * 512])
                            return f
                        evd = [mm] + ([vtmp_free] if vtmp_free is not None else [])
                        evS = sd.add("scalar", evd, mkEV())
                        def mkT():
                            def f(e):
                                for j in range(4):
                                    t = e.transpose(psT[:, j * 128:(j + 1) * 128],
                                                    vtmpv[:, j * 128:(j + 1) * 128],
                                                    idn[:, :])
                                return t
                            return f
                        td = [evS] + ([psT_free] if psT_free is not None else [])
                        tT = sd.add("tensor", td, mkT())
                        vtmp_free = tT
                        def mkVE(oc4=oc4, tch=tch):
                            def f(e):
                                for j in range(4):
                                    t = e.tensor_copy(
                                        vgv[:, tch * 4 + j,
                                            oc4 * 128:(oc4 + 1) * 128],
                                        psT[:, j * 128:(j + 1) * 128])
                                return t
                            return f
                        evV = sd.add("vector", [tT], mkVE())
                        psT_free = evV
                        psum_free[slot] = [evS]
        sd.emit(nc, S_, V, inc)

    def emit_attn(g):
        sd = Sched(V)
        exp_prev = []
        av_prev = None
        pt_slot_free = [None, None]
        expP_free = None
        mask_step = None
        for qt in range(8):
            if qt < 4:
                def mkMD(qt=qt):
                    def f(e):
                        return e.dma_start(mskEb[:, :, :],
                                           maskE[qt].transpose([1, 0, 2]))
                    return f
                mask_step = sd.add("sync", exp_prev + ([mask_step] if mask_step else []), mkMD(), n=16, sem="lmk")
            for oc4 in range(GH):
                def mkS(g=g, oc4=oc4, qt=qt):
                    def f(e):
                        qtile = qgv[:, oc4, qt * 128:(qt + 1) * 128]
                        e.matmul(psA[:, 1024:1536], qtile,
                                 kgv[:, oc4, qt * 128:qt * 128 + 512],
                                 start=True, stop=True)
                        return e.matmul(psA[:, 1536:1664], qtile,
                                        kgv[:, oc4, qt * 128 + 512:qt * 128 + 640],
                                        start=True, stop=True)
                    return f
                sdeps = list(exp_prev)
                if av_prev is not None:
                    sdeps.append(av_prev)
                s12 = sd.add("tensor", sdeps, mkS())
                def mkMk(qt=qt):
                    def f(e):
                        if qt < 4:
                            for s in range(4):
                                t = e.tensor_tensor(
                                    psA[:, 1024 + s * 128:1024 + (s + 1) * 128],
                                    psA[:, 1024 + s * 128:1024 + (s + 1) * 128],
                                    mskEb[:, s, :], ALU.add)
                            t = e.tensor_tensor(psA[:, 1536:1664],
                                                psA[:, 1536:1664],
                                                mskEb[:, 4, :], ALU.add)
                        else:
                            e.tensor_tensor(psA[:, 1024:1152], psA[:, 1024:1152],
                                            mskSb[:, 0, :], ALU.add)
                            t = e.tensor_tensor(psA[:, 1536:1664], psA[:, 1536:1664],
                                                mskSb[:, 1, :], ALU.add)
                        return t
                    return f
                mdeps = [s12] + ([mask_step] if qt < 4 else [])
                mk = sd.add("vector", mdeps, mkMk())
                def mkRx():
                    def f(e):
                        e.reduce_max(rmx1[:, :], psA[:, 1024:1536], AX.X)
                        e.reduce_max(rmx2[:, :], psA[:, 1536:1664], AX.X)
                        e.drain()
                        e.tensor_tensor(rmx1[:, :], rmx1[:, :], rmx2[:, :], ALU.max)
                        e.drain()
                        return e.tensor_scalar(rmx1[:, :], rmx1[:, :], -1.0,
                                               None, ALU.mult)
                    return f
                rx = sd.add("vector", [mk], mkRx())
                def mkEx():
                    def f(e):
                        e.activation(expPv[0:512].rearrange("p n -> p n") if False
                                     else expPv[:, 0:512] if False else expPv[:, :][:, 0:512],
                                     psA[:, 1024:1536], AF.Exp,
                                     bias=rmx1[:, :], accum_out=racc1[:, :])
                        return e.activation(expPv[:, :][:, 512:640],
                                            psA[:, 1536:1664], AF.Exp,
                                            bias=rmx1[:, :], accum_out=racc2[:, :])
                    return f
                edeps = [rx] + ([expP_free] if expP_free is not None else [])
                ex = sd.add("scalar", edeps, mkEx())
                exp_prev = [ex]
                def mkRs():
                    def f(e):
                        e.tensor_tensor(rsum[:, :], racc1[:, :], racc2[:, :], ALU.add)
                        e.drain()
                        e.reciprocal(rsum[:, :], rsum[:, :])
                        e.drain()
                        return e.tensor_scalar(diag[:, :], idn[:, :],
                                               rsum[:, :], None, ALU.mult)
                    return f
                rs = sd.add("vector", [ex], mkRs())
                evpt = []
                for s5 in range(5):
                    slot = s5 % 2
                    off = 0 if slot == 0 else 512
                    def mkPT(s5=s5, off=off):
                        def f(e):
                            return e.matmul(psB[:, off:off + 128],
                                            expPv[:, :][:, s5 * 128:(s5 + 1) * 128],
                                            diag[:, :], start=True, stop=True)
                        return f
                    pdeps = [rs]
                    if pt_slot_free[slot] is not None:
                        pdeps.append(pt_slot_free[slot])
                    pt = sd.add("tensor", pdeps, mkPT())
                    def mkPE(s5=s5, off=off):
                        def f(e):
                            return e.tensor_copy(
                                ptbv[:, s5 * 128:(s5 + 1) * 128],
                                psB[:, off:off + 128])
                        return f
                    pe = sd.add("vector", [pt], mkPE())
                    pt_slot_free[slot] = pe
                    evpt.append(pe)
                expP_free = evpt[-1]
                def mkPV(g=g, oc4=oc4, qt=qt):
                    def f(e):
                        for s5 in range(5):
                            t = e.matmul(psA[:, 1664:1792],
                                         vgv[:, qt + s5, oc4 * 128:(oc4 + 1) * 128],
                                         ptbv[:, s5 * 128:(s5 + 1) * 128],
                                         start=(s5 == 0), stop=(s5 == 4))
                        return t
                    return f
                pvdeps = list(evpt)
                if av_prev is not None:
                    pvdeps.append(av_prev)
                pv = sd.add("tensor", pvdeps, mkPV())
                def mkAV(oc4=oc4, qt=qt):
                    def f(e):
                        return e.tensor_copy(
                            atgv[:, oc4 * T + qt * 128:oc4 * T + (qt + 1) * 128],
                            psA[:, 1664:1792])
                    return f
                av = sd.add("vector", [pv], mkAV())
                av_prev = av

        ws_free = None
        wb_free = None
        psum_free = [None, None]
        for oc in range(FC):
            def mkD(oc=oc, g=g):
                def f(e):
                    return e.dma_start(
                        wsv.rearrange("p (a b) -> p a b", a=4),
                        wW["o"][oc][g * 512:(g + 1) * 512, :]
                        .rearrange("(a p) j -> p a j", p=128))
                return f
            dd = [ws_free] if ws_free is not None else []
            dq = sd.add("sync", dd, mkD(), n=16, sem="lws")
            def mkC():
                def f(e):
                    return e.activation(
                        wbv[:, 0:4, :],
                        wsv.rearrange("p (a b) -> p a b", a=4), AF.Copy)
                return f
            cd = [dq, av_prev] + ([wb_free] if wb_free is not None else [])
            ct = sd.add("scalar", cd, mkC())
            ws_free = ct
            for tch in range(2):
                slot = tch
                def mkOM(oc=oc, tch=tch, slot=slot):
                    def f(e):
                        p = psA[:, slot * 512:(slot + 1) * 512]
                        for hl in range(GH):
                            t = e.matmul(p, wbv[:, hl, :],
                                         atgv[:, hl * T + tch * 512:
                                              hl * T + (tch + 1) * 512],
                                         start=(hl == 0), stop=(hl == GH - 1))
                        return t
                    return f
                odeps = [ct]
                if psum_free[slot] is not None:
                    odeps.append(psum_free[slot])
                om = sd.add("tensor", odeps, mkOM())
                wb_free = om
                def mkOX(oc=oc, tch=tch, slot=slot):
                    def f(e):
                        xv = xo[:, oc, tch * 512:(tch + 1) * 512]
                        return e.scalar_tensor_tensor(
                            xv, psA[:, slot * 512:(slot + 1) * 512],
                            mv[:, MVO["ga"] + oc:MVO["ga"] + oc + 1],
                            xv, ALU.mult, ALU.add)
                    return f
                ox = sd.add("vector", [om], mkOX())
                psum_free[slot] = ox
        oe_prev = None
        for tch in range(2):
            def mkOA(g=g, tch=tch):
                def f(e):
                    for hl in range(GH):
                        h = GH * g + hl
                        t = e.matmul(
                            psC[0:R, 0:512],
                            lasb[:, (3 * FC + h) * R:(3 * FC + h + 1) * R],
                            atgv[:, hl * T + tch * 512:hl * T + (tch + 1) * 512],
                            start=(hl == 0), stop=(hl == GH - 1))
                    return t
                return f
            adeps = [av_prev] + ([oe_prev] if oe_prev is not None else [])
            oa = sd.add("tensor", adeps, mkOA())
            def mkOE(g=g, tch=tch):
                def f(e):
                    dst = tmpov[0:R, tch * 512:(tch + 1) * 512]
                    if g == 0:
                        return e.tensor_copy(dst, psC[0:R, 0:512])
                    return e.tensor_tensor(dst, dst, psC[0:R, 0:512], ALU.add)
                return f
            oe = sd.add("vector", [oa], mkOE())
            oe_prev = oe
        sd.emit(nc, S_, V, inc)

    S_ = S
    for g in range(NG):
        emit_proj(g)
        if g == 0 and "arena" in dumps:
            with nc.Block() as blk:
                @blk.gpsimd
                def _(e):
                    inc(e.dma_start(dumps["arena"][:, :], arena[:, :]), "gp", 16)
                    e.wait_ge(S["gp"], V["gp"])
        if g == 0 and "q" in dumps:
            with nc.Block() as blk:
                @blk.gpsimd
                def _(e):
                    inc(e.dma_start(dumps["q"][:, :, :],
                                    arb(QG_O, GH * T).rearrange("p (a b) -> p a b", a=GH)), "gp", 16)
                    inc(e.dma_start(dumps["k"][:, :, :],
                                    arb(KG_O, GH * TH).rearrange("p (a b) -> p a b", a=GH)), "gp", 16)
                    inc(e.dma_start(dumps["v"][:, :, :],
                                    arb(VG_O, 12 * GH * 128).rearrange("p (a b) -> p a b", a=12)), "gp", 16)
                    e.wait_ge(S["gp"], V["gp"])
        emit_attn(g)
        if g == 0 and "attn" in dumps:
            with nc.Block() as blk:
                @blk.gpsimd
                def _(e):
                    inc(e.dma_start(dumps["attn"][:, :, :],
                                    atgv.rearrange("p (a b) -> p a b", a=GH)), "gp", 16)
                    e.wait_ge(S["gp"], V["gp"])

    with nc.Block() as blk:
        @blk.scalar
        def _(e):
            inc(e.activation(tmplv[0:R, 0:T], tmpov[0:R, 0:T], AF.Copy,
                             scale=1.0 / R), "sc")
            V["obc"] = V["sc"]

    sd = Sched(V)
    psum_free = [None, None]
    dummy = sd.add("scalar", [], lambda e: e.activation(
        vtmpv[0:1, 0:2], vtmpv[0:1, 0:2], AF.Copy))
    for oc in range(FC):
        for tch in range(2):
            slot = tch
            def mkBM(oc=oc, tch=tch, slot=slot):
                def f(e):
                    return e.matmul(psA[:, slot * 512:(slot + 1) * 512],
                                    lbsv["o"][0:R, oc * 128:(oc + 1) * 128],
                                    tmplv[0:R, tch * 512:(tch + 1) * 512],
                                    start=True, stop=True)
                return f
            bdeps = [dummy]
            if psum_free[slot] is not None:
                bdeps.append(psum_free[slot])
            bm = sd.add("tensor", bdeps, mkBM())
            def mkBX(oc=oc, tch=tch, slot=slot):
                def f(e):
                    xv = xo[:, oc, tch * 512:(tch + 1) * 512]
                    return e.scalar_tensor_tensor(
                        xv, psA[:, slot * 512:(slot + 1) * 512],
                        mv[:, MVO["ga"] + oc:MVO["ga"] + oc + 1],
                        xv, ALU.mult, ALU.add)
                return f
            bx = sd.add("vector", [bm], mkBX())
            psum_free[slot] = bx
    sd.emit(nc, S_, V, inc)

    H2_O2 = 0
    INT_O2 = 16384
    H1B2 = 16384
    CST2 = 24576
    RW2 = 28672
    RAB2 = 31232
    WDS_O = 32768
    GUB_O = [65536, 67584]
    GELB_O = 69632

    h2v = arb(H2_O2, FC * T)
    h2_r = h2v.rearrange("p (a b) -> p a b", a=FC)
    cst2 = arb(CST2, FC * LNC)
    h1b2 = arf(H1B2, FC * LNC)
    rwv2 = {}
    for j, k in enumerate(["mean", "ms", "var", "rr", "bb"]):
        rwv2[k] = arf(RW2 + 2 * j * LNC, LNC)[0:1, :]
    rAB2 = arb(RAB2, 2 * LNC)[0:1, :]

    def cst2_v(kc):
        return cst2.rearrange("p (a b) -> p a b", a=FC)[:, kc, :]

    def h1b2_v(kc):
        return h1b2.rearrange("p (a b) -> p a b", a=FC)[:, kc, :]

    with nc.Block() as blk:
        b_ld, b_sc, b_tn, b_vc = V["ld"], V["sc"], V["tn"], V["vc"]

        @blk.scalar
        def _(e):
            for ch in range(NLN2):
                if ch >= 1:
                    e.wait_ge(S["tn"], b_tn + 3 * (ch - 1) + 2)
                inc(e.activation(cst2.rearrange("p (a b) -> p a b", a=FC),
                                 xo[:, :, ch * LNC:(ch + 1) * LNC],
                                 AF.Copy), "sc")
                e.wait_ge(S["tn"], b_tn + 3 * ch + 1)
                inc(e.activation(cst2, cst2, AF.Square), "sc")
                e.wait_ge(S["vc"], b_vc + 3 * ch + 1)
                inc(e.activation(rwv2["var"], rwv2["var"], AF.Sqrt,
                                 bias=epsb[0:1, 0:1]), "sc")

        @blk.tensor
        def _(e):
            for ch in range(NLN2):
                e.wait_ge(S["sc"], b_sc + 3 * ch + 1)
                if ch >= 1:
                    e.wait_ge(S["vc"], b_vc + 3 * (ch - 1) + 1)
                for kc in range(FC):
                    mm = e.matmul(psC[0:1, 0:LNC], onec[:, :], cst2_v(kc),
                                  start=(kc == 0), stop=(kc == FC - 1))
                inc(mm, "tn")
                e.wait_ge(S["sc"], b_sc + 3 * ch + 2)
                for kc in range(FC):
                    mm = e.matmul(psC[0:1, LNC:2 * LNC], onec[:, :], cst2_v(kc),
                                  start=(kc == 0), stop=(kc == FC - 1))
                inc(mm, "tn")
                e.wait_ge(S["vc"], b_vc + 3 * ch + 2)
                inc(e.matmul(psB[:, 0:2 * LNC], oner[:, :], rAB2,
                             start=True, stop=True), "tn")

        @blk.vector
        def _(e):
            for ch in range(NLN2):
                e.wait_ge(S["tn"], b_tn + 3 * ch + 2)
                e.tensor_scalar(rwv2["mean"], psC[0:1, 0:LNC], 1.0 / D, None, ALU.mult)
                e.tensor_scalar(rwv2["ms"], psC[0:1, LNC:2 * LNC], 1.0 / D, None, ALU.mult)
                e.tensor_tensor(rwv2["var"], rwv2["mean"], rwv2["mean"], ALU.mult)
                inc(e.tensor_tensor(rwv2["var"], rwv2["ms"], rwv2["var"],
                                    ALU.subtract), "vc")
                e.wait_ge(S["sc"], b_sc + 3 * ch + 3)
                e.reciprocal(rwv2["rr"], rwv2["var"])
                e.tensor_tensor(rwv2["bb"], rwv2["mean"], rwv2["rr"], ALU.mult)
                e.tensor_scalar(rwv2["bb"], rwv2["bb"], -1.0, None, ALU.mult)
                e.tensor_copy(rAB2[:, 0:LNC], rwv2["rr"])
                inc(e.tensor_copy(rAB2[:, LNC:2 * LNC], rwv2["bb"]), "vc")
                e.wait_ge(S["tn"], b_tn + 3 * ch + 3)
                for kc in range(FC):
                    e.tensor_tensor(h1b2_v(kc), xo[:, kc, ch * LNC:(ch + 1) * LNC],
                                    psB[:, 0:LNC], ALU.mult)
                    e.tensor_tensor(h1b2_v(kc), h1b2_v(kc), psB[:, LNC:2 * LNC],
                                    ALU.add)
                    last = e.tensor_scalar(
                        h2_r[:, kc, ch * LNC:(ch + 1) * LNC], h1b2_v(kc),
                        mv[:, MVO["cb1"] + kc:MVO["cb1"] + kc + 1],
                        mv[:, MVO["sb"] + kc:MVO["sb"] + kc + 1],
                        ALU.mult, ALU.add)
                inc(last, "vc")

    if "h2" in dumps:
        with nc.Block() as blk:
            @blk.gpsimd
            def _(e):
                inc(e.dma_start(dumps["h2"][:, :, :], h2_r), "gp", 16)
                e.wait_ge(S["gp"], V["gp"])
    INT2_O = 16384
    DWN_O = 49152
    GUB_O2 = [55296 + j * 2048 for j in range(6)]
    GELB2_O = [67584, 68608]

    inter2 = arb(INT2_O, 2 * FC * 2 * 512).rearrange(
        "p (q a h b) -> p q a h b", q=2, a=FC, h=2)
    psTf = psT[:, :].bitcast(F32)

    sd = Sched(V)
    NSUP = 8
    first_sync = [True]
    first_gp = [True]
    gub_free = [None] * 6
    gel_hist = {}
    imul_hist = {}
    dstt_hist = {}
    dmm_hist = {}
    ddma_hist = {}

    def emit_gup(s, nl):
        step = s * FC + nl
        gs = (2 * step) % 6
        us = (2 * step + 1) % 6
        goff = nl * (FC * 128 * 128)

        def mkGD(s=s, goff=goff, gs=gs, first=first_sync[0]):
            def f(e):
                if first:
                    e.wait_ge(S["cc"], V["deqcc"])
                return e.dma_start(
                    arb(GUB_O2[gs], 2048),
                    deq_out[s, goff:goff + 2048 * 128]
                    .rearrange("(p n) -> p n", p=128))
            return f
        gdd = [gub_free[gs]] if gub_free[gs] is not None else []
        gdma = sd.add("sync", gdd, mkGD(), n=16, sem="lb%d" % gs)
        first_sync[0] = False

        def mkUD(s=s, goff=goff, us=us):
            def f(e):
                return e.dma_start(
                    arb(GUB_O2[us], 2048),
                    deq_out[s, NSL + goff:NSL + goff + 2048 * 128]
                    .rearrange("(p n) -> p n", p=128))
            return f
        udd = [gub_free[us]] if gub_free[us] is not None else []
        udma = sd.add("sync", udd, mkUD(), n=16, sem="lb%d" % us)

        A = (nl % 2) * 1024

        def mkGM(gs=gs, A=A):
            def f(e):
                gb = arb(GUB_O2[gs], 2048).rearrange("p (a b) -> p a b", a=FC)
                for hf in range(2):
                    for kc in range(FC):
                        mm = e.matmul(
                            psA[:, A + hf * 512:A + hf * 512 + 512],
                            gb[:, kc, :],
                            h2_r[:, kc, hf * 512:hf * 512 + 512],
                            start=(kc == 0), stop=(kc == FC - 1))
                return mm
            return f
        gdeps = [gdma]
        if (s, nl - 2) in gel_hist:
            gdeps.append(gel_hist[(s, nl - 2)])
        elif nl < 2 and s > 0:
            gdeps.append(gel_hist[(s - 1, FC - 2 + nl)])
        gmm = sd.add("tensor", gdeps, mkGM())
        gub_free[gs] = gmm

        def mkUM(us=us):
            def f(e):
                ub = arb(GUB_O2[us], 2048).rearrange("p (a b) -> p a b", a=FC)
                for hf in range(2):
                    for kc in range(FC):
                        mm = e.matmul(
                            psB[:, hf * 512:hf * 512 + 512],
                            ub[:, kc, :],
                            h2_r[:, kc, hf * 512:hf * 512 + 512],
                            start=(kc == 0), stop=(kc == FC - 1))
                return mm
            return f
        udeps = [udma]
        if (s, nl - 1) in imul_hist:
            udeps.append(imul_hist[(s, nl - 1)])
        elif nl == 0 and s > 0:
            udeps.append(imul_hist[(s - 1, FC - 1)])
        umm = sd.add("tensor", udeps, mkUM())
        gub_free[us] = umm

        def mkGel(nl=nl, A=A):
            def f(e):
                for hf in range(2):
                    t = e.activation(
                        arb(GELB2_O[nl % 2] + hf * 512, 512),
                        psA[:, A + hf * 512:A + hf * 512 + 512],
                        AF.Gelu_apprx_tanh)
                return t
            return f
        geld = [gmm]
        if (s, nl - 2) in imul_hist:
            geld.append(imul_hist[(s, nl - 2)])
        elif nl < 2 and s > 0:
            geld.append(imul_hist[(s - 1, FC - 2 + nl)])
        gel = sd.add("scalar", geld, mkGel())
        gel_hist[(s, nl)] = gel

        def mkIM(s=s, nl=nl):
            def f(e):
                for hf in range(2):
                    t = e.tensor_tensor(
                        inter2[:, s % 2, nl, hf, :],
                        arb(GELB2_O[nl % 2] + hf * 512, 512),
                        psB[:, hf * 512:hf * 512 + 512],
                        ALU.mult)
                return t
            return f
        imdeps = [gel, umm]
        if s >= 2 and (s - 2, FC - 1) in dmm_hist:
            imdeps.append(dmm_hist[(s - 2, FC - 1)])
        im = sd.add("vector", imdeps, mkIM())
        imul_hist[(s, nl)] = im

    def emit_ddma(s, hc):
        def mkDD(s=s, hc=hc, first=first_gp[0]):
            def f(e):
                if first:
                    e.wait_ge(S["cc"], V["deqcc"])
                return e.dma_start(
                    arb(DWN_O + ((s * FC + hc) % 3) * 2048, 2048),
                    deq_out[s, 2 * NSL + hc * (NCK * 128 * 128):
                            2 * NSL + (hc + 1) * (NCK * 128 * 128)]
                    .rearrange("(p n) -> p n", p=128))
            return f
        fl = s * FC + hc
        ddd = [dmm_hist[divmod(fl - 3, FC)]] if fl >= 3 else []
        ddma = sd.add("gpsimd", ddd, mkDD(), n=16, sem="dd%d" % (fl % 3))
        first_gp[0] = False
        ddma_hist[(s, hc)] = ddma

    def emit_dmm(s, hc):
        def mkDM(s=s, hc=hc):
            def f(e):
                dw = arb(DWN_O + ((s * FC + hc) % 3) * 2048, 2048)
                for hf in range(2):
                    dst = psC[:, 0:512] if hf == 0 else psTf[:, 0:512]
                    for ns in range(FC):
                        mm = e.matmul(
                            dst, dw[:, ns * 128:(ns + 1) * 128],
                            inter2[:, s % 2, ns, hf, :],
                            start=(ns == 0), stop=(ns == FC - 1))
                return mm
            return f
        dmdeps = [ddma_hist[(s, hc)], imul_hist[(s, FC - 1)]]
        if (s, hc - 1) in dstt_hist:
            dmdeps.append(dstt_hist[(s, hc - 1)])
        elif hc == 0 and s > 0:
            dmdeps.append(dstt_hist[(s - 1, FC - 1)])
        dmm = sd.add("tensor", dmdeps, mkDM())
        dmm_hist[(s, hc)] = dmm

        def mkDX(hc=hc):
            def f(e):
                for hf in range(2):
                    src = psC[:, 0:512] if hf == 0 else psTf[:, 0:512]
                    xv = xo[:, hc, hf * 512:(hf + 1) * 512]
                    t = e.scalar_tensor_tensor(
                        xv, src,
                        mv[:, MVO["gb"] + hc:MVO["gb"] + hc + 1],
                        xv, ALU.mult, ALU.add)
                return t
            return f
        dx = sd.add("vector", [dmm], mkDX())
        dstt_hist[(s, hc)] = dx

    TILES = NSUP * FC
    for t in range(TILES + FC):
        s, nl = divmod(t, FC)
        if s < NSUP:
            emit_gup(s, nl)
        fl = t - 14
        if 0 <= fl < TILES:
            emit_ddma(*divmod(fl, FC))
        fp = t - 16
        if fp >= 0:
            emit_dmm(*divmod(fp, FC))
    if "inter" in dumps:
        def mkID():
            def f(e):
                return e.dma_start(
                    dumps["inter"][:, :, :],
                    inter2[:, (NSUP - 1) % 2, :, :, :]
                    .rearrange("p a h b -> p (a h b)")
                    .rearrange("p (a b) -> p a b", a=FC))
            return f
        sd.add("gpsimd", [imul_hist[(NSUP - 1, FC - 1)]], mkID(), n=16)
    sd.emit(nc, S_, V, inc)

    with nc.Block() as blk:
        @blk.gpsimd
        def _(e):
            inc(e.dma_start(xout.rearrange("(a p) t -> p a t", p=128),
                            xo[:, :, :]), "gp", 16)
            e.wait_ge(S["gp"], V["gp"])

    stk.close()
    return nc, dumps, V


def _run(inputs, dump=()):
    in_maps = prepare_inputs(inputs)
    nc, dumps, V = build_nc(dump=dump)
    res = run_bass_kernel_spmd(nc, in_maps, core_ids=list(range(NCORE)))
    return res


def kernel(**inputs):
    res = _run(inputs, dump=DUMP)
    outs = [r["xout"] for r in res.results]
    full = np.concatenate([o.T for o in outs], axis=0)
    return np.ascontiguousarray(full.reshape(1, 8192, D).astype(np.float32))


# version 10
# speedup vs baseline: 1.1058x; 1.0504x over previous
import contextlib
import numpy as np
import ml_dtypes

import concourse.bass as bass
import concourse.mybir as mybir
from concourse.bass_utils import run_bass_kernel_spmd

F32 = mybir.dt.float32
BF16 = mybir.dt.bfloat16
AF = mybir.ActivationFunctionType
ALU = mybir.AluOpType
AX = mybir.AxisListType

NCORE = 8
D = 2048
FC = 16
T = 1024
HB = 512
TH = T + HB
H = 16
DH = 128
GH = 4
NG = H // GH
R = 16
MLP = 16384
NCK = 16
LNC = 256
NLN1 = TH // LNC
NLN2 = T // LNC

BF = ml_dtypes.bfloat16
DUMP = ()
FFN_NO_DOWN = False
FFN_NO_GU = False
FFN_DOWN_SCRATCH = False
FFN_DOWN_SER = False



def _rope_tables():
    half = DH // 2
    theta = 1.0 / 10000.0 ** (np.arange(half, dtype=np.float64) / half)
    pos = np.arange(-HB, 8192, dtype=np.float64)
    ang = pos[:, None] * theta[None, :]
    cos = np.cos(ang).astype(np.float32)
    sin = np.sin(ang).astype(np.float32)
    cos_p = np.empty((128, HB + 8192), np.float32)
    sin_p = np.empty((128, HB + 8192), np.float32)
    cos_p[0::2] = cos.T
    cos_p[1::2] = cos.T
    sin_p[0::2] = -sin.T
    sin_p[1::2] = sin.T
    return cos_p, sin_p


def _perm_idx():
    idx = np.empty(D, np.int64)
    for h in range(H):
        for j in range(64):
            idx[h * 128 + 2 * j] = h * 128 + j
            idx[h * 128 + 2 * j + 1] = h * 128 + 64 + j
    return idx


def _tile_wT(wT):
    return np.ascontiguousarray(wT.reshape(D, FC, 128).transpose(1, 0, 2))


def _masks_for_core(core):
    NEG = np.float32(-1e9)
    qi = np.arange(128)[:, None]
    kj = np.arange(128)[None, :]
    std = np.zeros((2, 128, 128), np.float32)
    std[0][(kj + 0) < (qi + 1)] = NEG
    std[1][(kj + 512) > (qi + 512)] = NEG
    early = np.zeros((4, 5, 128, 128), np.float32)
    for qt in range(4):
        gq0 = core * T + qt * 128
        for s in range(5):
            m = np.zeros((128, 128), np.float32)
            kglob = gq0 - 512 + s * 128 + kj
            qglob = gq0 + qi
            bad = (kglob > qglob) | (kglob < qglob - 511) | (kglob < 0)
            m[bad] = NEG
            early[qt, s] = m
    return early, std


def _unpack_nvfp4(Wp):
    b = np.asarray(Wp).astype(np.uint8)
    lo = b & 15
    hi = (b >> 4) & 15
    return np.stack([lo, hi], axis=-1).reshape(b.shape[0], -1)


def prepare_inputs(d):
    x = np.asarray(d["L0_attn_x"], np.float32)[0]
    perm = _perm_idx()

    wqT = np.ascontiguousarray(np.asarray(d["L0_attn_wq"], np.float32).T)[:, perm]
    wkT = np.ascontiguousarray(np.asarray(d["L0_attn_wk"], np.float32).T)[:, perm]
    wvT = np.ascontiguousarray(np.asarray(d["L0_attn_wv"], np.float32).T)
    woT = np.ascontiguousarray(np.asarray(d["L0_attn_wo"], np.float32).T)
    wq_t, wk_t, wv_t, wo_t = map(_tile_wT, (wqT, wkT, wvT, woT))

    lAm = np.stack([
        np.asarray(d["L0_attn_qA"], np.float32).T.reshape(FC, 128, R),
        np.asarray(d["L0_attn_kA"], np.float32).T.reshape(FC, 128, R),
        np.asarray(d["L0_attn_vA"], np.float32).T.reshape(FC, 128, R),
        np.asarray(d["L0_attn_oA"], np.float32).T.reshape(FC, 128, R),
    ], axis=0)
    lqB = np.ascontiguousarray(np.asarray(d["L0_attn_qB"], np.float32).T[:, perm]).astype(BF)
    lkB = np.ascontiguousarray(np.asarray(d["L0_attn_kB"], np.float32).T[:, perm]).astype(BF)
    lvB = np.ascontiguousarray(np.asarray(d["L0_attn_vB"], np.float32).T).astype(BF)
    loB = np.ascontiguousarray(np.asarray(d["L0_attn_oB"], np.float32).T).astype(BF)

    normw = np.ascontiguousarray(np.asarray(d["L0_attn_norm_w"], np.float32).reshape(FC, 128).T)
    cvec = np.ascontiguousarray(np.asarray(d["L0_c"], np.float32).reshape(FC, 128).T)
    wmod = np.asarray(d["L0_wmod"], np.float32)

    cos_p, sin_p = _rope_tables()
    SC = np.float32(1.0 / np.sqrt(DH))

    gcT = _unpack_nvfp4(d["L0_ffn_gate_W_packed"]).T
    ucT = _unpack_nvfp4(d["L0_ffn_up_W_packed"]).T
    dcT = _unpack_nvfp4(d["L0_ffn_down_W_packed"]).T
    gsT = np.ascontiguousarray(np.asarray(d["L0_ffn_gate_scales"], np.float32).T)
    usT = np.ascontiguousarray(np.asarray(d["L0_ffn_up_scales"], np.float32).T)
    dsT = np.ascontiguousarray(np.asarray(d["L0_ffn_down_scales"], np.float32).T)

    selm = np.zeros((4, 128), np.float32)
    for p in range(128):
        selm[p >> 5, p] = 1.0

    shared = {
        "wq": wq_t, "wk": wk_t, "wv": wv_t, "wo": wo_t,
        "lA": np.ascontiguousarray(lAm),
        "lqB": lqB, "lkB": lkB, "lvB": lvB, "loB": loB,
        "normw": normw, "cvec": cvec,
        "ident": np.eye(128, dtype=np.float32).astype(BF),
        "onescol": np.ones((128, 1), np.float32).astype(BF),
        "onesrow": np.ones((1, 128), np.float32).astype(BF),
        "one11": np.ones((1, 1), np.float32),
        "epsv": np.array([[1e-5, 1e-6]], np.float32),
        "sel": selm.astype(BF),
    }

    in_maps = []
    for c in range(NCORE):
        s0 = c * T
        xs = np.zeros((TH, D), np.float32)
        lo = max(0, s0 - HB)
        if s0 > lo:
            xs[HB - (s0 - lo):HB] = x[lo:s0]
        xs[HB:] = x[s0:s0 + T]
        xTc = np.ascontiguousarray(xs.T)

        early, std = _masks_for_core(c)
        roff = s0
        im = dict(shared)
        im.update({
            "xT": xTc,
            "wmslab": np.ascontiguousarray(
                wmod[:, c * 1536:(c + 1) * 1536].reshape(FC, 128, 1536)),
            "ropeqc": (np.ascontiguousarray(cos_p[:, roff + HB:roff + HB + T]) * SC).astype(BF),
            "ropeqs": (np.ascontiguousarray(sin_p[:, roff + HB:roff + HB + T]) * SC).astype(BF),
            "ropekc": np.ascontiguousarray(cos_p[:, roff:roff + TH]).astype(BF),
            "ropeks": np.ascontiguousarray(sin_p[:, roff:roff + TH]).astype(BF),
            "maskE": early, "maskS": std,
        })
        n0 = c * NCK * 128
        im["gcodes"] = np.ascontiguousarray(
            gcT[:, n0:n0 + NCK * 128].reshape(FC, 128, NCK, 128)
            .transpose(2, 1, 0, 3)).astype(BF)
        im["ucodes"] = np.ascontiguousarray(
            ucT[:, n0:n0 + NCK * 128].reshape(FC, 128, NCK, 128)
            .transpose(2, 1, 0, 3)).astype(BF)
        im["dcodes"] = np.ascontiguousarray(
            dcT[n0:n0 + NCK * 128].reshape(NCK, 128, D)).astype(BF)
        im["gscale"] = np.ascontiguousarray(
            gsT[:, n0:n0 + NCK * 128].reshape(FC, 4, NCK, 128)
            .transpose(2, 1, 0, 3).reshape(NCK, 4, D))
        im["uscale"] = np.ascontiguousarray(
            usT[:, n0:n0 + NCK * 128].reshape(FC, 4, NCK, 128)
            .transpose(2, 1, 0, 3).reshape(NCK, 4, D))
        im["dscale"] = np.ascontiguousarray(
            dsT[c * 64:(c + 1) * 64].reshape(NCK, 4, D))
        in_maps.append(im)
    return in_maps




class Sched:
    ENG_SEM = {"sync": "ld", "tensor": "tn", "vector": "vc",
               "scalar": "sc", "gpsimd": "gp"}

    def __init__(self, V):
        self.steps = []
        self.base = {v: V.get(v, 0) for v in self.ENG_SEM.values()}

    def add(self, eng, deps, fn, n=1, sem=None):
        sid = len(self.steps)
        self.steps.append((eng, list(deps), fn, n, sem))
        return sid

    def emit(self, nc, S, V, inc):
        cnt = {}
        after = {}
        semof = {}
        for sid, (eng, deps, fn, n, sem) in enumerate(self.steps):
            sname = sem or self.ENG_SEM[eng]
            if sname not in self.base:
                self.base[sname] = V.get(sname, 0)
            cnt[sname] = cnt.get(sname, 0) + n
            after[sid] = (eng, sname, cnt[sname])
        per_eng = {e: [] for e in self.ENG_SEM}
        for sid, (eng, deps, fn, n, sem) in enumerate(self.steps):
            per_eng[eng].append(sid)

        sched = self

        with nc.Block() as blk:
            def make(eng_name):
                def body(e):
                    for sid in per_eng[eng_name]:
                        _, deps, fn, n, sem = sched.steps[sid]
                        sname = sem or sched.ENG_SEM[eng_name]
                        need = {}
                        for dp in deps:
                            deng, dsem, dcnt = after[dp]
                            if deng == eng_name and dsem == sched.ENG_SEM[deng]:
                                continue
                            if deng == eng_name and dsem != sched.ENG_SEM[deng]:
                                pass
                            need[dsem] = max(need.get(dsem, 0),
                                             sched.base[dsem] + dcnt)
                        for s2, val in need.items():
                            e.wait_ge(S[s2], val)
                        last = fn(e)
                        inc(last, sname, n)
                return body
            if per_eng["sync"]:
                blk.sync(make("sync"))
            if per_eng["tensor"]:
                blk.tensor(make("tensor"))
            if per_eng["scalar"]:
                blk.scalar(make("scalar"))
            if per_eng["vector"]:
                blk.vector(make("vector"))
            if per_eng["gpsimd"]:
                blk.gpsimd(make("gpsimd"))


def build_nc(dump=()):
    nc = bass.Bass("TRN2", target_bir_lowering=False)

    def din(name, shape, dtype):
        return nc.dram_tensor(name, shape, dtype, kind="ExternalInput")

    xT = din("xT", [D, TH], F32)
    wW = {"q": din("wq", [FC, D, 128], F32), "k": din("wk", [FC, D, 128], F32),
          "v": din("wv", [FC, D, 128], F32), "o": din("wo", [FC, D, 128], F32)}
    lA = din("lA", [4, FC, 128, R], F32)
    lB = {"q": din("lqB", [R, D], BF16), "k": din("lkB", [R, D], BF16),
          "v": din("lvB", [R, D], BF16), "o": din("loB", [R, D], BF16)}
    normw = din("normw", [128, FC], F32)
    cvec = din("cvec", [128, FC], F32)
    wmslab = din("wmslab", [FC, 128, 1536], F32)
    ropeqc = din("ropeqc", [128, T], BF16)
    ropeqs = din("ropeqs", [128, T], BF16)
    ropekc = din("ropekc", [128, TH], BF16)
    ropeks = din("ropeks", [128, TH], BF16)
    maskE = din("maskE", [4, 5, 128, 128], F32)
    maskS = din("maskS", [2, 128, 128], F32)
    identD = din("ident", [128, 128], BF16)
    onescolD = din("onescol", [128, 1], BF16)
    onesrowD = din("onesrow", [1, 128], BF16)
    one11 = din("one11", [1, 1], F32)
    epsD = din("epsv", [1, 2], F32)
    selD = din("sel", [4, 128], BF16)
    codesD = {"g": din("gcodes", [NCK, 128, FC, 128], BF16),
              "u": din("ucodes", [NCK, 128, FC, 128], BF16),
              "d": din("dcodes", [NCK, 128, D], BF16)}
    scaleD = {"g": din("gscale", [NCK, 4, D], F32),
              "u": din("uscale", [NCK, 4, D], F32),
              "d": din("dscale", [NCK, 4, D], F32)}

    xout = nc.dram_tensor("xout", [D, T], F32, kind="ExternalOutput")
    dumps = {}
    if "hn" in dump:
        dumps["hn"] = nc.dram_tensor("d_hn", [128, FC, TH], BF16, kind="ExternalOutput")
    if "xp" in dump:
        dumps["xp"] = nc.dram_tensor("d_xp", [128, FC, T], F32, kind="ExternalOutput")
    if "mv" in dump:
        dumps["mv"] = nc.dram_tensor("d_mv", [128, 96], F32, kind="ExternalOutput")
    if "qkv" in dump:
        dumps["q"] = nc.dram_tensor("d_q", [128, GH, T], BF16, kind="ExternalOutput")
        dumps["k"] = nc.dram_tensor("d_k", [128, GH, TH], BF16, kind="ExternalOutput")
        dumps["v"] = nc.dram_tensor("d_v", [128, TH // 128, GH * 128], BF16, kind="ExternalOutput")
    if "attn" in dump:
        dumps["attn"] = nc.dram_tensor("d_attn", [128, GH, T], BF16, kind="ExternalOutput")
    if "h2" in dump:
        dumps["h2"] = nc.dram_tensor("d_h2", [128, FC, T], BF16, kind="ExternalOutput")
    if "arena" in dump:
        dumps["arena"] = nc.dram_tensor("d_arena", [128, 70912], BF16, kind="ExternalOutput")
    if "inter" in dump:
        dumps["inter"] = nc.dram_tensor("d_inter", [128, FC, 512], BF16, kind="ExternalOutput")
    if "wdq" in dump:
        dumps["wdq"] = nc.dram_tensor("d_wdq", [3 * NCK * FC * 128 * 128], BF16,
                                      kind="ExternalOutput")

    NSL = NCK * FC * 128 * 128
    deq_in = nc.dram_tensor("deq_in", [3 * NSL], BF16)
    deq_out = nc.dram_tensor("deq_out", [NCORE, 3 * NSL], BF16, addr_space="Shared")
    mod_b = nc.dram_tensor("mod_b", [1, 1536], F32)
    mod_g = nc.dram_tensor("mod_g", [NCORE, 1536], F32)

    stk = contextlib.ExitStack()
    ent = stk.enter_context
    S = {k: ent(nc.semaphore(k)) for k in
         ["ld", "st", "cc", "tn", "vc", "sc", "gp",
          "lb0", "lb1", "lb2", "lb3", "lb4", "lb5", "ldd", "lws", "lws1", "lmk",
          "lc0", "lc1", "lsf", "lwm", "lmd", "lhx", "st0", "st1",
          "dd0", "dd1", "dd2"]}
    V = {k: 0 for k in S}

    def inc(instr, k, n=1):
        instr.then_inc(S[k], n)
        V[k] += n
        return V[k]

    xo = ent(nc.sbuf_tensor("xo", [128, FC, T], F32))
    mskSb = ent(nc.sbuf_tensor("mskSb", [128, 2, 128], F32))
    mskEb = ent(nc.sbuf_tensor("mskEb", [128, 5, 128], F32))
    idn = ent(nc.sbuf_tensor("idn", [128, 128], BF16))
    diag = ent(nc.sbuf_tensor("diag", [128, 128], BF16))
    mv = ent(nc.sbuf_tensor("mv", [128, 96], F32))
    nwsb = ent(nc.sbuf_tensor("nwsb", [128, FC], F32))
    csb = ent(nc.sbuf_tensor("csb", [128, FC], F32))
    csbb = ent(nc.sbuf_tensor("csbb", [128, FC], BF16))
    onec = ent(nc.sbuf_tensor("onec", [128, 1], BF16))
    oner = ent(nc.sbuf_tensor("oner", [1, 128], BF16))
    one1 = ent(nc.sbuf_tensor("one1", [1, 1], F32))
    epsb = ent(nc.sbuf_tensor("epsb", [1, 2], F32))
    selb = ent(nc.sbuf_tensor("selb", [4, 128], BF16))
    gcl = ent(nc.sbuf_tensor("gcl", [128, FC], F32))
    racc1 = ent(nc.sbuf_tensor("racc1", [128, 1], F32))
    racc2 = ent(nc.sbuf_tensor("racc2", [128, 1], F32))
    rmx1 = ent(nc.sbuf_tensor("rmx1", [128, 1], F32))
    rmx2 = ent(nc.sbuf_tensor("rmx2", [128, 1], F32))
    rsum = ent(nc.sbuf_tensor("rsum", [128, 1], F32))

    ARN = 70912
    arena = ent(nc.sbuf_tensor("arena", [128, ARN], BF16))

    def arb(off, n):
        return arena[:, off:off + n]

    def arf(off, n):
        return arena[:, off:off + 2 * n].bitcast(F32)

    MODROW_O = 0
    WMF_O = 3072
    WMB_O = 6144
    SCF_O = 7680
    SCB_O = 11776
    DQ_C = [13824, 15872]
    DQ_T = [17920, 19968, 22016, 24064]
    DQ_W = [26112, 28160]
    XST_O = 0
    HN_O = 16384
    H1_O = 40960
    CST_O = 49152
    RW_O = 53248
    RAB_O = 55808
    RQC_O, RQS_O, RKC_O, RKS_O = 0, 1024, 2048, 3584
    LAS_O = 5120
    LBS_O = {"q": 7168, "k": 9216, "v": 11264, "o": 13312}
    LASB_O = 15360
    TMPL_O = 40960
    TMPO_O = 42496
    QG_O = 44544
    KG_O = 48640
    VG_O = 54784
    ATG_O = 60928
    WS_O = 65024
    WB_O = [67072]
    SCR_O = 69120
    EXP_O = 69120
    PT_O = 69760
    VT_O = 70400
    QR_O = [69120, 69632, 70144]
    H2_O = 16384
    INT_O = 32768
    H1B2_O = 49152
    CST2_O = 57344
    RW2_O = 61440
    RAB2_O = 64000
    FW_O = [57344, 59392, 61440, 63488, 65536, 67584]
    GEL_O = 69632

    psA = ent(nc.psum_tensor("psA", [128, 2048], F32))
    psB = ent(nc.psum_tensor("psB", [128, 1024], F32))
    psC = ent(nc.psum_tensor("psC", [128, 512], F32))
    psT = ent(nc.psum_tensor("psT", [128, 1024], BF16))

    with nc.Block() as blk:
        @blk.sync
        def _(e):
            for dst, src in [
                (csb[:, :], cvec[:, :]), (nwsb[:, :], normw[:, :]),
                (one1[:, :], one11[:, :]), (selb[:, :], selD[:, :]),
                (epsb[:, :], epsD[:, :]),
                (idn[:, :], identD[:, :]),
                (onec[:, :], onescolD[:, :]), (oner[:, :], onesrowD[:, :]),
                (mskSb[:, 0, :], maskS[0]), (mskSb[:, 1, :], maskS[1]),
            ]:
                inc(e.dma_start(dst, src), "ld", 16)
            inc(e.dma_start(
                xo[:, :, :],
                xT[:, HB:TH].rearrange("(a p) t -> p a t", p=128)), "ld", 16)
            V["A1"] = V["ld"]

        @blk.scalar
        def _(e):
            e.wait_ge(S["ld"], V["A1"])
            inc(e.activation(csb[:, :], csb[:, :], AF.Silu), "sc")

    wm_f = arf(WMF_O, 1536)
    with nc.Block() as blk:
        b_ld, b_sc, b_tn = V["ld"], V["sc"], V["tn"]

        @blk.sync
        def _(e):
            for kc in range(FC):
                if kc >= 1:
                    e.wait_ge(S["tn"], b_tn + kc)
                inc(e.dma_start(wm_f[:, :], wmslab[kc]), "lwm", 16)

        @blk.tensor
        def _(e):
            e.wait_ge(S["sc"], b_sc)
            for kc in range(FC):
                e.wait_ge(S["lwm"], 16 * (kc + 1))
                for cc in range(3):
                    mm = e.matmul(psA[0:1, cc * 512:(cc + 1) * 512],
                                  csb[:, kc:kc + 1],
                                  wm_f[:, cc * 512:(cc + 1) * 512],
                                  start=(kc == 0), stop=(kc == FC - 1))
                inc(mm, "tn")

    modrow = arf(MODROW_O, 1536)
    with nc.Block() as blk:
        @blk.vector
        def _(e):
            inc(e.tensor_copy(modrow[0:1, :], psA[0:1, 0:1536]), "vc")
            V["modev"] = V["vc"]

        @blk.gpsimd
        def _(e):
            e.wait_ge(S["vc"], V["modev"])
            inc(e.dma_start(mod_b[:, :], modrow[0:1, :]), "gp", 16)
            e.wait_ge(S["gp"], V["gp"])
            inc(e.collective_compute(
                "AllGather", ALU.bypass, replica_groups=[list(range(NCORE))],
                ins=[mod_b.ap().opt()], outs=[mod_g.ap().opt()]), "cc")
            V["modcc"] = V["cc"]

    with nc.Block() as blk:
        b_ld, b_tn = V["ld"], V["tn"]

        @blk.sync
        def _(e):
            e.wait_ge(S["cc"], V["modcc"])
            for s in range(NCORE):
                if s >= 1:
                    e.wait_ge(S["tn"], b_tn + s)
                inc(e.dma_start(modrow[0:1, :], mod_g[s:s + 1, :]), "lmd", 16)

        @blk.tensor
        def _(e):
            for s in range(NCORE):
                e.wait_ge(S["lmd"], 16 * (s + 1))
                for j in range(12):
                    mm = e.matmul(psB[:, s * 12 + j:s * 12 + j + 1],
                                  modrow[0:1, j * 128:(j + 1) * 128],
                                  one1[:, :], start=True, stop=True)
                inc(mm, "tn")
            V["mvT"] = V["tn"]

        @blk.vector
        def _(e):
            e.wait_ge(S["tn"], V["mvT"])
            e.tensor_copy(mv[:, :], psB[:, 0:96])
            e.drain()
            for col in (32, 80):
                g = mv[:, col:col + FC]
                t0 = gcl[:, :]
                e.tensor_scalar(t0, g, -1.0, None, ALU.mult)
                e.drain()
                e.tensor_tensor(t0, g, t0, ALU.max)
                e.drain()
                e.tensor_scalar(t0, t0, 0.25, None, ALU.max)
                e.tensor_scalar(g, g, 0.0, 2.0, ALU.is_ge, ALU.mult)
                e.drain()
                e.tensor_scalar(g, g, -1.0, None, ALU.add)
                e.drain()
                e.tensor_tensor(g, t0, g, ALU.mult)
                e.drain()
            e.tensor_scalar(mv[:, 16:32], mv[:, 16:32], 1.0, None, ALU.add)
            inc(e.tensor_scalar(mv[:, 64:80], mv[:, 64:80], 1.0, None, ALU.add),
                "vc")
            V["mvdone"] = V["vc"]

    MVO = {"sa": 0, "ca1": 16, "ga": 32, "sb": 48, "cb1": 64, "gb": 80}

    scf = arf(SCF_O, 2048)
    scb = arb(SCB_O, 2048)
    NSLICES = 3 * NCK
    with nc.Block() as blk:
        b_ld, b_vc, b_tn, b_gp = V["ld"], V["vc"], V["tn"], V["gp"]

        @blk.sync
        def _(e):
            for i in range(NSLICES):
                mat = "gud"[i // NCK]
                ncl = i % NCK
                buf = i % 2
                if i >= 2:
                    e.wait_ge(S["vc"], b_vc + 2 * (i - 1))
                cb = arb(DQ_C[buf], 2048)
                if mat == "d":
                    inc(e.dma_start(cb, codesD["d"][ncl]), "lc%d" % buf, 16)
                else:
                    inc(e.dma_start(cb.rearrange("p (a b) -> p a b", a=FC),
                                    codesD[mat][ncl]), "lc%d" % buf, 16)
                if i >= 1:
                    e.wait_ge(S["vc"], b_vc + 2 * (i - 1) + 1)
                inc(e.dma_start(scf[0:4, 0:D], scaleD[mat][ncl]), "lsf", 16)

        @blk.vector
        def _(e):
            for i in range(NSLICES):
                buf = i % 2
                cb = arb(DQ_C[buf], 2048)
                tA = arb(DQ_T[0], 2048)
                tB = arb(DQ_T[1], 2048)
                tC = arb(DQ_T[2], 2048)
                tD = arb(DQ_T[3], 2048)
                wout = arb(DQ_W[buf], 2048)
                e.wait_ge(S["lc%d" % buf], 16 * (i // 2 + 1))
                e.wait_ge(S["lsf"], 16 * (i + 1))
                if i >= 1:
                    e.wait_ge(S["tn"], b_tn + i)
                inc(e.tensor_copy(scb[0:4, 0:D], scf[0:4, 0:D]), "vc")
                e.wait_ge(S["tn"], b_tn + i + 1)
                e.tensor_scalar(tA, cb, 8.0, -8.0, ALU.is_ge, ALU.mult)
                e.tensor_tensor(tB, cb, tA, ALU.add)
                e.tensor_scalar(tC, tA, 0.125, 0.5, ALU.mult, ALU.add)
                e.tensor_scalar(tD, tB, 2.0, -4.0, ALU.mult, ALU.add)
                e.tensor_tensor(tD, tB, tD, ALU.max)
                e.tensor_scalar(tB, tB, 4.0, -16.0, ALU.mult, ALU.add)
                e.tensor_tensor(tB, tD, tB, ALU.max)
                e.tensor_tensor(tB, tB, tC, ALU.mult)
                if i >= 2:
                    e.wait_ge(S["st%d" % buf], 16 * (i // 2))
                inc(e.tensor_tensor(wout, tB, psA[:, 0:2048], ALU.mult), "vc")

        @blk.tensor
        def _(e):
            for i in range(NSLICES):
                e.wait_ge(S["vc"], b_vc + 2 * i + 1)
                for kc in range(FC):
                    mm = e.matmul(psA[:, kc * 128:(kc + 1) * 128], selb[:, :],
                                  scb[0:4, kc * 128:(kc + 1) * 128],
                                  start=True, stop=True)
                inc(mm, "tn")

        @blk.gpsimd
        def _(e):
            for i in range(NSLICES):
                ncl = i % NCK
                buf = i % 2
                wout = arb(DQ_W[buf], 2048)
                e.wait_ge(S["vc"], b_vc + 2 * i + 2)
                if i // NCK == 2:
                    dreg = deq_in[2 * NSL:3 * NSL].rearrange(
                        "(a p n c) -> a p n c", a=FC, p=128, n=NCK)
                    inc(e.dma_start(
                        dreg[:, :, ncl, :].transpose([1, 0, 2]),
                        wout.rearrange("p (a c) -> p a c", a=FC)),
                        "st%d" % buf, 16)
                else:
                    off = (i // NCK) * NSL + ncl * (FC * 128 * 128)
                    inc(e.dma_start(
                        deq_in[off:off + 2048 * 128].rearrange("(p n) -> p n", p=128),
                        wout), "st%d" % buf, 16)
            e.wait_ge(S["st0"], 16 * (NSLICES // 2))
            e.wait_ge(S["st1"], 16 * (NSLICES // 2))
            inc(e.collective_compute(
                "AllGather", ALU.bypass, replica_groups=[list(range(NCORE))],
                ins=[deq_in.ap().opt()], outs=[deq_out.ap().opt()]), "cc")
            V["deqcc"] = V["cc"]

    if "wdq" in dumps:
        with nc.Block() as blk:
            @blk.gpsimd
            def _(e):
                inc(e.dma_start(dumps["wdq"][:], deq_in[:]), "gp", 16)
                e.wait_ge(S["gp"], V["gp"])

    def emit_ln(blk_nc, xsrc_fn, nchunks, hn_view_fn, mvo_c, mvo_s, mvo_g,
                with_rms, h1_off, cst_off, rw_off, rab_off):
        pass

    cst = arb(CST_O, FC * LNC)
    h1b = arf(H1_O, FC * LNC)
    hn = arb(HN_O, FC * TH)
    rwv = {}
    for j, k in enumerate(["mean", "ms", "var", "rr", "bb"]):
        rwv[k] = arf(RW_O + 2 * j * LNC, LNC)[0:1, :]
    rAB = arb(RAB_O, 2 * LNC)[0:1, :]

    def hn_v(kc, t0):
        return hn.rearrange("p (a b) -> p a b", a=FC)[:, kc, t0:t0 + LNC]

    def xsrc1(ch, kc):
        if ch < 2:
            return arf(XST_O, FC * 512).rearrange("p (a b) -> p a b", a=FC)[
                :, kc, ch * LNC:(ch + 1) * LNC]
        return xo[:, kc, (ch - 2) * LNC:(ch - 1) * LNC]

    def cst_v(kc):
        return cst.rearrange("p (a b) -> p a b", a=FC)[:, kc, :]

    def h1_v(kc):
        return h1b.rearrange("p (a b) -> p a b", a=FC)[:, kc, :]

    with nc.Block() as blk:
        b_ld, b_sc, b_tn, b_vc = V["ld"], V["sc"], V["tn"], V["vc"]

        @blk.sync
        def _(e):
            inc(e.dma_start(
                arf(XST_O, FC * 512).rearrange("p (a b) -> p a b", a=FC),
                xT[:, 0:HB].rearrange("(a p) t -> p a t", p=128)), "lhx", 16)

        @blk.scalar
        def _(e):
            for ch in range(NLN1):
                e.wait_ge(S["lhx"], 16)
                if ch >= 1:
                    e.wait_ge(S["tn"], b_tn + 5 * (ch - 1) + 4)
                xin = (xo[:, :, (ch - 2) * LNC:(ch - 1) * LNC] if ch >= 2 else
                       arf(XST_O, FC * 512).rearrange("p (a b) -> p a b", a=FC)
                       [:, :, ch * LNC:(ch + 1) * LNC])
                inc(e.activation(cst.rearrange("p (a b) -> p a b", a=FC),
                                 xin, AF.Copy), "sc")
                e.wait_ge(S["tn"], b_tn + 5 * ch + 1)
                inc(e.activation(cst, cst, AF.Square), "sc")
                e.wait_ge(S["vc"], b_vc + 6 * ch + 1)
                inc(e.activation(rwv["var"], rwv["var"], AF.Sqrt,
                                 bias=epsb[0:1, 0:1]), "sc")
                e.wait_ge(S["vc"], b_vc + 6 * ch + 3)
                inc(e.activation(cst.rearrange("p (a b) -> p a b", a=FC),
                                 h1b.rearrange("p (a b) -> p a b", a=FC),
                                 AF.Square), "sc")
                e.wait_ge(S["vc"], b_vc + 6 * ch + 4)
                inc(e.activation(rwv["ms"], rwv["ms"], AF.Sqrt,
                                 bias=epsb[0:1, 1:2]), "sc")

        @blk.tensor
        def _(e):
            for ch in range(NLN1):
                e.wait_ge(S["sc"], b_sc + 5 * ch + 1)
                if ch >= 1:
                    e.wait_ge(S["vc"], b_vc + 6 * (ch - 1) + 4)
                for kc in range(FC):
                    mm = e.matmul(psC[0:1, 0:LNC], onec[:, :], cst_v(kc),
                                  start=(kc == 0), stop=(kc == FC - 1))
                inc(mm, "tn")
                e.wait_ge(S["sc"], b_sc + 5 * ch + 2)
                for kc in range(FC):
                    mm = e.matmul(psC[0:1, LNC:2 * LNC], onec[:, :], cst_v(kc),
                                  start=(kc == 0), stop=(kc == FC - 1))
                inc(mm, "tn")
                e.wait_ge(S["vc"], b_vc + 6 * ch + 2)
                inc(e.matmul(psB[:, 0:2 * LNC], oner[:, :], rAB,
                             start=True, stop=True), "tn")
                e.wait_ge(S["sc"], b_sc + 5 * ch + 4)
                for kc in range(FC):
                    mm = e.matmul(psC[0:1, LNC:2 * LNC], onec[:, :], cst_v(kc),
                                  start=(kc == 0), stop=(kc == FC - 1))
                inc(mm, "tn")
                e.wait_ge(S["vc"], b_vc + 6 * ch + 5)
                inc(e.matmul(psB[:, 512:512 + LNC], oner[:, :],
                             rAB[:, 0:LNC], start=True, stop=True), "tn")

        @blk.vector
        def _(e):
            for ch in range(NLN1):
                own = ch >= 2
                t0 = ch * LNC
                e.wait_ge(S["tn"], b_tn + 5 * ch + 2)
                e.tensor_scalar(rwv["mean"], psC[0:1, 0:LNC], 1.0 / D, None, ALU.mult)
                e.tensor_scalar(rwv["ms"], psC[0:1, LNC:2 * LNC], 1.0 / D, None, ALU.mult)
                e.tensor_tensor(rwv["var"], rwv["mean"], rwv["mean"], ALU.mult)
                inc(e.tensor_tensor(rwv["var"], rwv["ms"], rwv["var"],
                                    ALU.subtract), "vc")
                e.wait_ge(S["sc"], b_sc + 5 * ch + 3)
                e.reciprocal(rwv["rr"], rwv["var"])
                e.tensor_tensor(rwv["bb"], rwv["mean"], rwv["rr"], ALU.mult)
                e.tensor_scalar(rwv["bb"], rwv["bb"], -1.0, None, ALU.mult)
                e.tensor_copy(rAB[:, 0:LNC], rwv["rr"])
                inc(e.tensor_copy(rAB[:, LNC:2 * LNC], rwv["bb"]), "vc")
                e.wait_ge(S["tn"], b_tn + 5 * ch + 3)
                for kc in range(FC):
                    e.tensor_tensor(h1_v(kc), xsrc1(ch, kc), psB[:, 0:LNC], ALU.mult)
                    e.tensor_tensor(h1_v(kc), h1_v(kc), psB[:, LNC:2 * LNC], ALU.add)
                    e.tensor_scalar(h1_v(kc), h1_v(kc),
                                    mv[:, MVO["ca1"] + kc:MVO["ca1"] + kc + 1],
                                    mv[:, MVO["sa"] + kc:MVO["sa"] + kc + 1],
                                    ALU.mult, ALU.add)
                if own:
                    for kc in range(FC):
                        xv = xo[:, kc, (ch - 2) * LNC:(ch - 1) * LNC]
                        last = e.scalar_tensor_tensor(
                            xv, h1_v(kc),
                            mv[:, MVO["ga"] + kc:MVO["ga"] + kc + 1],
                            xv, ALU.mult, ALU.add)
                    inc(last, "vc")
                else:
                    inc(e.tensor_copy(rmx1[:, :], rmx1[:, :]), "vc")
                e.wait_ge(S["tn"], b_tn + 5 * ch + 4)
                inc(e.tensor_scalar(rwv["ms"], psC[0:1, LNC:2 * LNC],
                                    1.0 / D, None, ALU.mult), "vc")
                e.wait_ge(S["sc"], b_sc + 5 * ch + 5)
                e.reciprocal(rwv["rr"], rwv["ms"])
                inc(e.tensor_copy(rAB[:, 0:LNC], rwv["rr"]), "vc")
                e.wait_ge(S["tn"], b_tn + 5 * ch + 5)
                for kc in range(FC):
                    last = e.scalar_tensor_tensor(
                        hn_v(kc, t0), h1_v(kc),
                        nwsb[:, kc:kc + 1], psB[:, 512:512 + LNC],
                        ALU.mult, ALU.mult)
                inc(last, "vc")

    if "hn" in dumps or "xp" in dumps or "mv" in dumps:
        with nc.Block() as blk:
            @blk.gpsimd
            def _(e):
                if "hn" in dumps:
                    inc(e.dma_start(
                        dumps["hn"].ap().rearrange("p a b -> p (a b)"), hn),
                        "gp", 16)
                if "xp" in dumps:
                    inc(e.dma_start(dumps["xp"][:, :, :], xo[:, :, :]), "gp", 16)
                if "mv" in dumps:
                    inc(e.dma_start(dumps["mv"][:, :], mv[:, :]), "gp", 16)
                e.wait_ge(S["gp"], V["gp"])

    SHUF = []
    for i in range(32):
        SHUF.append(i ^ 1)

    rqcv = arb(RQC_O, T)
    rqsv = arb(RQS_O, T)
    rkcv = arb(RKC_O, TH)
    rksv = arb(RKS_O, TH)
    lasv = arf(LAS_O, 1024)
    lasb = arb(LASB_O, 1024)
    lbsv = {k: arb(LBS_O[k], 2048) for k in "qkvo"}
    tmplv = arb(TMPL_O, 1536)
    tmpov = arf(TMPO_O, 1024)
    qgv = arb(QG_O, GH * T).rearrange("p (a b) -> p a b", a=GH)
    kgv = arb(KG_O, GH * TH).rearrange("p (a b) -> p a b", a=GH)
    vgv = arb(VG_O, 12 * GH * 128).rearrange("p (a b) -> p a b", a=12)
    atgv = arb(ATG_O, GH * T)
    wsv = arf(WS_O, 512)
    wsv2 = [arf(WS_O, 512), arf(66048, 512)]
    wbv = arb(WB_O[0], 2048).rearrange("p (a b) -> p a b", a=FC)
    expPv = arb(EXP_O, 640)
    ptbv = arb(PT_O, 640)
    vtmpv = arb(VT_O, 512)
    qrawv = arb(QR_O[0], 512)
    qrotv = arb(QR_O[1], 512)
    rbbv = arb(QR_O[2], 512)
    hn_r = hn.rearrange("p (a b) -> p a b", a=FC)

    with nc.Block() as blk:
        @blk.sync
        def _(e):
            inc(e.dma_start(rqcv, ropeqc[:, :]), "ld", 16)
            inc(e.dma_start(rqsv, ropeqs[:, :]), "ld", 16)
            inc(e.dma_start(rkcv, ropekc[:, :]), "ld", 16)
            inc(e.dma_start(rksv, ropeks[:, :]), "ld", 16)
            inc(e.dma_start(
                lasv.rearrange("p (a b c) -> p a b c", a=4, b=FC),
                lA[:, :, :, :].transpose([2, 0, 1, 3])), "ld", 16)
            for k in "qkvo":
                inc(e.dma_start(lbsv[k][0:R, :], lB[k][:, :]), "ld", 16)
            V["p3ld"] = V["ld"]

        @blk.scalar
        def _(e):
            e.wait_ge(S["ld"], V["p3ld"])
            inc(e.activation(lasb, lasv, AF.Copy), "sc")

    PRJ = [("q", 0, 2, HB), ("k", 1, 3, 0), ("v", 2, 3, 0)]

    def emit_proj(g):
        sd = Sched(V)
        S_ = S
        psum_free = [None, None]
        wb_free = None
        ws_free = [None, None]
        psT_free = None
        pd = {}
        qraw_free = None
        qrot_free = None
        rbb_free = None
        vtmp_free = None

        for pr, pi, NT, toff in PRJ:
            for tch in range(NT):
                def mkA(pi=pi, tch=tch, toff=toff):
                    def f(e):
                        for kc in range(FC):
                            mm = e.matmul(
                                psC[0:R, 0:512],
                                lasb[:, (pi * FC + kc) * R:(pi * FC + kc + 1) * R],
                                hn_r[:, kc, toff + tch * 512:toff + (tch + 1) * 512],
                                start=(kc == 0), stop=(kc == FC - 1))
                        return mm
                    return f
                deps = [pd[(pr, tch - 1, "ev")]] if tch >= 1 else                     ([pd[(PRJ[pi - 1][0], PRJ[pi - 1][2] - 1, "ev")]] if pi >= 1 else [])
                tA = sd.add("tensor", deps, mkA())
                def mkAe(pr=pr, tch=tch):
                    def f(e):
                        return e.activation(
                            tmplv[0:R, tch * 512:(tch + 1) * 512],
                            psC[0:R, 0:512], AF.Copy, scale=1.0 / R)
                    return f
                eA = sd.add("scalar", [tA], mkAe())
                pd[(pr, tch, "ev")] = eA

            for oc4 in range(GH):
                h = GH * g + oc4
                casts = []
                for q4 in range(4):
                    w = q4 % 2
                    def mkD(pr=pr, h=h, q4=q4, w=w):
                        def f(e):
                            return e.dma_start(
                                wsv2[w].rearrange("p (a b) -> p a b", a=4),
                                wW[pr][h][q4 * 512:(q4 + 1) * 512, :]
                                .rearrange("(a p) j -> p a j", p=128))
                        return f
                    dd = [ws_free[w]] if ws_free[w] is not None else []
                    dq = sd.add("sync", dd, mkD(), n=16,
                                sem="lws" if w == 0 else "lws1")
                    def mkC(q4=q4, w=w):
                        def f(e):
                            return e.activation(
                                wbv[:, 4 * q4:4 * (q4 + 1), :],
                                wsv2[w].rearrange("p (a b) -> p a b", a=4),
                                AF.Copy)
                        return f
                    cd = [dq] + ([wb_free] if wb_free is not None else [])
                    ct = sd.add("scalar", cd, mkC())
                    ws_free[w] = ct
                    casts.append(ct)

                for tch in range(NT):
                    slot = (oc4 * NT + tch) % 2
                    def mkM(pr=pr, h=h, tch=tch, toff=toff, slot=slot):
                        def f(e):
                            p = psA[:, slot * 512:(slot + 1) * 512]
                            for kc in range(FC):
                                e.matmul(p, wbv[:, kc, :],
                                         hn_r[:, kc, toff + tch * 512:
                                              toff + (tch + 1) * 512],
                                         start=(kc == 0), stop=False)
                            return e.matmul(
                                p, lbsv[pr][0:R, h * 128:(h + 1) * 128],
                                tmplv[0:R, tch * 512:(tch + 1) * 512],
                                start=False, stop=True)
                        return f
                    mdeps = casts + [pd[(pr, tch, "ev")]]
                    if psum_free[slot] is not None:
                        mdeps += psum_free[slot]
                    mm = sd.add("tensor", mdeps, mkM())
                    wb_free = mm

                    if pr in ("q", "k"):
                        cosv = rqcv if pr == "q" else rkcv
                        sinv = rqsv if pr == "q" else rksv
                        dst = (qgv if pr == "q" else kgv)
                        def mkES(slot=slot):
                            def f(e):
                                return e.copy(
                                    qrawv, psA[:, slot * 512:(slot + 1) * 512])
                            return f
                        esd = [mm] + ([qraw_free] if qraw_free is not None else [])
                        evS = sd.add("scalar", esd, mkES())
                        def mkSh():
                            def f(e):
                                return e.stream_shuffle(qrotv, qrawv, SHUF)
                            return f
                        sh_d = [evS] + ([qrot_free] if qrot_free is not None else [])
                        evV1 = sd.add("vector", sh_d, mkSh())
                        qraw_free = evV1
                        def mkA2(slot=slot, oc4=oc4, tch=tch, cosv=cosv, dst=dst):
                            def f(e):
                                return e.tensor_tensor(
                                    dst[:, oc4, tch * 512:(tch + 1) * 512],
                                    psA[:, slot * 512:(slot + 1) * 512],
                                    cosv[:, tch * 512:(tch + 1) * 512], ALU.mult)
                            return f
                        evV2 = sd.add("vector", [mm], mkA2())
                        def mkB(sinv=sinv, tch=tch):
                            def f(e):
                                return e.tensor_tensor(
                                    rbbv, qrotv,
                                    sinv[:, tch * 512:(tch + 1) * 512], ALU.mult)
                            return f
                        gb_d = [evV1] + ([rbb_free] if rbb_free is not None else [])
                        evG = sd.add("gpsimd", gb_d, mkB())
                        qrot_free = evG
                        def mkAdd(oc4=oc4, tch=tch, dst=dst):
                            def f(e):
                                return e.tensor_tensor(
                                    dst[:, oc4, tch * 512:(tch + 1) * 512],
                                    dst[:, oc4, tch * 512:(tch + 1) * 512],
                                    rbbv, ALU.add)
                            return f
                        evV3 = sd.add("vector", [evG], mkAdd())
                        rbb_free = evV3
                        psum_free[slot] = [evS, evV2]
                    else:
                        def mkEV(slot=slot):
                            def f(e):
                                return e.copy(
                                    vtmpv, psA[:, slot * 512:(slot + 1) * 512])
                            return f
                        evd = [mm] + ([vtmp_free] if vtmp_free is not None else [])
                        evS = sd.add("scalar", evd, mkEV())
                        def mkT():
                            def f(e):
                                for j in range(4):
                                    t = e.transpose(psT[:, j * 128:(j + 1) * 128],
                                                    vtmpv[:, j * 128:(j + 1) * 128],
                                                    idn[:, :])
                                return t
                            return f
                        td = [evS] + ([psT_free] if psT_free is not None else [])
                        tT = sd.add("tensor", td, mkT())
                        vtmp_free = tT
                        def mkVE(oc4=oc4, tch=tch):
                            def f(e):
                                for j in range(4):
                                    t = e.tensor_copy(
                                        vgv[:, tch * 4 + j,
                                            oc4 * 128:(oc4 + 1) * 128],
                                        psT[:, j * 128:(j + 1) * 128])
                                return t
                            return f
                        evV = sd.add("vector", [tT], mkVE())
                        psT_free = evV
                        psum_free[slot] = [evS]
        sd.emit(nc, S_, V, inc)

    def emit_attn(g):
        sd = Sched(V)
        exp_prev = []
        av_prev = None
        pt_slot_free = [None, None]
        expP_free = None
        mask_step = None
        for qt in range(8):
            if qt < 4:
                def mkMD(qt=qt):
                    def f(e):
                        return e.dma_start(mskEb[:, :, :],
                                           maskE[qt].transpose([1, 0, 2]))
                    return f
                mask_step = sd.add("sync", exp_prev + ([mask_step] if mask_step else []), mkMD(), n=16, sem="lmk")
            for oc4 in range(GH):
                def mkS(g=g, oc4=oc4, qt=qt):
                    def f(e):
                        qtile = qgv[:, oc4, qt * 128:(qt + 1) * 128]
                        e.matmul(psA[:, 1024:1536], qtile,
                                 kgv[:, oc4, qt * 128:qt * 128 + 512],
                                 start=True, stop=True)
                        return e.matmul(psA[:, 1536:1664], qtile,
                                        kgv[:, oc4, qt * 128 + 512:qt * 128 + 640],
                                        start=True, stop=True)
                    return f
                sdeps = list(exp_prev)
                if av_prev is not None:
                    sdeps.append(av_prev)
                s12 = sd.add("tensor", sdeps, mkS())
                def mkMk(qt=qt):
                    def f(e):
                        if qt < 4:
                            for s in range(4):
                                t = e.tensor_tensor(
                                    psA[:, 1024 + s * 128:1024 + (s + 1) * 128],
                                    psA[:, 1024 + s * 128:1024 + (s + 1) * 128],
                                    mskEb[:, s, :], ALU.add)
                            t = e.tensor_tensor(psA[:, 1536:1664],
                                                psA[:, 1536:1664],
                                                mskEb[:, 4, :], ALU.add)
                        else:
                            e.tensor_tensor(psA[:, 1024:1152], psA[:, 1024:1152],
                                            mskSb[:, 0, :], ALU.add)
                            t = e.tensor_tensor(psA[:, 1536:1664], psA[:, 1536:1664],
                                                mskSb[:, 1, :], ALU.add)
                        return t
                    return f
                mdeps = [s12] + ([mask_step] if qt < 4 else [])
                mk = sd.add("vector", mdeps, mkMk())
                def mkRx():
                    def f(e):
                        e.reduce_max(rmx1[:, :], psA[:, 1024:1536], AX.X)
                        e.reduce_max(rmx2[:, :], psA[:, 1536:1664], AX.X)
                        e.drain()
                        e.tensor_tensor(rmx1[:, :], rmx1[:, :], rmx2[:, :], ALU.max)
                        e.drain()
                        return e.tensor_scalar(rmx1[:, :], rmx1[:, :], -1.0,
                                               None, ALU.mult)
                    return f
                rx = sd.add("vector", [mk], mkRx())
                def mkEx():
                    def f(e):
                        e.activation(expPv[0:512].rearrange("p n -> p n") if False
                                     else expPv[:, 0:512] if False else expPv[:, :][:, 0:512],
                                     psA[:, 1024:1536], AF.Exp,
                                     bias=rmx1[:, :], accum_out=racc1[:, :])
                        return e.activation(expPv[:, :][:, 512:640],
                                            psA[:, 1536:1664], AF.Exp,
                                            bias=rmx1[:, :], accum_out=racc2[:, :])
                    return f
                edeps = [rx] + ([expP_free] if expP_free is not None else [])
                ex = sd.add("scalar", edeps, mkEx())
                exp_prev = [ex]
                def mkRs():
                    def f(e):
                        e.tensor_tensor(rsum[:, :], racc1[:, :], racc2[:, :], ALU.add)
                        e.drain()
                        e.reciprocal(rsum[:, :], rsum[:, :])
                        e.drain()
                        return e.tensor_scalar(diag[:, :], idn[:, :],
                                               rsum[:, :], None, ALU.mult)
                    return f
                rs = sd.add("vector", [ex], mkRs())
                evpt = []
                for s5 in range(5):
                    slot = s5 % 2
                    off = 0 if slot == 0 else 512
                    def mkPT(s5=s5, off=off):
                        def f(e):
                            return e.matmul(psB[:, off:off + 128],
                                            expPv[:, :][:, s5 * 128:(s5 + 1) * 128],
                                            diag[:, :], start=True, stop=True)
                        return f
                    pdeps = [rs]
                    if pt_slot_free[slot] is not None:
                        pdeps.append(pt_slot_free[slot])
                    pt = sd.add("tensor", pdeps, mkPT())
                    def mkPE(s5=s5, off=off):
                        def f(e):
                            return e.tensor_copy(
                                ptbv[:, s5 * 128:(s5 + 1) * 128],
                                psB[:, off:off + 128])
                        return f
                    pe = sd.add("vector", [pt], mkPE())
                    pt_slot_free[slot] = pe
                    evpt.append(pe)
                expP_free = evpt[-1]
                def mkPV(g=g, oc4=oc4, qt=qt):
                    def f(e):
                        for s5 in range(5):
                            t = e.matmul(psA[:, 1664:1792],
                                         vgv[:, qt + s5, oc4 * 128:(oc4 + 1) * 128],
                                         ptbv[:, s5 * 128:(s5 + 1) * 128],
                                         start=(s5 == 0), stop=(s5 == 4))
                        return t
                    return f
                pvdeps = list(evpt)
                if av_prev is not None:
                    pvdeps.append(av_prev)
                pv = sd.add("tensor", pvdeps, mkPV())
                def mkAV(oc4=oc4, qt=qt):
                    def f(e):
                        return e.tensor_copy(
                            atgv[:, oc4 * T + qt * 128:oc4 * T + (qt + 1) * 128],
                            psA[:, 1664:1792])
                    return f
                av = sd.add("vector", [pv], mkAV())
                av_prev = av

        ws_free = [None, None]
        wb_free = None
        psum_free = [None, None]
        for oc in range(FC):
            w = oc % 2
            def mkD(oc=oc, g=g, w=w):
                def f(e):
                    return e.dma_start(
                        wsv2[w].rearrange("p (a b) -> p a b", a=4),
                        wW["o"][oc][g * 512:(g + 1) * 512, :]
                        .rearrange("(a p) j -> p a j", p=128))
                return f
            dd = [ws_free[w]] if ws_free[w] is not None else []
            dq = sd.add("sync", dd, mkD(), n=16,
                        sem="lws" if w == 0 else "lws1")
            def mkC(w=w):
                def f(e):
                    return e.activation(
                        wbv[:, 0:4, :],
                        wsv2[w].rearrange("p (a b) -> p a b", a=4), AF.Copy)
                return f
            cd = [dq, av_prev] + ([wb_free] if wb_free is not None else [])
            ct = sd.add("scalar", cd, mkC())
            ws_free[w] = ct
            for tch in range(2):
                slot = tch
                def mkOM(oc=oc, tch=tch, slot=slot):
                    def f(e):
                        p = psA[:, slot * 512:(slot + 1) * 512]
                        for hl in range(GH):
                            t = e.matmul(p, wbv[:, hl, :],
                                         atgv[:, hl * T + tch * 512:
                                              hl * T + (tch + 1) * 512],
                                         start=(hl == 0), stop=(hl == GH - 1))
                        return t
                    return f
                odeps = [ct]
                if psum_free[slot] is not None:
                    odeps.append(psum_free[slot])
                om = sd.add("tensor", odeps, mkOM())
                wb_free = om
                def mkOX(oc=oc, tch=tch, slot=slot):
                    def f(e):
                        xv = xo[:, oc, tch * 512:(tch + 1) * 512]
                        return e.scalar_tensor_tensor(
                            xv, psA[:, slot * 512:(slot + 1) * 512],
                            mv[:, MVO["ga"] + oc:MVO["ga"] + oc + 1],
                            xv, ALU.mult, ALU.add)
                    return f
                ox = sd.add("vector", [om], mkOX())
                psum_free[slot] = ox
        oe_prev = None
        for tch in range(2):
            def mkOA(g=g, tch=tch):
                def f(e):
                    for hl in range(GH):
                        h = GH * g + hl
                        t = e.matmul(
                            psC[0:R, 0:512],
                            lasb[:, (3 * FC + h) * R:(3 * FC + h + 1) * R],
                            atgv[:, hl * T + tch * 512:hl * T + (tch + 1) * 512],
                            start=(hl == 0), stop=(hl == GH - 1))
                    return t
                return f
            adeps = [av_prev] + ([oe_prev] if oe_prev is not None else [])
            oa = sd.add("tensor", adeps, mkOA())
            def mkOE(g=g, tch=tch):
                def f(e):
                    dst = tmpov[0:R, tch * 512:(tch + 1) * 512]
                    if g == 0:
                        return e.tensor_copy(dst, psC[0:R, 0:512])
                    return e.tensor_tensor(dst, dst, psC[0:R, 0:512], ALU.add)
                return f
            oe = sd.add("vector", [oa], mkOE())
            oe_prev = oe
        sd.emit(nc, S_, V, inc)

    S_ = S
    for g in range(NG):
        emit_proj(g)
        if g == 0 and "arena" in dumps:
            with nc.Block() as blk:
                @blk.gpsimd
                def _(e):
                    inc(e.dma_start(dumps["arena"][:, :], arena[:, :]), "gp", 16)
                    e.wait_ge(S["gp"], V["gp"])
        if g == 0 and "q" in dumps:
            with nc.Block() as blk:
                @blk.gpsimd
                def _(e):
                    inc(e.dma_start(dumps["q"][:, :, :],
                                    arb(QG_O, GH * T).rearrange("p (a b) -> p a b", a=GH)), "gp", 16)
                    inc(e.dma_start(dumps["k"][:, :, :],
                                    arb(KG_O, GH * TH).rearrange("p (a b) -> p a b", a=GH)), "gp", 16)
                    inc(e.dma_start(dumps["v"][:, :, :],
                                    arb(VG_O, 12 * GH * 128).rearrange("p (a b) -> p a b", a=12)), "gp", 16)
                    e.wait_ge(S["gp"], V["gp"])
        emit_attn(g)
        if g == 0 and "attn" in dumps:
            with nc.Block() as blk:
                @blk.gpsimd
                def _(e):
                    inc(e.dma_start(dumps["attn"][:, :, :],
                                    atgv.rearrange("p (a b) -> p a b", a=GH)), "gp", 16)
                    e.wait_ge(S["gp"], V["gp"])

    with nc.Block() as blk:
        @blk.scalar
        def _(e):
            inc(e.activation(tmplv[0:R, 0:T], tmpov[0:R, 0:T], AF.Copy,
                             scale=1.0 / R), "sc")
            V["obc"] = V["sc"]

    sd = Sched(V)
    psum_free = [None, None]
    dummy = sd.add("scalar", [], lambda e: e.activation(
        vtmpv[0:1, 0:2], vtmpv[0:1, 0:2], AF.Copy))
    for oc in range(FC):
        for tch in range(2):
            slot = tch
            def mkBM(oc=oc, tch=tch, slot=slot):
                def f(e):
                    return e.matmul(psA[:, slot * 512:(slot + 1) * 512],
                                    lbsv["o"][0:R, oc * 128:(oc + 1) * 128],
                                    tmplv[0:R, tch * 512:(tch + 1) * 512],
                                    start=True, stop=True)
                return f
            bdeps = [dummy]
            if psum_free[slot] is not None:
                bdeps.append(psum_free[slot])
            bm = sd.add("tensor", bdeps, mkBM())
            def mkBX(oc=oc, tch=tch, slot=slot):
                def f(e):
                    xv = xo[:, oc, tch * 512:(tch + 1) * 512]
                    return e.scalar_tensor_tensor(
                        xv, psA[:, slot * 512:(slot + 1) * 512],
                        mv[:, MVO["ga"] + oc:MVO["ga"] + oc + 1],
                        xv, ALU.mult, ALU.add)
                return f
            bx = sd.add("vector", [bm], mkBX())
            psum_free[slot] = bx
    sd.emit(nc, S_, V, inc)

    H2_O2 = 0
    INT_O2 = 16384
    H1B2 = 16384
    CST2 = 24576
    RW2 = 28672
    RAB2 = 31232
    WDS_O = 32768
    GUB_O = [65536, 67584]
    GELB_O = 69632

    h2v = arb(H2_O2, FC * T)
    h2_r = h2v.rearrange("p (a b) -> p a b", a=FC)
    cst2 = arb(CST2, FC * LNC)
    h1b2 = arf(H1B2, FC * LNC)
    rwv2 = {}
    for j, k in enumerate(["mean", "ms", "var", "rr", "bb"]):
        rwv2[k] = arf(RW2 + 2 * j * LNC, LNC)[0:1, :]
    rAB2 = arb(RAB2, 2 * LNC)[0:1, :]

    def cst2_v(kc):
        return cst2.rearrange("p (a b) -> p a b", a=FC)[:, kc, :]

    def h1b2_v(kc):
        return h1b2.rearrange("p (a b) -> p a b", a=FC)[:, kc, :]

    with nc.Block() as blk:
        b_ld, b_sc, b_tn, b_vc = V["ld"], V["sc"], V["tn"], V["vc"]

        @blk.scalar
        def _(e):
            for ch in range(NLN2):
                if ch >= 1:
                    e.wait_ge(S["tn"], b_tn + 3 * (ch - 1) + 2)
                inc(e.activation(cst2.rearrange("p (a b) -> p a b", a=FC),
                                 xo[:, :, ch * LNC:(ch + 1) * LNC],
                                 AF.Copy), "sc")
                e.wait_ge(S["tn"], b_tn + 3 * ch + 1)
                inc(e.activation(cst2, cst2, AF.Square), "sc")
                e.wait_ge(S["vc"], b_vc + 3 * ch + 1)
                inc(e.activation(rwv2["var"], rwv2["var"], AF.Sqrt,
                                 bias=epsb[0:1, 0:1]), "sc")

        @blk.tensor
        def _(e):
            for ch in range(NLN2):
                e.wait_ge(S["sc"], b_sc + 3 * ch + 1)
                if ch >= 1:
                    e.wait_ge(S["vc"], b_vc + 3 * (ch - 1) + 1)
                for kc in range(FC):
                    mm = e.matmul(psC[0:1, 0:LNC], onec[:, :], cst2_v(kc),
                                  start=(kc == 0), stop=(kc == FC - 1))
                inc(mm, "tn")
                e.wait_ge(S["sc"], b_sc + 3 * ch + 2)
                for kc in range(FC):
                    mm = e.matmul(psC[0:1, LNC:2 * LNC], onec[:, :], cst2_v(kc),
                                  start=(kc == 0), stop=(kc == FC - 1))
                inc(mm, "tn")
                e.wait_ge(S["vc"], b_vc + 3 * ch + 2)
                inc(e.matmul(psB[:, 0:2 * LNC], oner[:, :], rAB2,
                             start=True, stop=True), "tn")

        @blk.vector
        def _(e):
            for ch in range(NLN2):
                e.wait_ge(S["tn"], b_tn + 3 * ch + 2)
                e.tensor_scalar(rwv2["mean"], psC[0:1, 0:LNC], 1.0 / D, None, ALU.mult)
                e.tensor_scalar(rwv2["ms"], psC[0:1, LNC:2 * LNC], 1.0 / D, None, ALU.mult)
                e.tensor_tensor(rwv2["var"], rwv2["mean"], rwv2["mean"], ALU.mult)
                inc(e.tensor_tensor(rwv2["var"], rwv2["ms"], rwv2["var"],
                                    ALU.subtract), "vc")
                e.wait_ge(S["sc"], b_sc + 3 * ch + 3)
                e.reciprocal(rwv2["rr"], rwv2["var"])
                e.tensor_tensor(rwv2["bb"], rwv2["mean"], rwv2["rr"], ALU.mult)
                e.tensor_scalar(rwv2["bb"], rwv2["bb"], -1.0, None, ALU.mult)
                e.tensor_copy(rAB2[:, 0:LNC], rwv2["rr"])
                inc(e.tensor_copy(rAB2[:, LNC:2 * LNC], rwv2["bb"]), "vc")
                e.wait_ge(S["tn"], b_tn + 3 * ch + 3)
                for kc in range(FC):
                    e.tensor_tensor(h1b2_v(kc), xo[:, kc, ch * LNC:(ch + 1) * LNC],
                                    psB[:, 0:LNC], ALU.mult)
                    e.tensor_tensor(h1b2_v(kc), h1b2_v(kc), psB[:, LNC:2 * LNC],
                                    ALU.add)
                    last = e.tensor_scalar(
                        h2_r[:, kc, ch * LNC:(ch + 1) * LNC], h1b2_v(kc),
                        mv[:, MVO["cb1"] + kc:MVO["cb1"] + kc + 1],
                        mv[:, MVO["sb"] + kc:MVO["sb"] + kc + 1],
                        ALU.mult, ALU.add)
                inc(last, "vc")

    if "h2" in dumps:
        with nc.Block() as blk:
            @blk.gpsimd
            def _(e):
                inc(e.dma_start(dumps["h2"][:, :, :], h2_r), "gp", 16)
                e.wait_ge(S["gp"], V["gp"])
    INT2_O = 16384
    DWN_O = 49152
    GUB_O2 = [55296 + j * 2048 for j in range(6)]
    GELB2_O = [67584, 68608]

    inter2 = arb(INT2_O, 2 * FC * 2 * 512).rearrange(
        "p (q a h b) -> p q a h b", q=2, a=FC, h=2)
    psTf = psT[:, :].bitcast(F32)

    sd = Sched(V)
    NSUP = 8
    first_sync = [True]
    first_gp = [True]
    gub_free = [None] * 6
    gel_hist = {}
    imul_hist = {}
    dstt_hist = {}
    dmm_hist = {}
    ddma_hist = {}

    def emit_gup(s, nl):
        step = s * FC + nl
        gs = (2 * step) % 6
        us = (2 * step + 1) % 6
        goff = nl * (FC * 128 * 128)

        def mkGD(s=s, goff=goff, gs=gs, first=first_sync[0]):
            def f(e):
                if first:
                    e.wait_ge(S["cc"], V["deqcc"])
                return e.dma_start(
                    arb(GUB_O2[gs], 2048),
                    deq_out[s, goff:goff + 2048 * 128]
                    .rearrange("(p n) -> p n", p=128))
            return f
        gdd = [gub_free[gs]] if gub_free[gs] is not None else []
        gdma = sd.add("sync", gdd, mkGD(), n=16, sem="lb%d" % gs)
        first_sync[0] = False

        def mkUD(s=s, goff=goff, us=us):
            def f(e):
                return e.dma_start(
                    arb(GUB_O2[us], 2048),
                    deq_out[s, NSL + goff:NSL + goff + 2048 * 128]
                    .rearrange("(p n) -> p n", p=128))
            return f
        udd = [gub_free[us]] if gub_free[us] is not None else []
        udma = sd.add("sync", udd, mkUD(), n=16, sem="lb%d" % us)

        A = (nl % 2) * 1024

        def mkGM(gs=gs, A=A):
            def f(e):
                gb = arb(GUB_O2[gs], 2048).rearrange("p (a b) -> p a b", a=FC)
                for hf in range(2):
                    for kc in range(FC):
                        mm = e.matmul(
                            psA[:, A + hf * 512:A + hf * 512 + 512],
                            gb[:, kc, :],
                            h2_r[:, kc, hf * 512:hf * 512 + 512],
                            start=(kc == 0), stop=(kc == FC - 1))
                return mm
            return f
        gdeps = [gdma]
        if (s, nl - 2) in gel_hist:
            gdeps.append(gel_hist[(s, nl - 2)])
        elif nl < 2 and s > 0:
            gdeps.append(gel_hist[(s - 1, FC - 2 + nl)])
        gmm = sd.add("tensor", gdeps, mkGM())
        gub_free[gs] = gmm

        def mkUM(us=us):
            def f(e):
                ub = arb(GUB_O2[us], 2048).rearrange("p (a b) -> p a b", a=FC)
                for hf in range(2):
                    for kc in range(FC):
                        mm = e.matmul(
                            psB[:, hf * 512:hf * 512 + 512],
                            ub[:, kc, :],
                            h2_r[:, kc, hf * 512:hf * 512 + 512],
                            start=(kc == 0), stop=(kc == FC - 1))
                return mm
            return f
        udeps = [udma]
        if (s, nl - 1) in imul_hist:
            udeps.append(imul_hist[(s, nl - 1)])
        elif nl == 0 and s > 0:
            udeps.append(imul_hist[(s - 1, FC - 1)])
        umm = sd.add("tensor", udeps, mkUM())
        gub_free[us] = umm

        def mkGel(nl=nl, A=A):
            def f(e):
                for hf in range(2):
                    t = e.activation(
                        arb(GELB2_O[nl % 2] + hf * 512, 512),
                        psA[:, A + hf * 512:A + hf * 512 + 512],
                        AF.Gelu_apprx_tanh)
                return t
            return f
        geld = [gmm]
        if (s, nl - 2) in imul_hist:
            geld.append(imul_hist[(s, nl - 2)])
        elif nl < 2 and s > 0:
            geld.append(imul_hist[(s - 1, FC - 2 + nl)])
        gel = sd.add("scalar", geld, mkGel())
        gel_hist[(s, nl)] = gel

        def mkIM(s=s, nl=nl):
            def f(e):
                for hf in range(2):
                    t = e.tensor_tensor(
                        inter2[:, s % 2, nl, hf, :],
                        arb(GELB2_O[nl % 2] + hf * 512, 512),
                        psB[:, hf * 512:hf * 512 + 512],
                        ALU.mult)
                return t
            return f
        imdeps = [gel, umm]
        if s >= 2 and (s - 2, FC - 1) in dmm_hist:
            imdeps.append(dmm_hist[(s - 2, FC - 1)])
        im = sd.add("vector", imdeps, mkIM())
        imul_hist[(s, nl)] = im

    def emit_ddma(s, hc):
        def mkDD(s=s, hc=hc, first=first_gp[0]):
            def f(e):
                if first:
                    e.wait_ge(S["cc"], V["deqcc"])
                return e.dma_start(
                    arb(DWN_O + ((s * FC + hc) % 3) * 2048, 2048),
                    deq_out[s, 2 * NSL + hc * (NCK * 128 * 128):
                            2 * NSL + (hc + 1) * (NCK * 128 * 128)]
                    .rearrange("(p n) -> p n", p=128))
            return f
        fl = s * FC + hc
        ddd = [dmm_hist[divmod(fl - 3, FC)]] if fl >= 3 else []
        ddma = sd.add("gpsimd", ddd, mkDD(), n=16, sem="dd%d" % (fl % 3))
        first_gp[0] = False
        ddma_hist[(s, hc)] = ddma

    def emit_dmm(s, hc):
        def mkDM(s=s, hc=hc):
            def f(e):
                dw = arb(DWN_O + ((s * FC + hc) % 3) * 2048, 2048)
                for hf in range(2):
                    dst = psC[:, 0:512] if hf == 0 else psTf[:, 0:512]
                    for ns in range(FC):
                        mm = e.matmul(
                            dst, dw[:, ns * 128:(ns + 1) * 128],
                            inter2[:, s % 2, ns, hf, :],
                            start=(ns == 0), stop=(ns == FC - 1))
                return mm
            return f
        dmdeps = [ddma_hist[(s, hc)], imul_hist[(s, FC - 1)]]
        if (s, hc - 1) in dstt_hist:
            dmdeps.append(dstt_hist[(s, hc - 1)])
        elif hc == 0 and s > 0:
            dmdeps.append(dstt_hist[(s - 1, FC - 1)])
        dmm = sd.add("tensor", dmdeps, mkDM())
        dmm_hist[(s, hc)] = dmm

        def mkDX(hc=hc):
            def f(e):
                for hf in range(2):
                    src = psC[:, 0:512] if hf == 0 else psTf[:, 0:512]
                    xv = xo[:, hc, hf * 512:(hf + 1) * 512]
                    t = e.scalar_tensor_tensor(
                        xv, src,
                        mv[:, MVO["gb"] + hc:MVO["gb"] + hc + 1],
                        xv, ALU.mult, ALU.add)
                return t
            return f
        dx = sd.add("vector", [dmm], mkDX())
        dstt_hist[(s, hc)] = dx

    TILES = NSUP * FC
    for t in range(TILES + FC):
        s, nl = divmod(t, FC)
        if s < NSUP:
            emit_gup(s, nl)
        fl = t - 14
        if 0 <= fl < TILES:
            emit_ddma(*divmod(fl, FC))
        fp = t - 16
        if fp >= 0:
            emit_dmm(*divmod(fp, FC))
    if "inter" in dumps:
        def mkID():
            def f(e):
                return e.dma_start(
                    dumps["inter"][:, :, :],
                    inter2[:, (NSUP - 1) % 2, :, :, :]
                    .rearrange("p a h b -> p (a h b)")
                    .rearrange("p (a b) -> p a b", a=FC))
            return f
        sd.add("gpsimd", [imul_hist[(NSUP - 1, FC - 1)]], mkID(), n=16)
    sd.emit(nc, S_, V, inc)

    with nc.Block() as blk:
        @blk.gpsimd
        def _(e):
            inc(e.dma_start(xout.rearrange("(a p) t -> p a t", p=128),
                            xo[:, :, :]), "gp", 16)
            e.wait_ge(S["gp"], V["gp"])

    stk.close()
    return nc, dumps, V


def _run(inputs, dump=()):
    in_maps = prepare_inputs(inputs)
    nc, dumps, V = build_nc(dump=dump)
    res = run_bass_kernel_spmd(nc, in_maps, core_ids=list(range(NCORE)))
    return res


def kernel(**inputs):
    res = _run(inputs, dump=DUMP)
    outs = [r["xout"] for r in res.results]
    full = np.concatenate([o.T for o in outs], axis=0)
    return np.ascontiguousarray(full.reshape(1, 8192, D).astype(np.float32))


# version 11 (final)
# speedup vs baseline: 1.1065x; 1.0006x over previous
import contextlib
import numpy as np
import ml_dtypes

import concourse.bass as bass
import concourse.mybir as mybir
from concourse.bass_utils import run_bass_kernel_spmd

F32 = mybir.dt.float32
BF16 = mybir.dt.bfloat16
AF = mybir.ActivationFunctionType
ALU = mybir.AluOpType
AX = mybir.AxisListType

NCORE = 8
D = 2048
FC = 16
T = 1024
HB = 512
TH = T + HB
H = 16
DH = 128
GH = 4
NG = H // GH
R = 16
MLP = 16384
NCK = 16
LNC = 256
NLN1 = TH // LNC
NLN2 = T // LNC

BF = ml_dtypes.bfloat16
DUMP = ()
FFN_NO_DOWN = False
FFN_NO_GU = False
FFN_DOWN_SCRATCH = False
FFN_DOWN_SER = False



def _rope_tables():
    half = DH // 2
    theta = 1.0 / 10000.0 ** (np.arange(half, dtype=np.float64) / half)
    pos = np.arange(-HB, 8192, dtype=np.float64)
    ang = pos[:, None] * theta[None, :]
    cos = np.cos(ang).astype(np.float32)
    sin = np.sin(ang).astype(np.float32)
    cos_p = np.empty((128, HB + 8192), np.float32)
    sin_p = np.empty((128, HB + 8192), np.float32)
    cos_p[0::2] = cos.T
    cos_p[1::2] = cos.T
    sin_p[0::2] = -sin.T
    sin_p[1::2] = sin.T
    return cos_p, sin_p


def _perm_idx():
    idx = np.empty(D, np.int64)
    for h in range(H):
        for j in range(64):
            idx[h * 128 + 2 * j] = h * 128 + j
            idx[h * 128 + 2 * j + 1] = h * 128 + 64 + j
    return idx


def _tile_wT(wT):
    return np.ascontiguousarray(wT.reshape(D, FC, 128).transpose(1, 0, 2))


def _masks_for_core(core):
    NEG = np.float32(-1e9)
    qi = np.arange(128)[:, None]
    kj = np.arange(128)[None, :]
    std = np.zeros((2, 128, 128), np.float32)
    std[0][(kj + 0) < (qi + 1)] = NEG
    std[1][(kj + 512) > (qi + 512)] = NEG
    early = np.zeros((4, 5, 128, 128), np.float32)
    for qt in range(4):
        gq0 = core * T + qt * 128
        for s in range(5):
            m = np.zeros((128, 128), np.float32)
            kglob = gq0 - 512 + s * 128 + kj
            qglob = gq0 + qi
            bad = (kglob > qglob) | (kglob < qglob - 511) | (kglob < 0)
            m[bad] = NEG
            early[qt, s] = m
    return early, std


def _unpack_nvfp4(Wp):
    b = np.asarray(Wp).astype(np.uint8)
    lo = b & 15
    hi = (b >> 4) & 15
    return np.stack([lo, hi], axis=-1).reshape(b.shape[0], -1)


def prepare_inputs(d):
    x = np.asarray(d["L0_attn_x"], np.float32)[0]
    perm = _perm_idx()

    wqT = np.ascontiguousarray(np.asarray(d["L0_attn_wq"], np.float32).T)[:, perm]
    wkT = np.ascontiguousarray(np.asarray(d["L0_attn_wk"], np.float32).T)[:, perm]
    wvT = np.ascontiguousarray(np.asarray(d["L0_attn_wv"], np.float32).T)
    woT = np.ascontiguousarray(np.asarray(d["L0_attn_wo"], np.float32).T)
    wq_t, wk_t, wv_t, wo_t = map(_tile_wT, (wqT, wkT, wvT, woT))

    lAm = np.stack([
        np.asarray(d["L0_attn_qA"], np.float32).T.reshape(FC, 128, R),
        np.asarray(d["L0_attn_kA"], np.float32).T.reshape(FC, 128, R),
        np.asarray(d["L0_attn_vA"], np.float32).T.reshape(FC, 128, R),
        np.asarray(d["L0_attn_oA"], np.float32).T.reshape(FC, 128, R),
    ], axis=0)
    lqB = np.ascontiguousarray(np.asarray(d["L0_attn_qB"], np.float32).T[:, perm]).astype(BF)
    lkB = np.ascontiguousarray(np.asarray(d["L0_attn_kB"], np.float32).T[:, perm]).astype(BF)
    lvB = np.ascontiguousarray(np.asarray(d["L0_attn_vB"], np.float32).T).astype(BF)
    loB = np.ascontiguousarray(np.asarray(d["L0_attn_oB"], np.float32).T).astype(BF)

    normw = np.ascontiguousarray(np.asarray(d["L0_attn_norm_w"], np.float32).reshape(FC, 128).T)
    cvec = np.ascontiguousarray(np.asarray(d["L0_c"], np.float32).reshape(FC, 128).T)
    wmod = np.asarray(d["L0_wmod"], np.float32)

    cos_p, sin_p = _rope_tables()
    SC = np.float32(1.0 / np.sqrt(DH))

    gcT = _unpack_nvfp4(d["L0_ffn_gate_W_packed"]).T
    ucT = _unpack_nvfp4(d["L0_ffn_up_W_packed"]).T
    dcT = _unpack_nvfp4(d["L0_ffn_down_W_packed"]).T
    gsT = np.ascontiguousarray(np.asarray(d["L0_ffn_gate_scales"], np.float32).T)
    usT = np.ascontiguousarray(np.asarray(d["L0_ffn_up_scales"], np.float32).T)
    dsT = np.ascontiguousarray(np.asarray(d["L0_ffn_down_scales"], np.float32).T)

    selm = np.zeros((4, 128), np.float32)
    for p in range(128):
        selm[p >> 5, p] = 1.0

    shared = {
        "wq": wq_t, "wk": wk_t, "wv": wv_t, "wo": wo_t,
        "lA": np.ascontiguousarray(lAm),
        "lqB": lqB, "lkB": lkB, "lvB": lvB, "loB": loB,
        "normw": normw, "cvec": cvec,
        "ident": np.eye(128, dtype=np.float32).astype(BF),
        "onescol": np.ones((128, 1), np.float32).astype(BF),
        "onesrow": np.ones((1, 128), np.float32).astype(BF),
        "one11": np.ones((1, 1), np.float32),
        "epsv": np.array([[1e-5, 1e-6]], np.float32),
        "sel": selm.astype(BF),
    }

    in_maps = []
    for c in range(NCORE):
        s0 = c * T
        xs = np.zeros((TH, D), np.float32)
        lo = max(0, s0 - HB)
        if s0 > lo:
            xs[HB - (s0 - lo):HB] = x[lo:s0]
        xs[HB:] = x[s0:s0 + T]
        xTc = np.ascontiguousarray(xs.T)

        early, std = _masks_for_core(c)
        roff = s0
        im = dict(shared)
        im.update({
            "xT": xTc,
            "wmslab": np.ascontiguousarray(
                wmod[:, c * 1536:(c + 1) * 1536].reshape(FC, 128, 1536)),
            "ropeqc": (np.ascontiguousarray(cos_p[:, roff + HB:roff + HB + T]) * SC).astype(BF),
            "ropeqs": (np.ascontiguousarray(sin_p[:, roff + HB:roff + HB + T]) * SC).astype(BF),
            "ropekc": np.ascontiguousarray(cos_p[:, roff:roff + TH]).astype(BF),
            "ropeks": np.ascontiguousarray(sin_p[:, roff:roff + TH]).astype(BF),
            "maskE": early, "maskS": std,
        })
        n0 = c * NCK * 128
        im["gcodes"] = np.ascontiguousarray(
            gcT[:, n0:n0 + NCK * 128].reshape(FC, 128, NCK, 128)
            .transpose(2, 1, 0, 3)).astype(BF)
        im["ucodes"] = np.ascontiguousarray(
            ucT[:, n0:n0 + NCK * 128].reshape(FC, 128, NCK, 128)
            .transpose(2, 1, 0, 3)).astype(BF)
        im["dcodes"] = np.ascontiguousarray(
            dcT[n0:n0 + NCK * 128].reshape(NCK, 128, D)).astype(BF)
        im["gscale"] = np.ascontiguousarray(
            gsT[:, n0:n0 + NCK * 128].reshape(FC, 4, NCK, 128)
            .transpose(2, 1, 0, 3).reshape(NCK, 4, D))
        im["uscale"] = np.ascontiguousarray(
            usT[:, n0:n0 + NCK * 128].reshape(FC, 4, NCK, 128)
            .transpose(2, 1, 0, 3).reshape(NCK, 4, D))
        im["dscale"] = np.ascontiguousarray(
            dsT[c * 64:(c + 1) * 64].reshape(NCK, 4, D))
        in_maps.append(im)
    return in_maps




class Sched:
    ENG_SEM = {"sync": "ld", "tensor": "tn", "vector": "vc",
               "scalar": "sc", "gpsimd": "gp"}

    def __init__(self, V):
        self.steps = []
        self.base = {v: V.get(v, 0) for v in self.ENG_SEM.values()}

    def add(self, eng, deps, fn, n=1, sem=None):
        sid = len(self.steps)
        self.steps.append((eng, list(deps), fn, n, sem))
        return sid

    def emit(self, nc, S, V, inc):
        cnt = {}
        after = {}
        semof = {}
        for sid, (eng, deps, fn, n, sem) in enumerate(self.steps):
            sname = sem or self.ENG_SEM[eng]
            if sname not in self.base:
                self.base[sname] = V.get(sname, 0)
            cnt[sname] = cnt.get(sname, 0) + n
            after[sid] = (eng, sname, cnt[sname])
        per_eng = {e: [] for e in self.ENG_SEM}
        for sid, (eng, deps, fn, n, sem) in enumerate(self.steps):
            per_eng[eng].append(sid)

        sched = self

        with nc.Block() as blk:
            def make(eng_name):
                def body(e):
                    for sid in per_eng[eng_name]:
                        _, deps, fn, n, sem = sched.steps[sid]
                        sname = sem or sched.ENG_SEM[eng_name]
                        need = {}
                        for dp in deps:
                            deng, dsem, dcnt = after[dp]
                            if deng == eng_name and dsem == sched.ENG_SEM[deng]:
                                continue
                            if deng == eng_name and dsem != sched.ENG_SEM[deng]:
                                pass
                            need[dsem] = max(need.get(dsem, 0),
                                             sched.base[dsem] + dcnt)
                        for s2, val in need.items():
                            e.wait_ge(S[s2], val)
                        last = fn(e)
                        inc(last, sname, n)
                return body
            if per_eng["sync"]:
                blk.sync(make("sync"))
            if per_eng["tensor"]:
                blk.tensor(make("tensor"))
            if per_eng["scalar"]:
                blk.scalar(make("scalar"))
            if per_eng["vector"]:
                blk.vector(make("vector"))
            if per_eng["gpsimd"]:
                blk.gpsimd(make("gpsimd"))


def build_nc(dump=()):
    nc = bass.Bass("TRN2", target_bir_lowering=False)

    def din(name, shape, dtype):
        return nc.dram_tensor(name, shape, dtype, kind="ExternalInput")

    xT = din("xT", [D, TH], F32)
    wW = {"q": din("wq", [FC, D, 128], F32), "k": din("wk", [FC, D, 128], F32),
          "v": din("wv", [FC, D, 128], F32), "o": din("wo", [FC, D, 128], F32)}
    lA = din("lA", [4, FC, 128, R], F32)
    lB = {"q": din("lqB", [R, D], BF16), "k": din("lkB", [R, D], BF16),
          "v": din("lvB", [R, D], BF16), "o": din("loB", [R, D], BF16)}
    normw = din("normw", [128, FC], F32)
    cvec = din("cvec", [128, FC], F32)
    wmslab = din("wmslab", [FC, 128, 1536], F32)
    ropeqc = din("ropeqc", [128, T], BF16)
    ropeqs = din("ropeqs", [128, T], BF16)
    ropekc = din("ropekc", [128, TH], BF16)
    ropeks = din("ropeks", [128, TH], BF16)
    maskE = din("maskE", [4, 5, 128, 128], F32)
    maskS = din("maskS", [2, 128, 128], F32)
    identD = din("ident", [128, 128], BF16)
    onescolD = din("onescol", [128, 1], BF16)
    onesrowD = din("onesrow", [1, 128], BF16)
    one11 = din("one11", [1, 1], F32)
    epsD = din("epsv", [1, 2], F32)
    selD = din("sel", [4, 128], BF16)
    codesD = {"g": din("gcodes", [NCK, 128, FC, 128], BF16),
              "u": din("ucodes", [NCK, 128, FC, 128], BF16),
              "d": din("dcodes", [NCK, 128, D], BF16)}
    scaleD = {"g": din("gscale", [NCK, 4, D], F32),
              "u": din("uscale", [NCK, 4, D], F32),
              "d": din("dscale", [NCK, 4, D], F32)}

    xout = nc.dram_tensor("xout", [D, T], F32, kind="ExternalOutput")
    dumps = {}
    if "hn" in dump:
        dumps["hn"] = nc.dram_tensor("d_hn", [128, FC, TH], BF16, kind="ExternalOutput")
    if "xp" in dump:
        dumps["xp"] = nc.dram_tensor("d_xp", [128, FC, T], F32, kind="ExternalOutput")
    if "mv" in dump:
        dumps["mv"] = nc.dram_tensor("d_mv", [128, 96], F32, kind="ExternalOutput")
    if "qkv" in dump:
        dumps["q"] = nc.dram_tensor("d_q", [128, GH, T], BF16, kind="ExternalOutput")
        dumps["k"] = nc.dram_tensor("d_k", [128, GH, TH], BF16, kind="ExternalOutput")
        dumps["v"] = nc.dram_tensor("d_v", [128, TH // 128, GH * 128], BF16, kind="ExternalOutput")
    if "attn" in dump:
        dumps["attn"] = nc.dram_tensor("d_attn", [128, GH, T], BF16, kind="ExternalOutput")
    if "h2" in dump:
        dumps["h2"] = nc.dram_tensor("d_h2", [128, FC, T], BF16, kind="ExternalOutput")
    if "arena" in dump:
        dumps["arena"] = nc.dram_tensor("d_arena", [128, 70912], BF16, kind="ExternalOutput")
    if "inter" in dump:
        dumps["inter"] = nc.dram_tensor("d_inter", [128, FC, 512], BF16, kind="ExternalOutput")
    if "wdq" in dump:
        dumps["wdq"] = nc.dram_tensor("d_wdq", [3 * NCK * FC * 128 * 128], BF16,
                                      kind="ExternalOutput")

    NSL = NCK * FC * 128 * 128
    deq_in = nc.dram_tensor("deq_in", [3 * NSL], BF16)
    deq_out = nc.dram_tensor("deq_out", [NCORE, 3 * NSL], BF16, addr_space="Shared")
    mod_b = nc.dram_tensor("mod_b", [1, 1536], F32)
    mod_g = nc.dram_tensor("mod_g", [NCORE, 1536], F32)

    stk = contextlib.ExitStack()
    ent = stk.enter_context
    S = {k: ent(nc.semaphore(k)) for k in
         ["ld", "st", "cc", "tn", "vc", "sc", "gp",
          "lb0", "lb1", "lb2", "lb3", "lb4", "lb5", "ldd", "lws", "lws1", "lmk",
          "lc0", "lc1", "lsf", "lwm", "lmd", "lhx", "st0", "st1",
          "dd0", "dd1", "dd2"]}
    V = {k: 0 for k in S}

    def inc(instr, k, n=1):
        instr.then_inc(S[k], n)
        V[k] += n
        return V[k]

    xo = ent(nc.sbuf_tensor("xo", [128, FC, T], F32))
    mskSb = ent(nc.sbuf_tensor("mskSb", [128, 2, 128], F32))
    mskEb = ent(nc.sbuf_tensor("mskEb", [128, 5, 128], F32))
    idn = ent(nc.sbuf_tensor("idn", [128, 128], BF16))
    diag = ent(nc.sbuf_tensor("diag", [128, 128], BF16))
    mv = ent(nc.sbuf_tensor("mv", [128, 96], F32))
    nwsb = ent(nc.sbuf_tensor("nwsb", [128, FC], F32))
    csb = ent(nc.sbuf_tensor("csb", [128, FC], F32))
    csbb = ent(nc.sbuf_tensor("csbb", [128, FC], BF16))
    onec = ent(nc.sbuf_tensor("onec", [128, 1], BF16))
    oner = ent(nc.sbuf_tensor("oner", [1, 128], BF16))
    one1 = ent(nc.sbuf_tensor("one1", [1, 1], F32))
    epsb = ent(nc.sbuf_tensor("epsb", [1, 2], F32))
    selb = ent(nc.sbuf_tensor("selb", [4, 128], BF16))
    gcl = ent(nc.sbuf_tensor("gcl", [128, FC], F32))
    racc1 = ent(nc.sbuf_tensor("racc1", [128, 1], F32))
    racc2 = ent(nc.sbuf_tensor("racc2", [128, 1], F32))
    rmx1 = ent(nc.sbuf_tensor("rmx1", [128, 1], F32))
    rmx2 = ent(nc.sbuf_tensor("rmx2", [128, 1], F32))
    rsum = ent(nc.sbuf_tensor("rsum", [128, 1], F32))

    ARN = 70912
    arena = ent(nc.sbuf_tensor("arena", [128, ARN], BF16))

    def arb(off, n):
        return arena[:, off:off + n]

    def arf(off, n):
        return arena[:, off:off + 2 * n].bitcast(F32)

    MODROW_O = 0
    WMF_O = 3072
    WMB_O = 6144
    SCF_O = 7680
    SCB_O = 11776
    DQ_C = [13824, 15872]
    DQ_T = [17920, 19968, 22016, 24064]
    DQ_W = [26112, 28160]
    XST_O = 0
    HN_O = 16384
    H1_O = 40960
    CST_O = 49152
    RW_O = 53248
    RAB_O = 55808
    RQC_O, RQS_O, RKC_O, RKS_O = 0, 1024, 2048, 3584
    LAS_O = 5120
    LBS_O = {"q": 7168, "k": 9216, "v": 11264, "o": 13312}
    LASB_O = 15360
    TMPL_O = 40960
    TMPO_O = 42496
    QG_O = 44544
    KG_O = 48640
    VG_O = 54784
    ATG_O = 60928
    WS_O = 65024
    WB_O = [67072]
    SCR_O = 69120
    EXP_O = 69120
    PT_O = 69760
    VT_O = 70400
    QR_O = [69120, 69632, 70144]
    H2_O = 16384
    INT_O = 32768
    H1B2_O = 49152
    CST2_O = 57344
    RW2_O = 61440
    RAB2_O = 64000
    FW_O = [57344, 59392, 61440, 63488, 65536, 67584]
    GEL_O = 69632

    psA = ent(nc.psum_tensor("psA", [128, 2048], F32))
    psB = ent(nc.psum_tensor("psB", [128, 1024], F32))
    psC = ent(nc.psum_tensor("psC", [128, 512], F32))
    psT = ent(nc.psum_tensor("psT", [128, 1024], BF16))

    with nc.Block() as blk:
        @blk.sync
        def _(e):
            for dst, src in [
                (csb[:, :], cvec[:, :]), (nwsb[:, :], normw[:, :]),
                (one1[:, :], one11[:, :]), (selb[:, :], selD[:, :]),
                (epsb[:, :], epsD[:, :]),
                (idn[:, :], identD[:, :]),
                (onec[:, :], onescolD[:, :]), (oner[:, :], onesrowD[:, :]),
                (mskSb[:, 0, :], maskS[0]), (mskSb[:, 1, :], maskS[1]),
            ]:
                inc(e.dma_start(dst, src), "ld", 16)
            inc(e.dma_start(
                xo[:, :, :],
                xT[:, HB:TH].rearrange("(a p) t -> p a t", p=128)), "ld", 16)
            V["A1"] = V["ld"]

        @blk.scalar
        def _(e):
            e.wait_ge(S["ld"], V["A1"])
            inc(e.activation(csb[:, :], csb[:, :], AF.Silu), "sc")

    wm_f = arf(WMF_O, 1536)
    with nc.Block() as blk:
        b_ld, b_sc, b_tn = V["ld"], V["sc"], V["tn"]

        @blk.sync
        def _(e):
            for kc in range(FC):
                if kc >= 1:
                    e.wait_ge(S["tn"], b_tn + kc)
                inc(e.dma_start(wm_f[:, :], wmslab[kc]), "lwm", 16)

        @blk.tensor
        def _(e):
            e.wait_ge(S["sc"], b_sc)
            for kc in range(FC):
                e.wait_ge(S["lwm"], 16 * (kc + 1))
                for cc in range(3):
                    mm = e.matmul(psA[0:1, cc * 512:(cc + 1) * 512],
                                  csb[:, kc:kc + 1],
                                  wm_f[:, cc * 512:(cc + 1) * 512],
                                  start=(kc == 0), stop=(kc == FC - 1))
                inc(mm, "tn")

    modrow = arf(MODROW_O, 1536)
    with nc.Block() as blk:
        @blk.vector
        def _(e):
            inc(e.tensor_copy(modrow[0:1, :], psA[0:1, 0:1536]), "vc")
            V["modev"] = V["vc"]

        @blk.gpsimd
        def _(e):
            e.wait_ge(S["vc"], V["modev"])
            inc(e.dma_start(mod_b[:, :], modrow[0:1, :]), "gp", 16)
            e.wait_ge(S["gp"], V["gp"])
            inc(e.collective_compute(
                "AllGather", ALU.bypass, replica_groups=[list(range(NCORE))],
                ins=[mod_b.ap().opt()], outs=[mod_g.ap().opt()]), "cc")
            V["modcc"] = V["cc"]

    with nc.Block() as blk:
        b_ld, b_tn = V["ld"], V["tn"]

        @blk.sync
        def _(e):
            e.wait_ge(S["cc"], V["modcc"])
            for s in range(NCORE):
                if s >= 1:
                    e.wait_ge(S["tn"], b_tn + s)
                inc(e.dma_start(modrow[0:1, :], mod_g[s:s + 1, :]), "lmd", 16)

        @blk.tensor
        def _(e):
            for s in range(NCORE):
                e.wait_ge(S["lmd"], 16 * (s + 1))
                for j in range(12):
                    mm = e.matmul(psB[:, s * 12 + j:s * 12 + j + 1],
                                  modrow[0:1, j * 128:(j + 1) * 128],
                                  one1[:, :], start=True, stop=True)
                inc(mm, "tn")
            V["mvT"] = V["tn"]

        @blk.vector
        def _(e):
            e.wait_ge(S["tn"], V["mvT"])
            e.tensor_copy(mv[:, :], psB[:, 0:96])
            e.drain()
            for col in (32, 80):
                g = mv[:, col:col + FC]
                t0 = gcl[:, :]
                e.tensor_scalar(t0, g, -1.0, None, ALU.mult)
                e.drain()
                e.tensor_tensor(t0, g, t0, ALU.max)
                e.drain()
                e.tensor_scalar(t0, t0, 0.25, None, ALU.max)
                e.tensor_scalar(g, g, 0.0, 2.0, ALU.is_ge, ALU.mult)
                e.drain()
                e.tensor_scalar(g, g, -1.0, None, ALU.add)
                e.drain()
                e.tensor_tensor(g, t0, g, ALU.mult)
                e.drain()
            e.tensor_scalar(mv[:, 16:32], mv[:, 16:32], 1.0, None, ALU.add)
            inc(e.tensor_scalar(mv[:, 64:80], mv[:, 64:80], 1.0, None, ALU.add),
                "vc")
            V["mvdone"] = V["vc"]

    MVO = {"sa": 0, "ca1": 16, "ga": 32, "sb": 48, "cb1": 64, "gb": 80}

    scf = arf(SCF_O, 2048)
    scb = arb(SCB_O, 2048)
    NSLICES = 3 * NCK
    with nc.Block() as blk:
        b_ld, b_vc, b_tn, b_gp = V["ld"], V["vc"], V["tn"], V["gp"]

        @blk.sync
        def _(e):
            for i in range(NSLICES):
                mat = "gud"[i // NCK]
                ncl = i % NCK
                buf = i % 2
                if i >= 2:
                    e.wait_ge(S["vc"], b_vc + 2 * (i - 1))
                cb = arb(DQ_C[buf], 2048)
                if mat == "d":
                    inc(e.dma_start(cb, codesD["d"][ncl]), "lc%d" % buf, 16)
                else:
                    inc(e.dma_start(cb.rearrange("p (a b) -> p a b", a=FC),
                                    codesD[mat][ncl]), "lc%d" % buf, 16)
                if i >= 1:
                    e.wait_ge(S["vc"], b_vc + 2 * (i - 1) + 1)
                inc(e.dma_start(scf[0:4, 0:D], scaleD[mat][ncl]), "lsf", 16)

        @blk.vector
        def _(e):
            for i in range(NSLICES):
                buf = i % 2
                cb = arb(DQ_C[buf], 2048)
                tA = arb(DQ_T[0], 2048)
                tB = arb(DQ_T[1], 2048)
                tC = arb(DQ_T[2], 2048)
                tD = arb(DQ_T[3], 2048)
                wout = arb(DQ_W[buf], 2048)
                e.wait_ge(S["lc%d" % buf], 16 * (i // 2 + 1))
                e.wait_ge(S["lsf"], 16 * (i + 1))
                if i >= 1:
                    e.wait_ge(S["tn"], b_tn + i)
                inc(e.tensor_copy(scb[0:4, 0:D], scf[0:4, 0:D]), "vc")
                e.wait_ge(S["tn"], b_tn + i + 1)
                e.tensor_scalar(tA, cb, 8.0, -8.0, ALU.is_ge, ALU.mult)
                e.tensor_tensor(tB, cb, tA, ALU.add)
                e.tensor_scalar(tC, tA, 0.125, 0.5, ALU.mult, ALU.add)
                e.tensor_scalar(tD, tB, 2.0, -4.0, ALU.mult, ALU.add)
                e.tensor_tensor(tD, tB, tD, ALU.max)
                e.tensor_scalar(tB, tB, 4.0, -16.0, ALU.mult, ALU.add)
                e.tensor_tensor(tB, tD, tB, ALU.max)
                e.tensor_tensor(tB, tB, tC, ALU.mult)
                if i >= 2:
                    e.wait_ge(S["st%d" % buf], 16 * (i // 2))
                inc(e.tensor_tensor(wout, tB, psA[:, 0:2048], ALU.mult), "vc")

        @blk.tensor
        def _(e):
            for i in range(NSLICES):
                e.wait_ge(S["vc"], b_vc + 2 * i + 1)
                for kc in range(FC):
                    mm = e.matmul(psA[:, kc * 128:(kc + 1) * 128], selb[:, :],
                                  scb[0:4, kc * 128:(kc + 1) * 128],
                                  start=True, stop=True)
                inc(mm, "tn")

        @blk.gpsimd
        def _(e):
            for i in range(NSLICES):
                ncl = i % NCK
                buf = i % 2
                wout = arb(DQ_W[buf], 2048)
                e.wait_ge(S["vc"], b_vc + 2 * i + 2)
                if i // NCK == 2:
                    dreg = deq_in[2 * NSL:3 * NSL].rearrange(
                        "(a p n c) -> a p n c", a=FC, p=128, n=NCK)
                    inc(e.dma_start(
                        dreg[:, :, ncl, :].transpose([1, 0, 2]),
                        wout.rearrange("p (a c) -> p a c", a=FC)),
                        "st%d" % buf, 16)
                else:
                    off = (i // NCK) * NSL + ncl * (FC * 128 * 128)
                    inc(e.dma_start(
                        deq_in[off:off + 2048 * 128].rearrange("(p n) -> p n", p=128),
                        wout), "st%d" % buf, 16)
            e.wait_ge(S["st0"], 16 * (NSLICES // 2))
            e.wait_ge(S["st1"], 16 * (NSLICES // 2))
            inc(e.collective_compute(
                "AllGather", ALU.bypass, replica_groups=[list(range(NCORE))],
                ins=[deq_in.ap().opt()], outs=[deq_out.ap().opt()]), "cc")
            V["deqcc"] = V["cc"]

    if "wdq" in dumps:
        with nc.Block() as blk:
            @blk.gpsimd
            def _(e):
                inc(e.dma_start(dumps["wdq"][:], deq_in[:]), "gp", 16)
                e.wait_ge(S["gp"], V["gp"])

    def emit_ln(blk_nc, xsrc_fn, nchunks, hn_view_fn, mvo_c, mvo_s, mvo_g,
                with_rms, h1_off, cst_off, rw_off, rab_off):
        pass

    cst = arb(CST_O, FC * LNC)
    h1b = arf(H1_O, FC * LNC)
    hn = arb(HN_O, FC * TH)
    rwv = {}
    for j, k in enumerate(["mean", "ms", "var", "rr", "bb"]):
        rwv[k] = arf(RW_O + 2 * j * LNC, LNC)[0:1, :]
    rAB = arb(RAB_O, 2 * LNC)[0:1, :]

    def hn_v(kc, t0):
        return hn.rearrange("p (a b) -> p a b", a=FC)[:, kc, t0:t0 + LNC]

    def xsrc1(ch, kc):
        if ch < 2:
            return arf(XST_O, FC * 512).rearrange("p (a b) -> p a b", a=FC)[
                :, kc, ch * LNC:(ch + 1) * LNC]
        return xo[:, kc, (ch - 2) * LNC:(ch - 1) * LNC]

    def cst_v(kc):
        return cst.rearrange("p (a b) -> p a b", a=FC)[:, kc, :]

    def h1_v(kc):
        return h1b.rearrange("p (a b) -> p a b", a=FC)[:, kc, :]

    with nc.Block() as blk:
        b_ld, b_sc, b_tn, b_vc = V["ld"], V["sc"], V["tn"], V["vc"]

        @blk.sync
        def _(e):
            inc(e.dma_start(
                arf(XST_O, FC * 512).rearrange("p (a b) -> p a b", a=FC),
                xT[:, 0:HB].rearrange("(a p) t -> p a t", p=128)), "lhx", 16)

        @blk.scalar
        def _(e):
            for ch in range(NLN1):
                e.wait_ge(S["lhx"], 16)
                if ch >= 1:
                    e.wait_ge(S["tn"], b_tn + 5 * (ch - 1) + 4)
                xin = (xo[:, :, (ch - 2) * LNC:(ch - 1) * LNC] if ch >= 2 else
                       arf(XST_O, FC * 512).rearrange("p (a b) -> p a b", a=FC)
                       [:, :, ch * LNC:(ch + 1) * LNC])
                inc(e.activation(cst.rearrange("p (a b) -> p a b", a=FC),
                                 xin, AF.Copy), "sc")
                e.wait_ge(S["tn"], b_tn + 5 * ch + 1)
                inc(e.activation(cst, cst, AF.Square), "sc")
                e.wait_ge(S["vc"], b_vc + 6 * ch + 1)
                inc(e.activation(rwv["var"], rwv["var"], AF.Sqrt,
                                 bias=epsb[0:1, 0:1]), "sc")
                e.wait_ge(S["vc"], b_vc + 6 * ch + 3)
                inc(e.activation(cst.rearrange("p (a b) -> p a b", a=FC),
                                 h1b.rearrange("p (a b) -> p a b", a=FC),
                                 AF.Square), "sc")
                e.wait_ge(S["vc"], b_vc + 6 * ch + 4)
                inc(e.activation(rwv["ms"], rwv["ms"], AF.Sqrt,
                                 bias=epsb[0:1, 1:2]), "sc")

        @blk.tensor
        def _(e):
            for ch in range(NLN1):
                e.wait_ge(S["sc"], b_sc + 5 * ch + 1)
                if ch >= 1:
                    e.wait_ge(S["vc"], b_vc + 6 * (ch - 1) + 4)
                for kc in range(FC):
                    mm = e.matmul(psC[0:1, 0:LNC], onec[:, :], cst_v(kc),
                                  start=(kc == 0), stop=(kc == FC - 1))
                inc(mm, "tn")
                e.wait_ge(S["sc"], b_sc + 5 * ch + 2)
                for kc in range(FC):
                    mm = e.matmul(psC[0:1, LNC:2 * LNC], onec[:, :], cst_v(kc),
                                  start=(kc == 0), stop=(kc == FC - 1))
                inc(mm, "tn")
                e.wait_ge(S["vc"], b_vc + 6 * ch + 2)
                inc(e.matmul(psB[:, 0:2 * LNC], oner[:, :], rAB,
                             start=True, stop=True), "tn")
                e.wait_ge(S["sc"], b_sc + 5 * ch + 4)
                for kc in range(FC):
                    mm = e.matmul(psC[0:1, LNC:2 * LNC], onec[:, :], cst_v(kc),
                                  start=(kc == 0), stop=(kc == FC - 1))
                inc(mm, "tn")
                e.wait_ge(S["vc"], b_vc + 6 * ch + 5)
                inc(e.matmul(psB[:, 512:512 + LNC], oner[:, :],
                             rAB[:, 0:LNC], start=True, stop=True), "tn")

        @blk.vector
        def _(e):
            for ch in range(NLN1):
                own = ch >= 2
                t0 = ch * LNC
                e.wait_ge(S["tn"], b_tn + 5 * ch + 2)
                e.tensor_scalar(rwv["mean"], psC[0:1, 0:LNC], 1.0 / D, None, ALU.mult)
                e.tensor_scalar(rwv["ms"], psC[0:1, LNC:2 * LNC], 1.0 / D, None, ALU.mult)
                e.tensor_tensor(rwv["var"], rwv["mean"], rwv["mean"], ALU.mult)
                inc(e.tensor_tensor(rwv["var"], rwv["ms"], rwv["var"],
                                    ALU.subtract), "vc")
                e.wait_ge(S["sc"], b_sc + 5 * ch + 3)
                e.reciprocal(rwv["rr"], rwv["var"])
                e.tensor_tensor(rwv["bb"], rwv["mean"], rwv["rr"], ALU.mult)
                e.tensor_scalar(rwv["bb"], rwv["bb"], -1.0, None, ALU.mult)
                e.tensor_copy(rAB[:, 0:LNC], rwv["rr"])
                inc(e.tensor_copy(rAB[:, LNC:2 * LNC], rwv["bb"]), "vc")
                e.wait_ge(S["tn"], b_tn + 5 * ch + 3)
                for kc in range(FC):
                    e.tensor_tensor(h1_v(kc), xsrc1(ch, kc), psB[:, 0:LNC], ALU.mult)
                    e.tensor_tensor(h1_v(kc), h1_v(kc), psB[:, LNC:2 * LNC], ALU.add)
                    e.tensor_scalar(h1_v(kc), h1_v(kc),
                                    mv[:, MVO["ca1"] + kc:MVO["ca1"] + kc + 1],
                                    mv[:, MVO["sa"] + kc:MVO["sa"] + kc + 1],
                                    ALU.mult, ALU.add)
                if own:
                    for kc in range(FC):
                        xv = xo[:, kc, (ch - 2) * LNC:(ch - 1) * LNC]
                        last = e.scalar_tensor_tensor(
                            xv, h1_v(kc),
                            mv[:, MVO["ga"] + kc:MVO["ga"] + kc + 1],
                            xv, ALU.mult, ALU.add)
                    inc(last, "vc")
                else:
                    inc(e.tensor_copy(rmx1[:, :], rmx1[:, :]), "vc")
                e.wait_ge(S["tn"], b_tn + 5 * ch + 4)
                inc(e.tensor_scalar(rwv["ms"], psC[0:1, LNC:2 * LNC],
                                    1.0 / D, None, ALU.mult), "vc")
                e.wait_ge(S["sc"], b_sc + 5 * ch + 5)
                e.reciprocal(rwv["rr"], rwv["ms"])
                inc(e.tensor_copy(rAB[:, 0:LNC], rwv["rr"]), "vc")
                e.wait_ge(S["tn"], b_tn + 5 * ch + 5)
                for kc in range(FC):
                    last = e.scalar_tensor_tensor(
                        hn_v(kc, t0), h1_v(kc),
                        nwsb[:, kc:kc + 1], psB[:, 512:512 + LNC],
                        ALU.mult, ALU.mult)
                inc(last, "vc")

    if "hn" in dumps or "xp" in dumps or "mv" in dumps:
        with nc.Block() as blk:
            @blk.gpsimd
            def _(e):
                if "hn" in dumps:
                    inc(e.dma_start(
                        dumps["hn"].ap().rearrange("p a b -> p (a b)"), hn),
                        "gp", 16)
                if "xp" in dumps:
                    inc(e.dma_start(dumps["xp"][:, :, :], xo[:, :, :]), "gp", 16)
                if "mv" in dumps:
                    inc(e.dma_start(dumps["mv"][:, :], mv[:, :]), "gp", 16)
                e.wait_ge(S["gp"], V["gp"])

    SHUF = []
    for i in range(32):
        SHUF.append(i ^ 1)

    rqcv = arb(RQC_O, T)
    rqsv = arb(RQS_O, T)
    rkcv = arb(RKC_O, TH)
    rksv = arb(RKS_O, TH)
    lasv = arf(LAS_O, 1024)
    lasb = arb(LASB_O, 1024)
    lbsv = {k: arb(LBS_O[k], 2048) for k in "qkvo"}
    tmplv = arb(TMPL_O, 1536)
    tmpov = arf(TMPO_O, 1024)
    qgv = arb(QG_O, GH * T).rearrange("p (a b) -> p a b", a=GH)
    kgv = arb(KG_O, GH * TH).rearrange("p (a b) -> p a b", a=GH)
    vgv = arb(VG_O, 12 * GH * 128).rearrange("p (a b) -> p a b", a=12)
    atgv = arb(ATG_O, GH * T)
    wsv = arf(WS_O, 512)
    wsv2 = [arf(WS_O, 512), arf(66048, 512)]
    wbv = arb(WB_O[0], 2048).rearrange("p (a b) -> p a b", a=FC)
    expPv = arb(EXP_O, 640)
    ptbv = arb(PT_O, 640)
    vtmpv = arb(VT_O, 512)
    qrawv = arb(QR_O[0], 512)
    qrotv = arb(QR_O[1], 512)
    rbbv = arb(QR_O[2], 512)
    hn_r = hn.rearrange("p (a b) -> p a b", a=FC)

    with nc.Block() as blk:
        @blk.sync
        def _(e):
            inc(e.dma_start(rqcv, ropeqc[:, :]), "ld", 16)
            inc(e.dma_start(rqsv, ropeqs[:, :]), "ld", 16)
            inc(e.dma_start(rkcv, ropekc[:, :]), "ld", 16)
            inc(e.dma_start(rksv, ropeks[:, :]), "ld", 16)
            inc(e.dma_start(
                lasv.rearrange("p (a b c) -> p a b c", a=4, b=FC),
                lA[:, :, :, :].transpose([2, 0, 1, 3])), "ld", 16)
            for k in "qkvo":
                inc(e.dma_start(lbsv[k][0:R, :], lB[k][:, :]), "ld", 16)
            V["p3ld"] = V["ld"]

        @blk.scalar
        def _(e):
            e.wait_ge(S["ld"], V["p3ld"])
            inc(e.activation(lasb, lasv, AF.Copy), "sc")

    PRJ = [("q", 0, 2, HB), ("k", 1, 3, 0), ("v", 2, 3, 0)]

    def emit_proj(g):
        sd = Sched(V)
        S_ = S
        psum_free = [None, None]
        wb_free = None
        ws_free = [None, None]
        psT_free = None
        pd = {}
        qraw_free = None
        qrot_free = None
        rbb_free = None
        vtmp_free = None

        for pr, pi, NT, toff in PRJ:
            for tch in range(NT):
                def mkA(pi=pi, tch=tch, toff=toff):
                    def f(e):
                        for kc in range(FC):
                            mm = e.matmul(
                                psC[0:R, 0:512],
                                lasb[:, (pi * FC + kc) * R:(pi * FC + kc + 1) * R],
                                hn_r[:, kc, toff + tch * 512:toff + (tch + 1) * 512],
                                start=(kc == 0), stop=(kc == FC - 1))
                        return mm
                    return f
                deps = [pd[(pr, tch - 1, "ev")]] if tch >= 1 else                     ([pd[(PRJ[pi - 1][0], PRJ[pi - 1][2] - 1, "ev")]] if pi >= 1 else [])
                tA = sd.add("tensor", deps, mkA())
                def mkAe(pr=pr, tch=tch):
                    def f(e):
                        return e.activation(
                            tmplv[0:R, tch * 512:(tch + 1) * 512],
                            psC[0:R, 0:512], AF.Copy, scale=1.0 / R)
                    return f
                eA = sd.add("scalar", [tA], mkAe())
                pd[(pr, tch, "ev")] = eA

            for oc4 in range(GH):
                h = GH * g + oc4
                casts = []
                for q4 in range(4):
                    w = q4 % 2
                    def mkD(pr=pr, h=h, q4=q4, w=w):
                        def f(e):
                            return e.dma_start(
                                wsv2[w].rearrange("p (a b) -> p a b", a=4),
                                wW[pr][h][q4 * 512:(q4 + 1) * 512, :]
                                .rearrange("(a p) j -> p a j", p=128))
                        return f
                    dd = [ws_free[w]] if ws_free[w] is not None else []
                    dq = sd.add("sync", dd, mkD(), n=16,
                                sem="lws" if w == 0 else "lws1")
                    def mkC(q4=q4, w=w):
                        def f(e):
                            return e.activation(
                                wbv[:, 4 * q4:4 * (q4 + 1), :],
                                wsv2[w].rearrange("p (a b) -> p a b", a=4),
                                AF.Copy)
                        return f
                    cd = [dq] + ([wb_free] if wb_free is not None else [])
                    ct = sd.add("scalar", cd, mkC())
                    ws_free[w] = ct
                    casts.append(ct)

                for tch in range(NT):
                    slot = (oc4 * NT + tch) % 2
                    def mkM(pr=pr, h=h, tch=tch, toff=toff, slot=slot):
                        def f(e):
                            p = psA[:, slot * 512:(slot + 1) * 512]
                            for kc in range(FC):
                                e.matmul(p, wbv[:, kc, :],
                                         hn_r[:, kc, toff + tch * 512:
                                              toff + (tch + 1) * 512],
                                         start=(kc == 0), stop=False)
                            return e.matmul(
                                p, lbsv[pr][0:R, h * 128:(h + 1) * 128],
                                tmplv[0:R, tch * 512:(tch + 1) * 512],
                                start=False, stop=True)
                        return f
                    mdeps = casts + [pd[(pr, tch, "ev")]]
                    if psum_free[slot] is not None:
                        mdeps += psum_free[slot]
                    mm = sd.add("tensor", mdeps, mkM())
                    wb_free = mm

                    if pr in ("q", "k"):
                        cosv = rqcv if pr == "q" else rkcv
                        sinv = rqsv if pr == "q" else rksv
                        dst = (qgv if pr == "q" else kgv)
                        def mkES(slot=slot):
                            def f(e):
                                return e.copy(
                                    qrawv, psA[:, slot * 512:(slot + 1) * 512])
                            return f
                        esd = [mm] + ([qraw_free] if qraw_free is not None else [])
                        evS = sd.add("scalar", esd, mkES())
                        def mkSh():
                            def f(e):
                                return e.stream_shuffle(qrotv, qrawv, SHUF)
                            return f
                        sh_d = [evS] + ([qrot_free] if qrot_free is not None else [])
                        evV1 = sd.add("vector", sh_d, mkSh())
                        qraw_free = evV1
                        def mkA2(slot=slot, oc4=oc4, tch=tch, cosv=cosv, dst=dst):
                            def f(e):
                                return e.tensor_tensor(
                                    dst[:, oc4, tch * 512:(tch + 1) * 512],
                                    psA[:, slot * 512:(slot + 1) * 512],
                                    cosv[:, tch * 512:(tch + 1) * 512], ALU.mult)
                            return f
                        evV2 = sd.add("vector", [mm], mkA2())
                        def mkB(sinv=sinv, tch=tch):
                            def f(e):
                                return e.tensor_tensor(
                                    rbbv, qrotv,
                                    sinv[:, tch * 512:(tch + 1) * 512], ALU.mult)
                            return f
                        gb_d = [evV1] + ([rbb_free] if rbb_free is not None else [])
                        evG = sd.add("gpsimd", gb_d, mkB())
                        qrot_free = evG
                        def mkAdd(oc4=oc4, tch=tch, dst=dst):
                            def f(e):
                                return e.tensor_tensor(
                                    dst[:, oc4, tch * 512:(tch + 1) * 512],
                                    dst[:, oc4, tch * 512:(tch + 1) * 512],
                                    rbbv, ALU.add)
                            return f
                        evV3 = sd.add("vector", [evG], mkAdd())
                        rbb_free = evV3
                        psum_free[slot] = [evS, evV2]
                    else:
                        def mkEV(slot=slot):
                            def f(e):
                                return e.copy(
                                    vtmpv, psA[:, slot * 512:(slot + 1) * 512])
                            return f
                        evd = [mm] + ([vtmp_free] if vtmp_free is not None else [])
                        evS = sd.add("scalar", evd, mkEV())
                        def mkT():
                            def f(e):
                                for j in range(4):
                                    t = e.transpose(psT[:, j * 128:(j + 1) * 128],
                                                    vtmpv[:, j * 128:(j + 1) * 128],
                                                    idn[:, :])
                                return t
                            return f
                        td = [evS] + ([psT_free] if psT_free is not None else [])
                        tT = sd.add("tensor", td, mkT())
                        vtmp_free = tT
                        def mkVE(oc4=oc4, tch=tch):
                            def f(e):
                                for j in range(4):
                                    t = e.tensor_copy(
                                        vgv[:, tch * 4 + j,
                                            oc4 * 128:(oc4 + 1) * 128],
                                        psT[:, j * 128:(j + 1) * 128])
                                return t
                            return f
                        evV = sd.add("vector", [tT], mkVE())
                        psT_free = evV
                        psum_free[slot] = [evS]
        sd.emit(nc, S_, V, inc)

    def emit_attn(g):
        sd = Sched(V)
        exp_prev = []
        av_prev = None
        pt_slot_free = [None, None]
        expP_free = None
        mask_step = None
        for qt in range(8):
            if qt < 4:
                def mkMD(qt=qt):
                    def f(e):
                        return e.dma_start(mskEb[:, :, :],
                                           maskE[qt].transpose([1, 0, 2]))
                    return f
                mask_step = sd.add("sync", exp_prev + ([mask_step] if mask_step else []), mkMD(), n=16, sem="lmk")
            for oc4 in range(GH):
                def mkS(g=g, oc4=oc4, qt=qt):
                    def f(e):
                        qtile = qgv[:, oc4, qt * 128:(qt + 1) * 128]
                        e.matmul(psA[:, 1024:1536], qtile,
                                 kgv[:, oc4, qt * 128:qt * 128 + 512],
                                 start=True, stop=True)
                        return e.matmul(psA[:, 1536:1664], qtile,
                                        kgv[:, oc4, qt * 128 + 512:qt * 128 + 640],
                                        start=True, stop=True)
                    return f
                sdeps = list(exp_prev)
                if av_prev is not None:
                    sdeps.append(av_prev)
                s12 = sd.add("tensor", sdeps, mkS())
                def mkMk(qt=qt):
                    def f(e):
                        if qt < 4:
                            for s in range(4):
                                t = e.tensor_tensor(
                                    psA[:, 1024 + s * 128:1024 + (s + 1) * 128],
                                    psA[:, 1024 + s * 128:1024 + (s + 1) * 128],
                                    mskEb[:, s, :], ALU.add)
                            t = e.tensor_tensor(psA[:, 1536:1664],
                                                psA[:, 1536:1664],
                                                mskEb[:, 4, :], ALU.add)
                        else:
                            e.tensor_tensor(psA[:, 1024:1152], psA[:, 1024:1152],
                                            mskSb[:, 0, :], ALU.add)
                            t = e.tensor_tensor(psA[:, 1536:1664], psA[:, 1536:1664],
                                                mskSb[:, 1, :], ALU.add)
                        return t
                    return f
                mdeps = [s12] + ([mask_step] if qt < 4 else [])
                mk = sd.add("vector", mdeps, mkMk())
                def mkRx():
                    def f(e):
                        e.reduce_max(rmx1[:, :], psA[:, 1024:1536], AX.X)
                        e.reduce_max(rmx2[:, :], psA[:, 1536:1664], AX.X)
                        e.drain()
                        e.tensor_tensor(rmx1[:, :], rmx1[:, :], rmx2[:, :], ALU.max)
                        e.drain()
                        return e.tensor_scalar(rmx1[:, :], rmx1[:, :], -1.0,
                                               None, ALU.mult)
                    return f
                rx = sd.add("vector", [mk], mkRx())
                def mkEx():
                    def f(e):
                        e.activation(expPv[0:512].rearrange("p n -> p n") if False
                                     else expPv[:, 0:512] if False else expPv[:, :][:, 0:512],
                                     psA[:, 1024:1536], AF.Exp,
                                     bias=rmx1[:, :], accum_out=racc1[:, :])
                        return e.activation(expPv[:, :][:, 512:640],
                                            psA[:, 1536:1664], AF.Exp,
                                            bias=rmx1[:, :], accum_out=racc2[:, :])
                    return f
                edeps = [rx] + ([expP_free] if expP_free is not None else [])
                ex = sd.add("scalar", edeps, mkEx())
                exp_prev = [ex]
                def mkRs():
                    def f(e):
                        e.tensor_tensor(rsum[:, :], racc1[:, :], racc2[:, :], ALU.add)
                        e.drain()
                        e.reciprocal(rsum[:, :], rsum[:, :])
                        e.drain()
                        return e.tensor_scalar(diag[:, :], idn[:, :],
                                               rsum[:, :], None, ALU.mult)
                    return f
                rs = sd.add("vector", [ex], mkRs())
                evpt = []
                for s5 in range(5):
                    slot = s5 % 2
                    off = 0 if slot == 0 else 512
                    def mkPT(s5=s5, off=off):
                        def f(e):
                            return e.matmul(psB[:, off:off + 128],
                                            expPv[:, :][:, s5 * 128:(s5 + 1) * 128],
                                            diag[:, :], start=True, stop=True)
                        return f
                    pdeps = [rs]
                    if pt_slot_free[slot] is not None:
                        pdeps.append(pt_slot_free[slot])
                    pt = sd.add("tensor", pdeps, mkPT())
                    def mkPE(s5=s5, off=off):
                        def f(e):
                            return e.tensor_copy(
                                ptbv[:, s5 * 128:(s5 + 1) * 128],
                                psB[:, off:off + 128])
                        return f
                    pe = sd.add("vector", [pt], mkPE())
                    pt_slot_free[slot] = pe
                    evpt.append(pe)
                expP_free = evpt[-1]
                def mkPV(g=g, oc4=oc4, qt=qt):
                    def f(e):
                        for s5 in range(5):
                            t = e.matmul(psA[:, 1664:1792],
                                         vgv[:, qt + s5, oc4 * 128:(oc4 + 1) * 128],
                                         ptbv[:, s5 * 128:(s5 + 1) * 128],
                                         start=(s5 == 0), stop=(s5 == 4))
                        return t
                    return f
                pvdeps = list(evpt)
                if av_prev is not None:
                    pvdeps.append(av_prev)
                pv = sd.add("tensor", pvdeps, mkPV())
                def mkAV(oc4=oc4, qt=qt):
                    def f(e):
                        return e.tensor_copy(
                            atgv[:, oc4 * T + qt * 128:oc4 * T + (qt + 1) * 128],
                            psA[:, 1664:1792])
                    return f
                av = sd.add("vector", [pv], mkAV())
                av_prev = av

        ws_free = [None, None]
        wb_free = None
        psum_free = [None, None]
        for oc in range(FC):
            w = oc % 2
            def mkD(oc=oc, g=g, w=w):
                def f(e):
                    return e.dma_start(
                        wsv2[w].rearrange("p (a b) -> p a b", a=4),
                        wW["o"][oc][g * 512:(g + 1) * 512, :]
                        .rearrange("(a p) j -> p a j", p=128))
                return f
            dd = [ws_free[w]] if ws_free[w] is not None else []
            dq = sd.add("sync", dd, mkD(), n=16,
                        sem="lws" if w == 0 else "lws1")
            def mkC(w=w):
                def f(e):
                    return e.activation(
                        wbv[:, 0:4, :],
                        wsv2[w].rearrange("p (a b) -> p a b", a=4), AF.Copy)
                return f
            cd = [dq] + ([wb_free] if wb_free is not None else [])
            ct = sd.add("scalar", cd, mkC())
            ws_free[w] = ct
            for tch in range(2):
                slot = tch
                def mkOM(oc=oc, tch=tch, slot=slot):
                    def f(e):
                        p = psA[:, slot * 512:(slot + 1) * 512]
                        for hl in range(GH):
                            t = e.matmul(p, wbv[:, hl, :],
                                         atgv[:, hl * T + tch * 512:
                                              hl * T + (tch + 1) * 512],
                                         start=(hl == 0), stop=(hl == GH - 1))
                        return t
                    return f
                odeps = [ct, av_prev]
                if psum_free[slot] is not None:
                    odeps.append(psum_free[slot])
                om = sd.add("tensor", odeps, mkOM())
                wb_free = om
                def mkOX(oc=oc, tch=tch, slot=slot):
                    def f(e):
                        xv = xo[:, oc, tch * 512:(tch + 1) * 512]
                        return e.scalar_tensor_tensor(
                            xv, psA[:, slot * 512:(slot + 1) * 512],
                            mv[:, MVO["ga"] + oc:MVO["ga"] + oc + 1],
                            xv, ALU.mult, ALU.add)
                    return f
                ox = sd.add("vector", [om], mkOX())
                psum_free[slot] = ox
        oe_prev = None
        for tch in range(2):
            def mkOA(g=g, tch=tch):
                def f(e):
                    for hl in range(GH):
                        h = GH * g + hl
                        t = e.matmul(
                            psC[0:R, 0:512],
                            lasb[:, (3 * FC + h) * R:(3 * FC + h + 1) * R],
                            atgv[:, hl * T + tch * 512:hl * T + (tch + 1) * 512],
                            start=(hl == 0), stop=(hl == GH - 1))
                    return t
                return f
            adeps = [av_prev] + ([oe_prev] if oe_prev is not None else [])
            oa = sd.add("tensor", adeps, mkOA())
            def mkOE(g=g, tch=tch):
                def f(e):
                    dst = tmpov[0:R, tch * 512:(tch + 1) * 512]
                    if g == 0:
                        return e.tensor_copy(dst, psC[0:R, 0:512])
                    return e.tensor_tensor(dst, dst, psC[0:R, 0:512], ALU.add)
                return f
            oe = sd.add("vector", [oa], mkOE())
            oe_prev = oe
        sd.emit(nc, S_, V, inc)

    S_ = S
    for g in range(NG):
        emit_proj(g)
        if g == 0 and "arena" in dumps:
            with nc.Block() as blk:
                @blk.gpsimd
                def _(e):
                    inc(e.dma_start(dumps["arena"][:, :], arena[:, :]), "gp", 16)
                    e.wait_ge(S["gp"], V["gp"])
        if g == 0 and "q" in dumps:
            with nc.Block() as blk:
                @blk.gpsimd
                def _(e):
                    inc(e.dma_start(dumps["q"][:, :, :],
                                    arb(QG_O, GH * T).rearrange("p (a b) -> p a b", a=GH)), "gp", 16)
                    inc(e.dma_start(dumps["k"][:, :, :],
                                    arb(KG_O, GH * TH).rearrange("p (a b) -> p a b", a=GH)), "gp", 16)
                    inc(e.dma_start(dumps["v"][:, :, :],
                                    arb(VG_O, 12 * GH * 128).rearrange("p (a b) -> p a b", a=12)), "gp", 16)
                    e.wait_ge(S["gp"], V["gp"])
        emit_attn(g)
        if g == 0 and "attn" in dumps:
            with nc.Block() as blk:
                @blk.gpsimd
                def _(e):
                    inc(e.dma_start(dumps["attn"][:, :, :],
                                    atgv.rearrange("p (a b) -> p a b", a=GH)), "gp", 16)
                    e.wait_ge(S["gp"], V["gp"])

    with nc.Block() as blk:
        @blk.scalar
        def _(e):
            inc(e.activation(tmplv[0:R, 0:T], tmpov[0:R, 0:T], AF.Copy,
                             scale=1.0 / R), "sc")
            V["obc"] = V["sc"]

    sd = Sched(V)
    psum_free = [None, None]
    dummy = sd.add("scalar", [], lambda e: e.activation(
        vtmpv[0:1, 0:2], vtmpv[0:1, 0:2], AF.Copy))
    for oc in range(FC):
        for tch in range(2):
            slot = tch
            def mkBM(oc=oc, tch=tch, slot=slot):
                def f(e):
                    return e.matmul(psA[:, slot * 512:(slot + 1) * 512],
                                    lbsv["o"][0:R, oc * 128:(oc + 1) * 128],
                                    tmplv[0:R, tch * 512:(tch + 1) * 512],
                                    start=True, stop=True)
                return f
            bdeps = [dummy]
            if psum_free[slot] is not None:
                bdeps.append(psum_free[slot])
            bm = sd.add("tensor", bdeps, mkBM())
            def mkBX(oc=oc, tch=tch, slot=slot):
                def f(e):
                    xv = xo[:, oc, tch * 512:(tch + 1) * 512]
                    return e.scalar_tensor_tensor(
                        xv, psA[:, slot * 512:(slot + 1) * 512],
                        mv[:, MVO["ga"] + oc:MVO["ga"] + oc + 1],
                        xv, ALU.mult, ALU.add)
                return f
            bx = sd.add("vector", [bm], mkBX())
            psum_free[slot] = bx
    sd.emit(nc, S_, V, inc)

    H2_O2 = 0
    INT_O2 = 16384
    H1B2 = 16384
    CST2 = 24576
    RW2 = 28672
    RAB2 = 31232
    WDS_O = 32768
    GUB_O = [65536, 67584]
    GELB_O = 69632

    h2v = arb(H2_O2, FC * T)
    h2_r = h2v.rearrange("p (a b) -> p a b", a=FC)
    cst2 = arb(CST2, FC * LNC)
    h1b2 = arf(H1B2, FC * LNC)
    rwv2 = {}
    for j, k in enumerate(["mean", "ms", "var", "rr", "bb"]):
        rwv2[k] = arf(RW2 + 2 * j * LNC, LNC)[0:1, :]
    rAB2 = arb(RAB2, 2 * LNC)[0:1, :]

    def cst2_v(kc):
        return cst2.rearrange("p (a b) -> p a b", a=FC)[:, kc, :]

    def h1b2_v(kc):
        return h1b2.rearrange("p (a b) -> p a b", a=FC)[:, kc, :]

    with nc.Block() as blk:
        b_ld, b_sc, b_tn, b_vc = V["ld"], V["sc"], V["tn"], V["vc"]

        @blk.scalar
        def _(e):
            for ch in range(NLN2):
                if ch >= 1:
                    e.wait_ge(S["tn"], b_tn + 3 * (ch - 1) + 2)
                inc(e.activation(cst2.rearrange("p (a b) -> p a b", a=FC),
                                 xo[:, :, ch * LNC:(ch + 1) * LNC],
                                 AF.Copy), "sc")
                e.wait_ge(S["tn"], b_tn + 3 * ch + 1)
                inc(e.activation(cst2, cst2, AF.Square), "sc")
                e.wait_ge(S["vc"], b_vc + 3 * ch + 1)
                inc(e.activation(rwv2["var"], rwv2["var"], AF.Sqrt,
                                 bias=epsb[0:1, 0:1]), "sc")

        @blk.tensor
        def _(e):
            for ch in range(NLN2):
                e.wait_ge(S["sc"], b_sc + 3 * ch + 1)
                if ch >= 1:
                    e.wait_ge(S["vc"], b_vc + 3 * (ch - 1) + 1)
                for kc in range(FC):
                    mm = e.matmul(psC[0:1, 0:LNC], onec[:, :], cst2_v(kc),
                                  start=(kc == 0), stop=(kc == FC - 1))
                inc(mm, "tn")
                e.wait_ge(S["sc"], b_sc + 3 * ch + 2)
                for kc in range(FC):
                    mm = e.matmul(psC[0:1, LNC:2 * LNC], onec[:, :], cst2_v(kc),
                                  start=(kc == 0), stop=(kc == FC - 1))
                inc(mm, "tn")
                e.wait_ge(S["vc"], b_vc + 3 * ch + 2)
                inc(e.matmul(psB[:, 0:2 * LNC], oner[:, :], rAB2,
                             start=True, stop=True), "tn")

        @blk.vector
        def _(e):
            for ch in range(NLN2):
                e.wait_ge(S["tn"], b_tn + 3 * ch + 2)
                e.tensor_scalar(rwv2["mean"], psC[0:1, 0:LNC], 1.0 / D, None, ALU.mult)
                e.tensor_scalar(rwv2["ms"], psC[0:1, LNC:2 * LNC], 1.0 / D, None, ALU.mult)
                e.tensor_tensor(rwv2["var"], rwv2["mean"], rwv2["mean"], ALU.mult)
                inc(e.tensor_tensor(rwv2["var"], rwv2["ms"], rwv2["var"],
                                    ALU.subtract), "vc")
                e.wait_ge(S["sc"], b_sc + 3 * ch + 3)
                e.reciprocal(rwv2["rr"], rwv2["var"])
                e.tensor_tensor(rwv2["bb"], rwv2["mean"], rwv2["rr"], ALU.mult)
                e.tensor_scalar(rwv2["bb"], rwv2["bb"], -1.0, None, ALU.mult)
                e.tensor_copy(rAB2[:, 0:LNC], rwv2["rr"])
                inc(e.tensor_copy(rAB2[:, LNC:2 * LNC], rwv2["bb"]), "vc")
                e.wait_ge(S["tn"], b_tn + 3 * ch + 3)
                for kc in range(FC):
                    e.tensor_tensor(h1b2_v(kc), xo[:, kc, ch * LNC:(ch + 1) * LNC],
                                    psB[:, 0:LNC], ALU.mult)
                    e.tensor_tensor(h1b2_v(kc), h1b2_v(kc), psB[:, LNC:2 * LNC],
                                    ALU.add)
                    last = e.tensor_scalar(
                        h2_r[:, kc, ch * LNC:(ch + 1) * LNC], h1b2_v(kc),
                        mv[:, MVO["cb1"] + kc:MVO["cb1"] + kc + 1],
                        mv[:, MVO["sb"] + kc:MVO["sb"] + kc + 1],
                        ALU.mult, ALU.add)
                inc(last, "vc")

    if "h2" in dumps:
        with nc.Block() as blk:
            @blk.gpsimd
            def _(e):
                inc(e.dma_start(dumps["h2"][:, :, :], h2_r), "gp", 16)
                e.wait_ge(S["gp"], V["gp"])
    INT2_O = 16384
    DWN_O = 49152
    GUB_O2 = [55296 + j * 2048 for j in range(6)]
    GELB2_O = [67584, 68608]

    inter2 = arb(INT2_O, 2 * FC * 2 * 512).rearrange(
        "p (q a h b) -> p q a h b", q=2, a=FC, h=2)
    psTf = psT[:, :].bitcast(F32)

    sd = Sched(V)
    NSUP = 8
    first_sync = [True]
    first_gp = [True]
    gub_free = [None] * 6
    gel_hist = {}
    imul_hist = {}
    dstt_hist = {}
    dmm_hist = {}
    ddma_hist = {}

    def emit_gup(s, nl):
        step = s * FC + nl
        gs = (2 * step) % 6
        us = (2 * step + 1) % 6
        goff = nl * (FC * 128 * 128)

        def mkGD(s=s, goff=goff, gs=gs, first=first_sync[0]):
            def f(e):
                if first:
                    e.wait_ge(S["cc"], V["deqcc"])
                return e.dma_start(
                    arb(GUB_O2[gs], 2048),
                    deq_out[s, goff:goff + 2048 * 128]
                    .rearrange("(p n) -> p n", p=128))
            return f
        gdd = [gub_free[gs]] if gub_free[gs] is not None else []
        gdma = sd.add("sync", gdd, mkGD(), n=16, sem="lb%d" % gs)
        first_sync[0] = False

        def mkUD(s=s, goff=goff, us=us):
            def f(e):
                return e.dma_start(
                    arb(GUB_O2[us], 2048),
                    deq_out[s, NSL + goff:NSL + goff + 2048 * 128]
                    .rearrange("(p n) -> p n", p=128))
            return f
        udd = [gub_free[us]] if gub_free[us] is not None else []
        udma = sd.add("sync", udd, mkUD(), n=16, sem="lb%d" % us)

        A = (nl % 2) * 1024

        def mkGM(gs=gs, A=A):
            def f(e):
                gb = arb(GUB_O2[gs], 2048).rearrange("p (a b) -> p a b", a=FC)
                for hf in range(2):
                    for kc in range(FC):
                        mm = e.matmul(
                            psA[:, A + hf * 512:A + hf * 512 + 512],
                            gb[:, kc, :],
                            h2_r[:, kc, hf * 512:hf * 512 + 512],
                            start=(kc == 0), stop=(kc == FC - 1))
                return mm
            return f
        gdeps = [gdma]
        if (s, nl - 2) in gel_hist:
            gdeps.append(gel_hist[(s, nl - 2)])
        elif nl < 2 and s > 0:
            gdeps.append(gel_hist[(s - 1, FC - 2 + nl)])
        gmm = sd.add("tensor", gdeps, mkGM())
        gub_free[gs] = gmm

        def mkUM(us=us):
            def f(e):
                ub = arb(GUB_O2[us], 2048).rearrange("p (a b) -> p a b", a=FC)
                for hf in range(2):
                    for kc in range(FC):
                        mm = e.matmul(
                            psB[:, hf * 512:hf * 512 + 512],
                            ub[:, kc, :],
                            h2_r[:, kc, hf * 512:hf * 512 + 512],
                            start=(kc == 0), stop=(kc == FC - 1))
                return mm
            return f
        udeps = [udma]
        if (s, nl - 1) in imul_hist:
            udeps.append(imul_hist[(s, nl - 1)])
        elif nl == 0 and s > 0:
            udeps.append(imul_hist[(s - 1, FC - 1)])
        umm = sd.add("tensor", udeps, mkUM())
        gub_free[us] = umm

        def mkGel(nl=nl, A=A):
            def f(e):
                for hf in range(2):
                    t = e.activation(
                        arb(GELB2_O[nl % 2] + hf * 512, 512),
                        psA[:, A + hf * 512:A + hf * 512 + 512],
                        AF.Gelu_apprx_tanh)
                return t
            return f
        geld = [gmm]
        if (s, nl - 2) in imul_hist:
            geld.append(imul_hist[(s, nl - 2)])
        elif nl < 2 and s > 0:
            geld.append(imul_hist[(s - 1, FC - 2 + nl)])
        gel = sd.add("scalar", geld, mkGel())
        gel_hist[(s, nl)] = gel

        def mkIM(s=s, nl=nl):
            def f(e):
                for hf in range(2):
                    t = e.tensor_tensor(
                        inter2[:, s % 2, nl, hf, :],
                        arb(GELB2_O[nl % 2] + hf * 512, 512),
                        psB[:, hf * 512:hf * 512 + 512],
                        ALU.mult)
                return t
            return f
        imdeps = [gel, umm]
        if s >= 2 and (s - 2, FC - 1) in dmm_hist:
            imdeps.append(dmm_hist[(s - 2, FC - 1)])
        im = sd.add("vector", imdeps, mkIM())
        imul_hist[(s, nl)] = im

    def emit_ddma(s, hc):
        def mkDD(s=s, hc=hc, first=first_gp[0]):
            def f(e):
                if first:
                    e.wait_ge(S["cc"], V["deqcc"])
                return e.dma_start(
                    arb(DWN_O + ((s * FC + hc) % 3) * 2048, 2048),
                    deq_out[s, 2 * NSL + hc * (NCK * 128 * 128):
                            2 * NSL + (hc + 1) * (NCK * 128 * 128)]
                    .rearrange("(p n) -> p n", p=128))
            return f
        fl = s * FC + hc
        ddd = [dmm_hist[divmod(fl - 3, FC)]] if fl >= 3 else []
        ddma = sd.add("gpsimd", ddd, mkDD(), n=16, sem="dd%d" % (fl % 3))
        first_gp[0] = False
        ddma_hist[(s, hc)] = ddma

    def emit_dmm(s, hc):
        def mkDM(s=s, hc=hc):
            def f(e):
                dw = arb(DWN_O + ((s * FC + hc) % 3) * 2048, 2048)
                for hf in range(2):
                    dst = psC[:, 0:512] if hf == 0 else psTf[:, 0:512]
                    for ns in range(FC):
                        mm = e.matmul(
                            dst, dw[:, ns * 128:(ns + 1) * 128],
                            inter2[:, s % 2, ns, hf, :],
                            start=(ns == 0), stop=(ns == FC - 1))
                return mm
            return f
        dmdeps = [ddma_hist[(s, hc)], imul_hist[(s, FC - 1)]]
        if (s, hc - 1) in dstt_hist:
            dmdeps.append(dstt_hist[(s, hc - 1)])
        elif hc == 0 and s > 0:
            dmdeps.append(dstt_hist[(s - 1, FC - 1)])
        dmm = sd.add("tensor", dmdeps, mkDM())
        dmm_hist[(s, hc)] = dmm

        def mkDX(hc=hc):
            def f(e):
                for hf in range(2):
                    src = psC[:, 0:512] if hf == 0 else psTf[:, 0:512]
                    xv = xo[:, hc, hf * 512:(hf + 1) * 512]
                    t = e.scalar_tensor_tensor(
                        xv, src,
                        mv[:, MVO["gb"] + hc:MVO["gb"] + hc + 1],
                        xv, ALU.mult, ALU.add)
                return t
            return f
        dx = sd.add("vector", [dmm], mkDX())
        dstt_hist[(s, hc)] = dx

    TILES = NSUP * FC
    for t in range(TILES + FC):
        s, nl = divmod(t, FC)
        if s < NSUP:
            emit_gup(s, nl)
        fl = t - 14
        if 0 <= fl < TILES:
            emit_ddma(*divmod(fl, FC))
        fp = t - 16
        if fp >= 0:
            emit_dmm(*divmod(fp, FC))
    if "inter" in dumps:
        def mkID():
            def f(e):
                return e.dma_start(
                    dumps["inter"][:, :, :],
                    inter2[:, (NSUP - 1) % 2, :, :, :]
                    .rearrange("p a h b -> p (a h b)")
                    .rearrange("p (a b) -> p a b", a=FC))
            return f
        sd.add("gpsimd", [imul_hist[(NSUP - 1, FC - 1)]], mkID(), n=16)
    sd.emit(nc, S_, V, inc)

    with nc.Block() as blk:
        @blk.gpsimd
        def _(e):
            inc(e.dma_start(xout.rearrange("(a p) t -> p a t", p=128),
                            xo[:, :, :]), "gp", 16)
            e.wait_ge(S["gp"], V["gp"])

    stk.close()
    return nc, dumps, V


def _run(inputs, dump=()):
    in_maps = prepare_inputs(inputs)
    nc, dumps, V = build_nc(dump=dump)
    res = run_bass_kernel_spmd(nc, in_maps, core_ids=list(range(NCORE)))
    return res


def kernel(**inputs):
    res = _run(inputs, dump=DUMP)
    outs = [r["xout"] for r in res.results]
    full = np.concatenate([o.T for o in outs], axis=0)
    return np.ascontiguousarray(full.reshape(1, 8192, D).astype(np.float32))
